# version 06
# baseline (speedup 1.0000x reference)
import contextlib
import numpy as np
import concourse.bass as bass
import concourse.mybir as mybir
from concourse.bass import IndirectOffsetOnAxis
from concourse.bass_utils import run_bass_kernel_spmd

F32 = mybir.dt.float32
I32 = mybir.dt.int32
AF = mybir.ActivationFunctionType
OP = mybir.AluOpType
AX = mybir.AxisListType.X

B, L, D, H, DH = 8, 4096, 1024, 16, 64
C = 128
NCH = L // C
NEU = 5
E, F2, F = 8, 2816, 1408
NFJ = F // 128
QT = 1024
NQ = L // QT
W1ROWS = 8 * 22 * 128
W2ROWS = 8 * 1408
NTT = QT // 128
LNEPS = 1e-5
SROWS = 512
NSLOT = 2 * L // SROWS + E
NTIL = L // 128
DEBUG_TAPS = False

ENGS = ("sync", "tensor", "vector", "scalar", "gpsimd")


class _Sem:
    def __init__(self, name):
        self.name = name
        self.n = 0
        self.h = None


class _Prog:
    def __init__(self):
        self.streams = {k: [] for k in ENGS}
        self.hw = {k: {} for k in ENGS}

    def raw(self, eng, fn):
        self.streams[eng].append(fn)

    def wait(self, eng, sem, val):
        if sem is None or val is None or val <= 0:
            return
        if self.hw[eng].get(sem.name, 0) >= val:
            return
        self.hw[eng][sem.name] = val
        self.raw(eng, lambda E, s=sem, v=val: E.wait_ge(s.h, v))

    auto_sem = {}

    def ins(self, eng, build, inc=None, dma=False):
        if inc is None and not dma:
            inc = self.auto_sem.get(eng)
        if inc is not None:
            k = 16 if dma else 1
            inc.n += k
            v = inc.n
            self.raw(eng, lambda E, b=build, s=inc, kk=k: b(E).then_inc(s.h, kk))
            return v
        self.raw(eng, lambda E, b=build: b(E))
        return None


def build_nc(nch=NCH, nq=NQ, nexp=E, taps=None, do_router=True, do_sweep1=True, do_sweep2=True):
    nc = bass.Bass("TRN2", target_bir_lowering=False)
    es = contextlib.ExitStack()

    d_x = nc.dram_tensor("x", [L, D], F32, kind="ExternalInput").ap()
    d_wt = nc.dram_tensor("wt", [D, D], F32, kind="ExternalInput").ap()
    d_gam = nc.dram_tensor("gam", [1, D], F32, kind="ExternalInput").ap()
    d_bet = nc.dram_tensor("bet", [1, D], F32, kind="ExternalInput").ap()
    d_eta = nc.dram_tensor("eta", [1, 1], F32, kind="ExternalInput").ap()
    d_lml = nc.dram_tensor("lml", [1, 1], F32, kind="ExternalInput").ap()
    d_rwt = nc.dram_tensor("rwt", [128, 64], F32, kind="ExternalInput").ap()
    d_w1t = nc.dram_tensor("w1t", [E * 22 * 128, 1024], F32, kind="ExternalInput").ap()
    d_w2 = nc.dram_tensor("w2", [E * F, 1024], F32, kind="ExternalInput").ap()
    d_ident = nc.dram_tensor("ident", [128, 128], F32, kind="ExternalInput").ap()
    d_iota = nc.dram_tensor("iotac", [128, 1], F32, kind="ExternalInput").ap()
    d_zo = nc.dram_tensor("zo", [128, 2], F32, kind="ExternalInput").ap()
    d_ones1 = nc.dram_tensor("ones1", [1, 128], F32, kind="ExternalInput").ap()
    d_revi = nc.dram_tensor("revi", [128, 8], F32, kind="ExternalInput").ap()
    d_msks = nc.dram_tensor("msks", [128, 512], F32, kind="ExternalInput").ap()
    d_mski = nc.dram_tensor("mski", [128, 512], F32, kind="ExternalInput").ap()
    d_cval = nc.dram_tensor("cval", [128, 4], F32, kind="ExternalInput").ap()
    d_tri8 = nc.dram_tensor("tri8", [8, 8], F32, kind="ExternalInput").ap()
    d_th16 = nc.dram_tensor("th16", [128, 16], F32, kind="ExternalInput").ap()
    _taps = DEBUG_TAPS if taps is None else taps
    scr_kind = "ExternalOutput" if _taps else "Internal"
    d_xnscr = nc.dram_tensor("xnscr", [L, D], F32, kind=scr_kind).ap()
    d_pscr = nc.dram_tensor("pscr", [L, D], F32, kind=scr_kind).ap()
    NROWS = NSLOT * SROWS
    d_idxmap = nc.dram_tensor("idxmap", [NROWS, 1], I32, kind="Internal").ap()
    d_pvals = nc.dram_tensor("pvals", [NROWS, 1], F32, kind="Internal").ap()
    d_oscr = nc.dram_tensor("oscr", [NROWS, D], F32, kind="Internal").ap()
    d_dbg = nc.dram_tensor("dbg", [6 * 128, D], F32,
                           kind="ExternalOutput" if _taps else "Internal").ap()
    d_y = nc.dram_tensor("y", [L, D], F32, kind="ExternalOutput").ap()

    _probe = nc.alloc_sbuf_tensor("layout_base_probe", [128, 8], F32)
    SBUF_BASE = int(nc.lookup_mloc(_probe).addr)
    alloc_ptr = [SBUF_BASE]

    def sal(name, shape, at=None):
        if at is not None:
            alloc_ptr[0] = at
        off = alloc_ptr[0]
        t = nc.alloc_sbuf_tensor_at(name, shape, F32, offset=off)
        alloc_ptr[0] = off + ((shape[-1] * 4 + 31) // 32) * 32
        return t.ap(), off

    ident, _ = sal("ident", [128, 128])
    masks4, _ = sal("masks4", [128, 512])
    maski4, _ = sal("maski4", [128, 512])
    bmasks4, _ = sal("bmasks4", [128, 512])
    bmaski4, _ = sal("bmaski4", [128, 512])
    iotac, _ = sal("iotac", [128, 1])
    zo, _ = sal("zo", [128, 2])
    ones1, _ = sal("ones1", [1, 128])
    revi8, _ = sal("revi8", [128, 8])
    screw, _ = sal("screw", [1, 16])
    scv, _ = sal("scv", [128, 16])
    lampp, _ = sal("lampp", [128, 1])
    lampi, _ = sal("lampi", [128, 1])
    etain, _ = sal("etain", [1, 1])
    lmlin, _ = sal("lmlin", [1, 1])
    gamin, _ = sal("gamin", [1, D])
    betin, _ = sal("betin", [1, D])
    gbg, _ = sal("gbg", [128, D])
    gbb, _ = sal("gbb", [128, D])
    rwt, _ = sal("rwt", [128, 64])
    cval, _ = sal("cval", [128, 4])
    tri8, _ = sal("tri8", [8, 8])
    th16, _ = sal("th16", [128, 16])
    nc.const_aps.register(F32, LNEPS, cval[:, 0:1])

    def ial(name, shape):
        off = alloc_ptr[0]
        t = nc.alloc_sbuf_tensor_at(name, shape, I32, offset=off)
        alloc_ptr[0] = off + ((shape[-1] * 4 + 31) // 32) * 32
        return t.ap()

    e_sb, _ = sal("e_sb", [128, 256])
    pr_sb, _ = sal("pr_sb", [128, 256])
    sc1, _ = sal("sc1", [128, 256])
    sc2, _ = sal("sc2", [128, 256])
    m1_sb, _ = sal("m1_sb", [128, 256])
    pcol, _ = sal("pcol", [128, 256])
    lp_sb, _ = sal("lp_sb", [128, 256])
    poscols, _ = sal("poscols", [128, 64])
    cb_sb, _ = sal("cb_sb", [128, 8])
    tmp8, _ = sal("tmp8", [128, 8])
    m2tmp, _ = sal("m2tmp", [128, 8])
    junk8, _ = sal("junk8", [128, 8])
    pv12, _ = sal("pv12", [128, 2])
    tokf, _ = sal("tokf", [128, 1])
    cntc, _ = sal("cntc", [128, 16])
    rowsc, _ = sal("rowsc", [1, 16])
    ebas, _ = sal("ebas", [128, 48])
    s24, _ = sal("s24", [128, 40])
    rows64, _ = sal("rows64", [1, 64])
    iota0, _ = sal("iota0", [128, 1])
    toki = ial("toki", [128, 1])
    posi1 = ial("posi1", [128, 1])
    posi2 = ial("posi2", [128, 1])
    tok32 = ial("tok32", [128, 8])
    idxw = ial("idxw", [128, 8])
    zeri = ial("zeri", [128, 48])
    pvs, _ = sal("pvs", [128, 8])
    COMMON_END = alloc_ptr[0]
    ARENA = (COMMON_END + 255) // 256 * 256

    lamc_col = scv[:, 3:4]
    blamc_col = scv[:, 4:5]
    beta_col = scv[:, 2:3]
    zcol = zo[:, 0:1]
    onecol = zo[:, 1:2]

    wt_sb, _ = sal("wt_sb", [128, 8192], at=ARENA)
    x_tok, _ = sal("x_tok", [128, 2048])
    xn_tok, _ = sal("xn_tok", [128, 2048])
    xa_fm, _ = sal("xa_fm", [128, 2048])
    kodd, _ = sal("kodd", [128, 2048])
    static_t, _ = sal("static_t", [128, 2048])
    vd_yn, _ = sal("vd_yn", [128, 1024])
    bb, _ = sal("bb", [128, 1024])
    u0, _ = sal("u0", [128, 1024])
    u1, _ = sal("u1", [128, 1024])
    gus, _ = sal("gus", [128, 2048])
    gui, _ = sal("gui", [128, 2048])
    ktok, _ = sal("ktok", [128, 1024])
    s_sb, _ = sal("s_sb", [128, 1024])
    part2, _ = sal("part2", [128, 2048])
    junk, _ = sal("junk", [128, 2048])
    xc, _ = sal("xc", [128, 2048])
    lncol, _ = sal("lncol", [128, 32])
    P1_END = alloc_ptr[0]

    xn_ld, _ = sal("xn_ld", [128, 2048], at=ARENA)
    fm_tmp, _ = sal("fm_tmp", [128, 1024])
    silb, _ = sal("silb", [128, 1024])
    xng, _ = sal("xng", [128, 2048])
    xfm, _ = sal("xfm", [128, 8192])
    g_sl, _ = sal("g_sl", [128, 11 * 512])
    w1ab, _ = sal("w1ab", [128, 2048])
    w1bb, _ = sal("w1bb", [128, 2048])
    w2b, _ = sal("w2b", [128, 2048])
    outb, _ = sal("outb", [128, 8192])
    pbuf, _ = sal("pbuf", [128, 2048])
    o1b, _ = sal("o1b", [128, 2048])
    o2b, _ = sal("o2b", [128, 2048])
    ybuf, _ = sal("ybuf", [128, 2048])
    P2_END = alloc_ptr[0]

    P2_END = alloc_ptr[0]
    xn_ld2 = xn_ld
    assert P1_END <= 229000 and P2_END <= 229000, (P1_END, P2_END)

    pb = [es.enter_context(nc.psum_tensor(f"pb{i}", [128, 512], F32)) for i in range(8)]

    sems = {}
    for name in ("sLC", "sLX0", "sLX1", "sW1P0", "sW1P1", "sW2", "sSX0", "sSX1", "sSP0", "sSP1", "sTAP", "sLQ", "sGX0", "sGX1", "sW2B", "sY0", "sY1", "sOUT", "sT", "sV", "sS"):
        s = _Sem(name)
        s.h = es.enter_context(nc.semaphore(name))
        sems[name] = s
    sLC, sW2, sTAP = sems["sLC"], sems["sW2"], sems["sTAP"]
    sLXp = [sems["sLX0"], sems["sLX1"]]
    sW1p = [sems["sW1P0"], sems["sW1P1"]]
    sSXp = [sems["sSX0"], sems["sSX1"]]
    sSPp = [sems["sSP0"], sems["sSP1"]]
    sLQ, sOUT = sems["sLQ"], sems["sOUT"]
    sGXp = [sems["sGX0"], sems["sGX1"]]
    sW2p = [sems["sW2"], sems["sW2B"]]
    sYp = [sems["sY0"], sems["sY1"]]
    sT, sV, sS = sems["sT"], sems["sV"], sems["sS"]
    P = _Prog()
    P.auto_sem = {"tensor": sT, "vector": sV, "scalar": sS}
    W_ = P.wait
    I_ = P.ins

    for i in range(8):
        I_("sync", lambda E, i=i: E.dma_start(out=wt_sb[:, 1024 * i:1024 * i + 1024],
                                              in_=d_wt[128 * i:128 * i + 128, :]), inc=sLC, dma=True)
    for dst, src in ((ident, d_ident), (masks4, d_msks), (maski4, d_mski),
                     (iotac, d_iota), (zo, d_zo), (ones1, d_ones1), (revi8, d_revi),
                     (etain, d_eta), (lmlin, d_lml), (gamin, d_gam), (betin, d_bet),
                     (rwt, d_rwt), (cval, d_cval), (tri8, d_tri8), (th16, d_th16)):
        I_("sync", lambda E, d=dst, s=src: E.dma_start(out=d[:, :], in_=s[:, :]), inc=sLC, dma=True)
    ld_done = sLC.n

    v_init = I_("vector", lambda E: E.memset(s_sb[:, :], 0.0), inc=sV)
    v_scr0 = I_("vector", lambda E: E.memset(screw[0:1, :], 0.0), inc=sV)
    W_("vector", sLC, ld_done)

    def sync_step(eng, build, sem):
        v = I_(eng, build, inc=sem)
        W_(eng, sem, v)
        return v

    W_("scalar", sLC, ld_done)
    W_("scalar", sV, v_scr0)
    sync_step("scalar", lambda E: E.activation(screw[0:1, 0:1], lmlin[0:1, 0:1], AF.Sigmoid), sS)
    sync_step("scalar", lambda E: E.activation(screw[0:1, 1:2], screw[0:1, 0:1], AF.Ln), sS)
    s_ll = sync_step("scalar", lambda E: E.activation(screw[0:1, 3:4], screw[0:1, 1:2], AF.Exp,
                                                      bias=0.0, scale=128.0), sS)
    W_("vector", sS, s_ll)
    sync_step("vector", lambda E: E.tensor_scalar(screw[0:1, 5:6], screw[0:1, 0:1], -1.0, 1.0,
                                                  OP.mult, OP.add), sV)
    sync_step("vector", lambda E: E.tensor_tensor(screw[0:1, 6:7], screw[0:1, 5:6], etain[0:1, 0:1],
                                                  OP.mult), sV)
    sync_step("vector", lambda E: E.tensor_scalar_mul(screw[0:1, 2:3], screw[0:1, 6:7], 1.0 / DH), sV)
    v_beta = sync_step("vector", lambda E: E.tensor_tensor(screw[0:1, 4:5], screw[0:1, 2:3],
                                                           screw[0:1, 3:4], OP.mult), sV)
    W_("tensor", sV, v_beta)
    t_bc = I_("tensor", lambda E: E.matmul(pb[0][:, 0:16], ones1[0:1, :], screw[0:1, :],
                                           start=True, stop=True), inc=sT)
    W_("vector", sT, t_bc)
    v_scv = sync_step("vector", lambda E: E.tensor_copy(scv[:, :], pb[0][:, 0:16]), sV)
    sync_step("vector", lambda E: E.tensor_tensor(lncol[:, 0:1], iotac[:, 0:1], scv[:, 1:2],
                                                  OP.mult), sV)
    v_l2 = sync_step("vector", lambda E: E.tensor_scalar_mul(lncol[:, 1:2], lncol[:, 0:1], -1.0), sV)
    W_("scalar", sV, v_l2)
    I_("scalar", lambda E: E.activation(lampp[:, 0:1], lncol[:, 0:1], AF.Exp,
                                        bias=zcol, scale=onecol), inc=sS)
    s_lamp = I_("scalar", lambda E: E.activation(lampi[:, 0:1], lncol[:, 1:2], AF.Exp,
                                                 bias=zcol, scale=onecol), inc=sS)
    I_("vector", lambda E: E.tensor_scalar_mul(bmasks4[:, :], masks4[:, :], scv[:, 2:3]), inc=sV)
    v_bm = I_("vector", lambda E: E.tensor_scalar_mul(bmaski4[:, :], maski4[:, :], scv[:, 2:3]), inc=sV)
    W_("tensor", sLC, ld_done)
    v_gb = 0
    for i in range(2):
        W_("tensor", sV, v_gb)
        I_("tensor", lambda E, i=i: E.matmul(pb[1][:, 0:512], ones1[0:1, :],
                                             gamin[0:1, 512 * i:512 * i + 512],
                                             start=True, stop=True, skip_group_check=True), inc=sT)
        t_g1 = I_("tensor", lambda E, i=i: E.matmul(pb[2][:, 0:512], ones1[0:1, :],
                                                    betin[0:1, 512 * i:512 * i + 512],
                                                    start=True, stop=True, skip_group_check=True), inc=sT)
        W_("vector", sT, t_g1)
        I_("vector", lambda E, i=i: E.tensor_copy(gbg[:, 512 * i:512 * i + 512], pb[1][:, 0:512]), inc=sV)
        v_gb = I_("vector", lambda E, i=i: E.tensor_copy(gbb[:, 512 * i:512 * i + 512], pb[2][:, 0:512]), inc=sV)
    prolog_v = sV.n
    prolog_s = sS.n
    prolog_t = sT.n

    vPar = {}
    swCnt = {}
    M = {}

    def PAB(buf, cc):
        pa = cc % 2
        return buf[:, 1024 * pa:1024 * pa + 1024]

    def K_ap(h, cc):
        j = h // 2
        if h % 2 == 0:
            return PAB(xa_fm, cc)[0:64, 128 * j:128 * j + 128]
        return PAB(kodd, cc)[0:64, 128 * j:128 * j + 128]

    def LNC(cc, k):
        return lncol[:, 16 * (cc % 2) + k:16 * (cc % 2) + k + 1]

    def emit_load(cc):
        if cc >= nch:
            return
        W_("sync", sV, vPar.get(cc - 2, 0))
        M.setdefault(cc, {})["lx"] = I_("sync", lambda E, cc=cc: E.dma_start(
            out=PAB(x_tok, cc), in_=d_x[C * cc:C * cc + C, :]), inc=sLXp[cc % 2], dma=True)

    def emit_B1(cc):
        m = M.setdefault(cc, {})
        pa = cc % 2
        xtk, xntk = PAB(x_tok, cc), PAB(xn_tok, cc)
        xcb, jkb = PAB(xc, cc), PAB(junk, cc)
        p2 = M.get(cc - 2, {})
        W_("vector", sLXp[pa], m["lx"])
        if cc == 0:
            W_("vector", sS, s_lamp)
        W_("vector", sS, p2.get("s8", 0))
        v1 = I_("vector", lambda E, x=xtk: E.reduce_sum(LNC(cc, 2), x, axis=AX), inc=sV)
        W_("scalar", sV, v1)
        s1 = I_("scalar", lambda E: E.mul(LNC(cc, 3), LNC(cc, 2), -1.0 / D), inc=sS)
        W_("vector", sS, s1)
        v2 = I_("vector", lambda E, x=xtk, xc_=xcb: E.tensor_scalar_add(xc_, x, LNC(cc, 3)), inc=sV)
        W_("vector", sV, v2)
        v3 = I_("vector", lambda E, xc_=xcb, jk=jkb: E.scalar_tensor_tensor(
            jk, xc_, 0.0, xc_, OP.add, OP.mult, accum_out=LNC(cc, 4)), inc=sV)
        W_("scalar", sV, v3)
        s2a = I_("scalar", lambda E: E.mul(LNC(cc, 5), LNC(cc, 4), 1.0 / D), inc=sS)
        W_("scalar", sS, s2a)
        s2 = I_("scalar", lambda E: E.add(LNC(cc, 5), LNC(cc, 5), LNEPS), inc=sS)
        W_("vector", sS, s2)
        v4 = I_("vector", lambda E: E.reciprocal(LNC(cc, 6), LNC(cc, 5)), inc=sV)
        W_("scalar", sV, v4)
        s_sq1 = I_("scalar", lambda E: E.activation(LNC(cc, 7), LNC(cc, 6), AF.Sqrt), inc=sS)
        W_("scalar", sS, s_sq1)
        W_("scalar", sSXp[pa], swCnt.get(("x", cc - 2), 0))
        W_("scalar", sT, p2.get("t_tp_last", 0))
        m["s_xn"] = I_("scalar", lambda E, xn=xntk, xc_=xcb: E.activation(
            xn, xc_, AF.Identity, bias=zcol, scale=LNC(cc, 7)), inc=sS)
        W_("gpsimd", sS, m["s_xn"])
        swCnt[("x", cc)] = I_("gpsimd", lambda E, cc=cc, xn=xntk: E.dma_start(
            out=d_xnscr[C * cc:C * cc + C, :], in_=xn), inc=sSXp[pa], dma=True)

    def emit_B1t(cc):
        m = M[cc]
        pa = cc % 2
        xntk = PAB(xn_tok, cc)
        p2 = M.get(cc - 2, {})
        p1m = M.get(cc - 1, {})
        W_("tensor", sS, m["s_xn"])
        if cc == 0:
            W_("tensor", sV, v_bm)
        W_("tensor", sV, p1m.get("v_n_last", 0))
        W_("tensor", sV, p1m.get("v_k1", 0))
        t_tp = []
        for bk in range(2):
            for jj in range(4):
                j = 4 * bk + jj
                t = I_("tensor", lambda E, bk=bk, jj=jj, j=j, xn=xntk: E.matmul(
                    pb[6 + bk][:, 128 * jj:128 * jj + 128], xn[:, 128 * j:128 * j + 128],
                    ident[:, :], is_transpose=True, start=(jj == 0), stop=(jj == 3),
                    skip_group_check=True), inc=sT)
            t_tp.append(t)
        for bk in range(2):
            for jj in range(4):
                j = 4 * bk + jj
                t = I_("tensor", lambda E, bk=bk, jj=jj, j=j, xn=xntk: E.matmul(
                    pb[bk][0:64, 128 * jj:128 * jj + 128],
                    xn[:, 128 * j + 64:128 * j + 128],
                    ident[:, :], is_transpose=True, start=(jj == 0), stop=(jj == 3),
                    skip_group_check=True), inc=sT)
            t_tp.append(t)
        m["t_tp_last"] = sT.n
        s_tanh = []
        W_("scalar", sT, p2.get("t_sd_last", 0))
        for bk in range(2):
            W_("scalar", sT, t_tp[bk])
            s_tanh.append(I_("scalar", lambda E, bk=bk, cc=cc: E.activation(
                PAB(xa_fm, cc)[:, 512 * bk:512 * bk + 512], pb[6 + bk][:, 0:512],
                AF.Tanh), inc=sS))
        for bk in range(2):
            W_("scalar", sT, t_tp[2 + bk])
            s_tanh.append(I_("scalar", lambda E, bk=bk, cc=cc: E.activation(
                PAB(kodd, cc)[0:64, 512 * bk:512 * bk + 512], pb[bk][0:64, 0:512],
                AF.Tanh), inc=sS))
        m["s_tanh"] = s_tanh

    def emit_B2(cc):
        m = M[cc]
        p1m = M.get(cc - 1, {})
        p2 = M.get(cc - 2, {})
        W_("tensor", sS, m["s_tanh"][1])
        W_("tensor", sV, p1m.get("v_o1", 0))
        t_st = []
        for eb in range(2):
            for dc in range(8):
                t = I_("tensor", lambda E, eb=eb, dc=dc, cc=cc: E.matmul(
                    pb[4 + eb][:, 0:512],
                    PAB(xa_fm, cc)[:, 128 * dc:128 * dc + 128],
                    wt_sb[:, 1024 * dc + 512 * eb:1024 * dc + 512 * eb + 512],
                    start=(dc == 0), stop=(dc == 7), skip_group_check=True), inc=sT)
            t_st.append(t)
        v_st = []
        W_("vector", sS, p2.get("s_vd", 0))
        for eb in range(2):
            W_("vector", sT, t_st[eb])
            v_st.append(I_("vector", lambda E, eb=eb, cc=cc: E.tensor_copy(
                PAB(static_t, cc)[:, 512 * eb:512 * eb + 512], pb[4 + eb][:, 0:512]),
                inc=sV))
        m["v_st"] = v_st
        if _taps and cc == 0:
            W_("gpsimd", sV, v_st[1])
            M[cc]["tap_static"] = I_("gpsimd", lambda E, cc=cc: E.dma_start(
                out=d_dbg[0:128, :], in_=PAB(static_t, cc)), inc=sTAP, dma=True)

    def emit_C1(cc):
        m = M[cc]
        p1m = M.get(cc - 1, {})
        s_tanh = m["s_tanh"]
        W_("tensor", sS, s_tanh[3])
        W_("tensor", sV, p1m.get("v_su", 0))
        t_g = []
        for bk in range(4):
            for qq in range(4):
                h = 4 * bk + qq
                t = I_("tensor", lambda E, bk=bk, qq=qq, h=h, cc=cc: E.matmul(
                    pb[bk][:, 128 * qq:128 * qq + 128], K_ap(h, cc), K_ap(h, cc),
                    start=(qq == 0), stop=(qq == 3), skip_group_check=True), inc=sT)
            t_g.append(t)
        W_("vector", sT, p1m.get("t_sd_last", 0))
        v_g = []
        for bk in range(4):
            W_("vector", sT, t_g[bk])
            I_("vector", lambda E, bk=bk: E.tensor_tensor(
                gus[:, 512 * bk:512 * bk + 512], pb[bk][:, 0:512], bmasks4[:, :],
                OP.mult), inc=sV)
            v_g.append(I_("vector", lambda E, bk=bk: E.tensor_tensor(
                gui[:, 512 * bk:512 * bk + 512], pb[bk][:, 0:512], bmaski4[:, :],
                OP.mult), inc=sV))
        m["v_g"] = v_g
        t_tk = []
        for bk in range(2):
            W_("tensor", sV, v_g[bk])
            for qq in range(8):
                h = 8 * bk + qq
                t = I_("tensor", lambda E, bk=bk, qq=qq, h=h, cc=cc: E.matmul(
                    pb[bk][:, 64 * qq:64 * qq + 64], K_ap(h, cc), ident[0:64, 0:64],
                    is_transpose=True, start=(qq == 0), stop=(qq == 7),
                    skip_group_check=True), inc=sT)
            t_tk.append(t)
        v_k = []
        for bk in range(2):
            W_("vector", sT, t_tk[bk])
            v_k.append(I_("vector", lambda E, bk=bk: E.tensor_copy(
                ktok[:, 512 * bk:512 * bk + 512], pb[bk][:, 0:512]), inc=sV))
        m["v_k1"] = v_k[1]
        m["v_k"] = v_k
        W_("scalar", sV, p1m.get("vg1", 0))
        W_("scalar", sV, m["v_st"][1])
        m["s_vd"] = I_("scalar", lambda E, cc=cc: E.activation(
            vd_yn[:, :], PAB(static_t, cc), AF.Identity, bias=zcol,
            scale=lampi[:, 0:1]), inc=sS)
        if cc == 0:
            W_("tensor", sV, v_init)
        t_ks = []
        for bk in range(2):
            W_("tensor", sV, m["v_st"][bk])
            for qq in range(8):
                h = 8 * bk + qq
                t = I_("tensor", lambda E, bk=bk, qq=qq, h=h, cc=cc: E.matmul(
                    pb[4 + bk][:, 64 * qq:64 * qq + 64], K_ap(h, cc),
                    s_sb[0:64, 64 * h:64 * h + 64],
                    start=(qq == 0), stop=(qq == 7), skip_group_check=True), inc=sT)
            t_ks.append(t)
        v_b = []
        for bk in range(2):
            W_("vector", sT, t_ks[bk]); W_("vector", sS, m["s_vd"])
            v_b.append(I_("vector", lambda E, bk=bk: E.tensor_tensor(
                bb[:, 512 * bk:512 * bk + 512], vd_yn[:, 512 * bk:512 * bk + 512],
                pb[4 + bk][:, 0:512], OP.subtract), inc=sV))
        m["v_b"] = v_b
        if _taps and cc == 0:
            W_("gpsimd", sV, v_b[1])
            I_("gpsimd", lambda E: E.dma_start(out=d_dbg[128:256, :], in_=bb[:, :]),
               inc=sTAP, dma=True)
            M[cc]["tap_vd"] = I_("gpsimd", lambda E: E.dma_start(
                out=d_dbg[256:384, :], in_=vd_yn[:, :]), inc=sTAP, dma=True)
        ubufs = [u0, u1]
        v_n = [0, 0]
        if _taps and cc == 1:
            W_("vector", sTAP, sTAP.n)
        W_("vector", sV, v_b[1])
        for it in range(1, NEU + 1):
            src = bb if it == 1 else ubufs[1 - it % 2]
            dst = ubufs[it % 2]
            t_n = []
            for bk in range(2):
                W_("tensor", sV, v_g[3])
                W_("tensor", sV, v_b[bk] if it == 1 else v_n[bk])
                if it == 1:
                    W_("tensor", sS, s_tanh[bk])
                for qq in range(8):
                    h = 8 * bk + qq
                    t = I_("tensor", lambda E, bk=bk, qq=qq, h=h, src=src: E.matmul(
                        pb[6 + bk][:, 64 * qq:64 * qq + 64],
                        gus[:, 128 * h:128 * h + 128], src[:, 64 * h:64 * h + 64],
                        start=(qq == 0), stop=(qq == 7), skip_group_check=True), inc=sT)
                t_n.append(t)
            for bk in range(2):
                W_("vector", sT, t_n[bk])
                v_n[bk] = I_("vector", lambda E, bk=bk, dst=dst: E.tensor_tensor(
                    dst[:, 512 * bk:512 * bk + 512], bb[:, 512 * bk:512 * bk + 512],
                    pb[6 + bk][:, 0:512], OP.subtract), inc=sV)
        m["v_n"] = list(v_n)
        m["v_n_last"] = max(v_n)
        m["ufin"] = ubufs[NEU % 2]
        if _taps and cc == 0:
            W_("gpsimd", sV, m["v_n_last"])
            M[cc]["tap_u"] = I_("gpsimd", lambda E, u=m["ufin"]: E.dma_start(
                out=d_dbg[384:512, :], in_=u[:, :]), inc=sTAP, dma=True)

    def emit_C2a(cc):
        m = M[cc]
        ufin = m["ufin"]
        t_o = []
        for bk in range(2):
            W_("tensor", sV, m["v_n"][bk])
            W_("tensor", sV, m["v_b"][bk])
            for qq in range(8):
                h = 8 * bk + qq
                t = I_("tensor", lambda E, bk=bk, qq=qq, h=h, u=ufin: E.matmul(
                    pb[4 + bk][:, 64 * qq:64 * qq + 64],
                    gui[:, 128 * h:128 * h + 128], u[:, 64 * h:64 * h + 64],
                    start=False, stop=(qq == 7), skip_group_check=True), inc=sT)
            t_o.append(t)
        m["t_o"] = t_o
        v_o = []
        if _taps and cc == 0 and "tap_static" in m:
            W_("vector", sTAP, sTAP.n)
        for bk in range(2):
            W_("vector", sT, t_o[bk])
            v_o.append(I_("vector", lambda E, bk=bk, cc=cc: E.scalar_tensor_tensor(
                PAB(static_t, cc)[:, 512 * bk:512 * bk + 512], pb[4 + bk][:, 0:512],
                lampp[:, 0:1], PAB(static_t, cc)[:, 512 * bk:512 * bk + 512],
                OP.mult, OP.add), inc=sV))
        m["v_o"] = v_o
        m["v_o1"] = v_o[1]

    def emit_C2b(cc):
        m = M[cc]
        ufin = m["ufin"]
        t_sd = []
        for bk in range(2):
            W_("tensor", sV, m["v_k"][bk]); W_("tensor", sV, m["v_g"][2 + bk])
            for qq in range(8):
                h = 8 * bk + qq
                t = I_("tensor", lambda E, bk=bk, qq=qq, h=h, u=ufin: E.matmul(
                    pb[2 + bk][0:64, 64 * qq:64 * qq + 64],
                    ktok[:, 64 * h:64 * h + 64], u[:, 64 * h:64 * h + 64],
                    start=(qq == 0), stop=(qq == 7), skip_group_check=True), inc=sT)
            t_sd.append(t)
        m["t_sd_last"] = sT.n
        W_("vector", sT, t_sd[0]); W_("vector", sT, t_sd[1])
        v_sm2 = 0
        for bk in range(2):
            v_sm2 = I_("vector", lambda E, bk=bk: E.tensor_scalar_mul(
                s_sb[0:64, 512 * bk:512 * bk + 512],
                s_sb[0:64, 512 * bk:512 * bk + 512], lamc_col[0:64, :]), inc=sV)
        W_("vector", sV, v_sm2)
        v_su = 0
        for bk in range(2):
            v_su = I_("vector", lambda E, bk=bk: E.scalar_tensor_tensor(
                s_sb[0:64, 512 * bk:512 * bk + 512], pb[2 + bk][0:64, 0:512],
                blamc_col[0:64, :], s_sb[0:64, 512 * bk:512 * bk + 512],
                OP.mult, OP.add), inc=sV)
        m["v_su"] = v_su

    def emit_D(cc):
        m = M[cc]
        pa = cc % 2
        xtk, ptk = PAB(x_tok, cc), PAB(part2, cc)
        xcb, jkb = PAB(xc, cc), PAB(junk, cc)
        stt_ = PAB(static_t, cc)
        if _taps and cc == 0:
            W_("gpsimd", sV, m["v_o1"])
            I_("gpsimd", lambda E, s=stt_: E.dma_start(out=d_dbg[512:640, :], in_=s),
               inc=sTAP, dma=True)
            W_("gpsimd", sS, s_lamp)
            I_("gpsimd", lambda E: E.dma_start(out=d_dbg[640:641, 0:1], in_=lampp[0:1, 0:1]),
               inc=sTAP, dma=True)
            I_("gpsimd", lambda E: E.dma_start(out=d_dbg[641:642, 0:16], in_=scv[0:1, :]),
               inc=sTAP, dma=True)
            I_("gpsimd", lambda E: E.dma_start(out=d_dbg[642:643, 0:1], in_=lampi[0:1, 0:1]),
               inc=sTAP, dma=True)
        W_("vector", sV, m["v_o1"])
        v5 = I_("vector", lambda E, s=stt_: E.reduce_sum(LNC(cc, 8), s, axis=AX), inc=sV)
        W_("scalar", sV, v5)
        s5 = I_("scalar", lambda E: E.mul(LNC(cc, 9), LNC(cc, 8), -1.0 / D), inc=sS)
        W_("vector", sS, s5)
        v6 = I_("vector", lambda E, s=stt_, xc_=xcb: E.tensor_scalar_add(
            xc_, s, LNC(cc, 9)), inc=sV)
        W_("vector", sV, v6)
        v7 = I_("vector", lambda E, xc_=xcb, jk=jkb: E.scalar_tensor_tensor(
            jk, xc_, 0.0, xc_, OP.add, OP.mult, accum_out=LNC(cc, 10)), inc=sV)
        W_("scalar", sV, v7)
        s6a = I_("scalar", lambda E: E.mul(LNC(cc, 11), LNC(cc, 10), 1.0 / D), inc=sS)
        W_("scalar", sS, s6a)
        s6 = I_("scalar", lambda E: E.add(LNC(cc, 11), LNC(cc, 11), LNEPS), inc=sS)
        W_("vector", sS, s6)
        v8 = I_("vector", lambda E: E.reciprocal(LNC(cc, 12), LNC(cc, 11)), inc=sV)
        W_("scalar", sV, v8)
        s_sq2 = I_("scalar", lambda E: E.activation(LNC(cc, 13), LNC(cc, 12), AF.Sqrt), inc=sS)
        W_("scalar", sS, s_sq2)
        if _taps and cc == 0 and "tap_vd" in m:
            W_("scalar", sTAP, sTAP.n)
        W_("scalar", sV, m["v_b"][1])
        s8 = I_("scalar", lambda E, xc_=xcb: E.activation(
            vd_yn[:, :], xc_, AF.Identity, bias=zcol, scale=LNC(cc, 13)), inc=sS)
        m["s8"] = s8
        W_("vector", sS, s8)
        W_("vector", sSPp[pa], swCnt.get(("p", cc - 2), 0))
        vg1 = I_("vector", lambda E, jk=jkb: E.tensor_mul(jk, vd_yn[:, :], gbg[:, :]), inc=sV)
        m["vg1"] = vg1
        W_("vector", sV, vg1)
        vg2 = I_("vector", lambda E, jk=jkb: E.tensor_add(jk, jk, gbb[:, :]), inc=sV)
        W_("vector", sV, vg2)
        vP = I_("vector", lambda E, x=xtk, p=ptk, jk=jkb: E.tensor_add(p, jk, x), inc=sV)
        vPar[cc] = vP
        W_("gpsimd", sV, vP)
        swCnt[("p", cc)] = I_("gpsimd", lambda E, cc=cc, p=ptk: E.dma_start(
            out=d_pscr[C * cc:C * cc + C, :], in_=p), inc=sSPp[pa], dma=True)

    emit_load(0)
    emit_load(1)
    if nch > 0:
        emit_B1(0)
        emit_B1t(0)
        emit_B2(0)
    for cc in range(nch):
        emit_C1(cc)
        if cc + 1 < nch:
            emit_B1(cc + 1)
        emit_C2a(cc)
        emit_C2b(cc)
        if cc + 1 < nch:
            emit_B1t(cc + 1)
            emit_B2(cc + 1)
        emit_D(cc)
        emit_load(cc + 2)


    fins = ((sT, sT.n), (sV, sV.n), (sS, sS.n),
            (sSXp[0], sSXp[0].n), (sSXp[1], sSXp[1].n),
            (sSPp[0], sSPp[0].n), (sSPp[1], sSPp[1].n),
            (sLXp[0], sLXp[0].n), (sLXp[1], sLXp[1].n),
            (sLC, sLC.n), (sTAP, sTAP.n))
    for eng in ENGS:
        for sem, val in fins:
            W_(eng, sem, val)

    GIOA = IndirectOffsetOnAxis

    v_i0 = I_("vector", lambda E: E.tensor_scalar_add(iota0[:, 0:1], iotac[:, 0:1], -1.0), inc=sV)

    z_ms = I_("vector", lambda E: E.memset(zeri[:, :], 0), inc=sV)
    W_("gpsimd", sV, z_ms)
    for i in range(NROWS // 4096):
        I_("gpsimd", lambda E, i=i: E.dma_start(
            out=d_idxmap[4096 * i:4096 * i + 4096, 0:1].rearrange("(p f) o -> p (f o)", p=128),
            in_=zeri[:, 0:32]), inc=sLQ, dma=True)
        I_("gpsimd", lambda E, i=i: E.dma_start(
            out=d_pvals[4096 * i:4096 * i + 4096, 0:1].rearrange("(p f) o -> p (f o)", p=128),
            in_=zeri[:, 0:32]), inc=sLQ, dma=True)
    zinit_mark = sLQ.n

    xnld_p = {0: 0, 1: 0}
    t_tr = {}
    t_pl = 0
    for t8 in range(NTIL):
        W_("sync", sT, xnld_p[t8 % 2])
        lxv = I_("sync", lambda E, t8=t8: E.dma_start(
            out=xn_ld[:, 1024 * (t8 % 2):1024 * (t8 % 2) + 1024],
            in_=d_xnscr[128 * t8:128 * t8 + 128, :]), inc=sLXp[t8 % 2], dma=True)
        W_("tensor", sLXp[t8 % 2], lxv)
        for bk in range(2):
            if t8 > 0:
                W_("tensor", sV, t_tr.get("evac%d_%d" % (bk, t8 - 1), 0))
            for jj in range(4):
                dc = 4 * bk + jj
                t = I_("tensor", lambda E, t8=t8, bk=bk, jj=jj, dc=dc: E.matmul(
                    pb[bk][:, 128 * jj:128 * jj + 128],
                    xn_ld[:, 1024 * (t8 % 2) + 128 * dc:1024 * (t8 % 2) + 128 * dc + 128],
                    ident[:, :], is_transpose=True, start=(jj == 0), stop=(jj == 3),
                    skip_group_check=True), inc=sT)
            t_tr["tp%d_%d" % (bk, t8)] = t
        xnld_p[t8 % 2] = sT.n
        ev_last = 0
        for bk in range(2):
            W_("vector", sT, t_tr["tp%d_%d" % (bk, t8)])
            for jj in range(4):
                dc = 4 * bk + jj
                ev_last = I_("vector", lambda E, bk=bk, jj=jj, dc=dc: E.tensor_copy(
                    fm_tmp[:, 128 * dc:128 * dc + 128],
                    pb[bk][:, 128 * jj:128 * jj + 128]), inc=sV)
            t_tr["evac%d_%d" % (bk, t8)] = ev_last
        W_("tensor", sV, ev_last)
        for dc in range(8):
            t_pl = I_("tensor", lambda E, t8=t8, dc=dc: E.matmul(
                pb[2][:, 8 * t8:8 * t8 + 8],
                fm_tmp[:, 128 * dc:128 * dc + 128], rwt[:, 8 * dc:8 * dc + 8],
                start=(t8 == 0 and dc == 0), stop=(dc == 7),
                skip_group_check=True), inc=sT)

    def vgroup(build_fn):
        last = 0
        for tt in range(NTIL):
            last = I_("vector", build_fn(tt), inc=sV if tt == NTIL - 1 else None)
        W_("vector", sV, last)
        return last

    def sgroup(build_fn):
        last = 0
        for tt in range(NTIL):
            last = I_("scalar", build_fn(tt), inc=sS if tt == NTIL - 1 else None)
        W_("scalar", sS, last)
        return last

    W_("vector", sT, t_pl)
    v_m1 = vgroup(lambda tt: lambda E, tt=tt: E.reduce_max(
        pcol[:, 8 * tt:8 * tt + 1], pb[2][:, 8 * tt:8 * tt + 8], axis=AX))
    W_("scalar", sV, v_m1)
    sgroup(lambda tt: lambda E, tt=tt: E.mul(
        pcol[:, 8 * tt + 1:8 * tt + 2], pcol[:, 8 * tt:8 * tt + 1], -1.0))
    s_exp = sgroup(lambda tt: lambda E, tt=tt: E.activation(
        e_sb[:, 8 * tt:8 * tt + 8], pb[2][:, 8 * tt:8 * tt + 8], AF.Exp,
        bias=pcol[:, 8 * tt + 1:8 * tt + 2], scale=onecol))
    W_("vector", sS, s_exp)
    vgroup(lambda tt: lambda E, tt=tt: E.reduce_sum(
        pcol[:, 8 * tt + 2:8 * tt + 3], e_sb[:, 8 * tt:8 * tt + 8], axis=AX))
    v_ri = vgroup(lambda tt: lambda E, tt=tt: E.reciprocal(
        pcol[:, 8 * tt + 3:8 * tt + 4], pcol[:, 8 * tt + 2:8 * tt + 3]))
    v_sm = vgroup(lambda tt: lambda E, tt=tt: E.reduce_max(
        pcol[:, 8 * tt + 7:8 * tt + 8], e_sb[:, 8 * tt:8 * tt + 8], axis=AX))
    W_("scalar", sV, v_ri)
    s_pr = sgroup(lambda tt: lambda E, tt=tt: E.activation(
        pr_sb[:, 8 * tt:8 * tt + 8], e_sb[:, 8 * tt:8 * tt + 8], AF.Identity,
        bias=zcol, scale=pcol[:, 8 * tt + 3:8 * tt + 4]))
    W_("vector", sV, v_sm)
    vgroup(lambda tt: lambda E, tt=tt: E.tensor_single_scalar(
        sc1[:, 8 * tt:8 * tt + 8], e_sb[:, 8 * tt:8 * tt + 8],
        pcol[:, 8 * tt + 7:8 * tt + 8], OP.is_equal))
    vgroup(lambda tt: lambda E, tt=tt: E.tensor_tensor(
        sc1[:, 8 * tt:8 * tt + 8], sc1[:, 8 * tt:8 * tt + 8], revi8[:, :], OP.mult))
    vgroup(lambda tt: lambda E, tt=tt: E.reduce_max(
        pcol[:, 8 * tt + 4:8 * tt + 5], sc1[:, 8 * tt:8 * tt + 8], axis=AX))
    vgroup(lambda tt: lambda E, tt=tt: E.tensor_single_scalar(
        sc1[:, 8 * tt:8 * tt + 8], sc1[:, 8 * tt:8 * tt + 8],
        pcol[:, 8 * tt + 4:8 * tt + 5], OP.is_equal))
    vgroup(lambda tt: lambda E, tt=tt: E.tensor_copy(
        m1_sb[:, 8 * tt:8 * tt + 8], sc1[:, 8 * tt:8 * tt + 8]))
    W_("vector", sS, s_pr)
    vgroup(lambda tt: lambda E, tt=tt: E.scalar_tensor_tensor(
        e_sb[:, 8 * tt:8 * tt + 8], sc1[:, 8 * tt:8 * tt + 8], -2.0,
        e_sb[:, 8 * tt:8 * tt + 8], OP.mult, OP.add))
    vgroup(lambda tt: lambda E, tt=tt: E.reduce_max(
        pcol[:, 8 * tt + 5:8 * tt + 6], e_sb[:, 8 * tt:8 * tt + 8], axis=AX))
    vgroup(lambda tt: lambda E, tt=tt: E.tensor_single_scalar(
        sc2[:, 8 * tt:8 * tt + 8], e_sb[:, 8 * tt:8 * tt + 8],
        pcol[:, 8 * tt + 5:8 * tt + 6], OP.is_equal))
    vgroup(lambda tt: lambda E, tt=tt: E.tensor_tensor(
        sc2[:, 8 * tt:8 * tt + 8], sc2[:, 8 * tt:8 * tt + 8], revi8[:, :], OP.mult))
    vgroup(lambda tt: lambda E, tt=tt: E.reduce_max(
        pcol[:, 8 * tt + 6:8 * tt + 7], sc2[:, 8 * tt:8 * tt + 8], axis=AX))
    vgroup(lambda tt: lambda E, tt=tt: E.tensor_single_scalar(
        sc2[:, 8 * tt:8 * tt + 8], sc2[:, 8 * tt:8 * tt + 8],
        pcol[:, 8 * tt + 6:8 * tt + 7], OP.is_equal))
    vgroup(lambda tt: lambda E, tt=tt: E.tensor_add(
        sc1[:, 8 * tt:8 * tt + 8], sc1[:, 8 * tt:8 * tt + 8], sc2[:, 8 * tt:8 * tt + 8]))
    v_lp = vgroup(lambda tt: lambda E, tt=tt: E.tensor_mul(
        lp_sb[:, 8 * tt:8 * tt + 8], sc1[:, 8 * tt:8 * tt + 8],
        pr_sb[:, 8 * tt:8 * tt + 8]))

    t_cnt = 0
    W_("tensor", sV, v_lp)
    for tt in range(NTIL):
        t_cnt = I_("tensor", lambda E, tt=tt: E.matmul(
            pb[3][0:8, 0:1], sc1[:, 8 * tt:8 * tt + 8], onecol[:, 0:1],
            start=(tt == 0), stop=(tt == NTIL - 1), skip_group_check=True), inc=sT)
    W_("vector", sT, t_cnt)

    def vstep(build, eng="vector"):
        sem = {"vector": sV, "scalar": sS}[eng]
        v = I_(eng, build, inc=sem)
        W_(eng, sem, v)
        return v

    vstep(lambda E: E.tensor_copy(cntc[0:8, 0:1], pb[3][0:8, 0:1]))
    vstep(lambda E: E.tensor_single_scalar(s24[0:8, 24:24 + 16], th16[0:8, :],
                                           cntc[0:8, 0:1], OP.is_lt))
    vstep(lambda E: E.reduce_sum(cntc[0:8, 3:4], s24[0:8, 24:24 + 16], axis=AX))
    v_pad = vstep(lambda E: E.tensor_scalar_mul(cntc[0:8, 4:5], cntc[0:8, 3:4], float(SROWS)))
    W_("tensor", sV, v_pad)
    t_po = I_("tensor", lambda E: E.matmul(pb[3][0:8, 8:9], tri8[:, :], cntc[0:8, 4:5],
                                           start=False, stop=True, skip_group_check=True), inc=sT)
    W_("vector", sT, t_po)
    v_poc = vstep(lambda E: E.tensor_copy(cntc[0:8, 5:6], pb[3][0:8, 8:9]))
    W_("tensor", sV, v_poc)
    t_por = I_("tensor", lambda E: E.matmul(pb[3][0:1, 16:24], cntc[0:8, 5:6], ident[0:8, 0:8],
                                            is_transpose=True, start=False, stop=True,
                                            skip_group_check=True), inc=sT)
    W_("vector", sT, t_por)
    v_porow = vstep(lambda E: E.tensor_copy(rows64[0:1, 0:8], pb[3][0:1, 16:24]))
    W_("tensor", sV, v_porow)
    t_pob = I_("tensor", lambda E: E.matmul(pb[1][:, 0:8], ones1[0:1, :], rows64[0:1, 0:8],
                                            start=True, stop=True, skip_group_check=True), inc=sT)
    W_("vector", sT, t_pob)
    vstep(lambda E: E.tensor_copy(s24[:, 0:8], pb[1][:, 0:8]))
    vstep(lambda E: E.tensor_scalar_mul(cntc[:, 6:7], iota0[:, 0:1], float(SROWS)))
    vstep(lambda E: E.tensor_single_scalar(s24[:, 8:16], s24[:, 0:8], cntc[:, 6:7], OP.is_le))
    vstep(lambda E: E.reduce_sum(cntc[:, 8:9], s24[:, 8:16], axis=AX))
    vstep(lambda E: E.tensor_scalar_add(cntc[:, 9:10], cntc[:, 8:9], -1.0))
    vstep(lambda E: E.tensor_scalar_mul(cntc[:, 10:11], cntc[:, 9:10], float(22 * 128)))
    v_eb = vstep(lambda E: E.tensor_scalar_mul(cntc[:, 11:12], cntc[:, 9:10], float(F)))
    W_("tensor", sV, v_eb)
    t_e1r = I_("tensor", lambda E: E.matmul(pb[3][0:1, 32:32 + NSLOT], cntc[0:NSLOT, 10:11],
                                            ident[0:NSLOT, 0:NSLOT], is_transpose=True,
                                            start=False, stop=True, skip_group_check=True), inc=sT)
    t_e2r = I_("tensor", lambda E: E.matmul(pb[3][0:1, 64:64 + NSLOT], cntc[0:NSLOT, 11:12],
                                            ident[0:NSLOT, 0:NSLOT], is_transpose=True,
                                            start=False, stop=True, skip_group_check=True), inc=sT)
    W_("vector", sT, t_e2r)
    vstep(lambda E: E.tensor_copy(rows64[0:1, 8:8 + NSLOT], pb[3][0:1, 32:32 + NSLOT]))
    v_ebr = vstep(lambda E: E.tensor_copy(rows64[0:1, 32:32 + NSLOT], pb[3][0:1, 64:64 + NSLOT]))
    W_("tensor", sV, v_ebr)
    t_eb1 = I_("tensor", lambda E: E.matmul(pb[1][:, 8:8 + NSLOT], ones1[0:1, :],
                                            rows64[0:1, 8:8 + NSLOT], start=False, stop=True,
                                            skip_group_check=True), inc=sT)
    t_eb2 = I_("tensor", lambda E: E.matmul(pb[1][:, 64:64 + NSLOT], ones1[0:1, :],
                                            rows64[0:1, 32:32 + NSLOT], start=False, stop=True,
                                            skip_group_check=True), inc=sT)
    W_("vector", sT, t_eb2)
    vstep(lambda E: E.tensor_copy(ebas[:, 0:NSLOT], pb[1][:, 8:8 + NSLOT]))
    v_ebas = vstep(lambda E: E.tensor_copy(ebas[:, 24:24 + NSLOT], pb[1][:, 64:64 + NSLOT]))

    prev_vcu = vstep(lambda E: E.tensor_copy(rows64[0:1, 56:64], rows64[0:1, 0:8]))
    sc_marks = 0
    for tt in range(NTIL):
        W_("tensor", sV, prev_vcu)
        t_rk = I_("tensor", lambda E, tt=tt: E.matmul(
            pb[0][:, 0:8], masks4[:, 0:128], sc1[:, 8 * tt:8 * tt + 8],
            start=True, stop=True, skip_group_check=True), inc=sT)
        t_cb = I_("tensor", lambda E: E.matmul(
            pb[0][:, 8:16], ones1[0:1, :], rows64[0:1, 56:64],
            start=False, stop=True, skip_group_check=True), inc=sT)
        t_tsum = I_("tensor", lambda E, tt=tt: E.matmul(
            pb[0][0:1, 16:24], onecol[:, 0:1], sc1[:, 8 * tt:8 * tt + 8],
            start=False, stop=True, skip_group_check=True), inc=sT)
        W_("vector", sT, t_tsum)
        W_("vector", sT, t_cb)
        v_t8 = vstep(lambda E: E.tensor_copy(cb_sb[:, 0:8], pb[0][:, 8:16]))
        v_t9 = vstep(lambda E: E.tensor_tensor(tmp8[:, 0:8], pb[0][:, 0:8], cb_sb[:, 0:8], OP.add))
        vstep(lambda E, tt=tt: E.tensor_tensor(
            m2tmp[:, 0:8], sc1[:, 8 * tt:8 * tt + 8], m1_sb[:, 8 * tt:8 * tt + 8], OP.subtract))
        vstep(lambda E, tt=tt: E.scalar_tensor_tensor(
            junk8[:, 0:8], tmp8[:, 0:8], 0.0, m1_sb[:, 8 * tt:8 * tt + 8],
            OP.add, OP.mult, accum_out=poscols[:, 2 * tt:2 * tt + 1]))
        vstep(lambda E, tt=tt: E.scalar_tensor_tensor(
            junk8[:, 0:8], tmp8[:, 0:8], 0.0, m2tmp[:, 0:8],
            OP.add, OP.mult, accum_out=poscols[:, 2 * tt + 1:2 * tt + 2]))
        vstep(lambda E, tt=tt: E.scalar_tensor_tensor(
            junk8[:, 0:8], lp_sb[:, 8 * tt:8 * tt + 8], 0.0, m1_sb[:, 8 * tt:8 * tt + 8],
            OP.add, OP.mult, accum_out=pv12[:, 0:1]))
        vstep(lambda E, tt=tt: E.scalar_tensor_tensor(
            junk8[:, 0:8], lp_sb[:, 8 * tt:8 * tt + 8], 0.0, m2tmp[:, 0:8],
            OP.add, OP.mult, accum_out=pv12[:, 1:2]))
        vstep(lambda E, tt=tt: E.tensor_scalar_add(tokf[:, 0:1], iota0[:, 0:1], float(128 * tt)))
        vstep(lambda E: E.tensor_copy(toki[:, 0:1], tokf[:, 0:1]))
        vstep(lambda E, tt=tt: E.tensor_copy(posi1[:, 0:1], poscols[:, 2 * tt:2 * tt + 1]))
        v_cast = vstep(lambda E, tt=tt: E.tensor_copy(posi2[:, 0:1], poscols[:, 2 * tt + 1:2 * tt + 2]))
        v_cu = vstep(lambda E: E.tensor_tensor(
            rows64[0:1, 56:64], rows64[0:1, 56:64], pb[0][0:1, 16:24], OP.add))
        prev_vcu = v_cu
        W_("gpsimd", sLQ, zinit_mark)
        W_("gpsimd", sV, v_cast)
        I_("gpsimd", lambda E: E.indirect_dma_start(
            d_idxmap[0:NROWS, 0:1], GIOA(posi1[:, 0:1], 0), toki[:, 0:1], None),
            inc=sLQ, dma=True)
        I_("gpsimd", lambda E: E.indirect_dma_start(
            d_idxmap[0:NROWS, 0:1], GIOA(posi2[:, 0:1], 0), toki[:, 0:1], None),
            inc=sLQ, dma=True)
        I_("gpsimd", lambda E: E.indirect_dma_start(
            d_pvals[0:NROWS, 0:1], GIOA(posi1[:, 0:1], 0), pv12[:, 0:1], None),
            inc=sLQ, dma=True)
        sc_marks = I_("gpsimd", lambda E: E.indirect_dma_start(
            d_pvals[0:NROWS, 0:1], GIOA(posi2[:, 0:1], 0), pv12[:, 1:2], None),
            inc=sLQ, dma=True)
        W_("vector", sLQ, sc_marks)

    w1g_p = {0: [], 1: []}
    w2g_p = {0: 0, 1: 0}
    t_w1free = {0: 0, 1: 0}
    t_w2free = {0: 0, 1: 0}
    v_idxw_free = {0: 0, 1: 0}
    g_xng = {0: 0, 1: 0}
    t_xng_free = {0: 0, 1: 0}
    v_yfree = [0, 0]
    v_oev = [0, 0, 0, 0]
    s_silfree = [0, 0]
    st_out = {0: 0, 1: 0}
    W_("tensor", sV, sV.n)
    W_("tensor", sS, sS.n)

    prep_ev = {0: (0, 0), 1: (0, 0)}
    w1mk_all = {}

    def emit_prep(s):
        sp = s % 2
        W_("vector", sV, v_ebas)
        for sem, val in w1g_p[sp]:
            W_("vector", sem, val)
        vstep(lambda E, s=s: E.tensor_tensor(tokf[:, 0:1], ebas[:, s:s + 1], iota0[:, 0:1], OP.add))
        vstep(lambda E, s=s, sp=sp: E.tensor_copy(idxw[:, 4 * sp:4 * sp + 1], tokf[:, 0:1]))
        vstep(lambda E, s=s: E.tensor_tensor(tokf[:, 0:1], ebas[:, 24 + s:24 + s + 1],
                                             iota0[:, 0:1], OP.add))
        v_idxw = vstep(lambda E, s=s, sp=sp: E.tensor_copy(idxw[:, 4 * sp + 1:4 * sp + 2],
                                                           tokf[:, 0:1]))
        M_idxw[sp] = v_idxw
        W_("sync", sLQ, sc_marks)
        W_("sync", sGXp[0], g_xng[0]); W_("sync", sGXp[1], g_xng[1])
        W_("sync", sV, max(v_oev))
        ld_tok = 0
        for tt in range(4):
            I_("sync", lambda E, s=s, tt=tt, sp=sp: E.dma_start(
                out=tok32[:, 4 * sp + tt:4 * sp + tt + 1],
                in_=d_idxmap[SROWS * s + 128 * tt:SROWS * s + 128 * tt + 128, 0:1]),
                inc=sLXp[0], dma=True)
            ld_tok = I_("sync", lambda E, s=s, tt=tt, sp=sp: E.dma_start(
                out=pvs[:, 4 * sp + tt:4 * sp + tt + 1],
                in_=d_pvals[SROWS * s + 128 * tt:SROWS * s + 128 * tt + 128, 0:1]),
                inc=sLXp[0], dma=True)
        ev = 0
        for tt in range(4):
            gp = tt % 2
            W_("gpsimd", sLXp[0], ld_tok)
            W_("gpsimd", sT, t_xng_free[gp])
            g_xng[gp] = I_("gpsimd", lambda E, tt=tt, gp=gp, sp=sp: E.indirect_dma_start(
                xng[:, 1024 * gp:1024 * gp + 1024], None,
                d_xnscr[0:L, :], GIOA(tok32[:, 4 * sp + tt:4 * sp + tt + 1], 0)),
                inc=sGXp[gp], dma=True)
            W_("tensor", sGXp[gp], g_xng[gp])
            W_("tensor", sV, v_oev[0]); W_("tensor", sV, v_oev[1])
            for bk in range(2):
                W_("tensor", sV, t_tr.get("sev%d" % bk, 0))
                for jj in range(4):
                    dc = 4 * bk + jj
                    t = I_("tensor", lambda E, tt=tt, gp=gp, bk=bk, jj=jj, dc=dc: E.matmul(
                        pb[4 + bk][:, 128 * jj:128 * jj + 128],
                        xng[:, 1024 * gp + 128 * dc:1024 * gp + 128 * dc + 128],
                        ident[:, :], is_transpose=True, start=(jj == 0), stop=(jj == 3),
                        skip_group_check=True), inc=sT)
                t_tr["stp%d" % bk] = t
            t_xng_free[gp] = sT.n
            for bk in range(2):
                W_("vector", sT, t_tr["stp%d" % bk])
                for jj in range(4):
                    dc = 4 * bk + jj
                    ev = I_("vector", lambda E, tt=tt, bk=bk, jj=jj, dc=dc, sp=sp: E.tensor_copy(
                        xfm[:, 4096 * sp + 512 * dc + 128 * tt:4096 * sp + 512 * dc + 128 * tt + 128],
                        pb[4 + bk][:, 128 * jj:128 * jj + 128]), inc=sV)
                t_tr["sev%d" % bk] = ev
        prep_ev[sp] = (ev, ld_tok)
        w1mk_all[s] = {}
        for fj in (0, 1):
            if fj < NFJ:
                w1mk_all[s][fj] = emit_w1g(fj, s, sp, v_idxw)

    def emit_w1g(fj, s, sp, v_idxw):
        fp = fj % 2
        W_("gpsimd", sV, v_idxw)
        W_("gpsimd", sT, t_w1free[fp])
        mk = 0
        for ab, buf in ((0, w1ab), (1, w1bb)):
            mk = I_("gpsimd", lambda E, s=s, fj=fj, fp=fp, ab=ab, buf=buf, sp=sp: E.indirect_dma_start(
                buf[:, 1024 * fp:1024 * fp + 1024], None,
                d_w1t[0:W1ROWS, :], GIOA(idxw[:, 4 * sp:4 * sp + 1], 0),
                element_offset=128 * (fj + 11 * ab) * 1024), inc=sW1p[fp], dma=True)
        return mk

    M_idxw = {0: 0, 1: 0}
    emit_prep(0)
    for s in range(NSLOT):
        sp = s % 2
        v_idxw = M_idxw[sp]
        w1mk = w1mk_all[s]
        v_xfm, ld_tok = prep_ev[sp]
        W_("tensor", sV, v_xfm)
        for fj in range(NFJ):
            fp = fj % 2
            rot = fj % 2
            bA, bB = 2 * rot, 2 * rot + 1
            W_("tensor", sW1p[fp], w1mk[fj])
            W_("tensor", sV, v_yfree[rot])
            W_("tensor", sS, s_silfree[rot])
            for dc in range(8):
                I_("tensor", lambda E, fj=fj, fp=fp, dc=dc, bA=bA, sp=sp: E.matmul(
                    pb[bA][:, 0:512],
                    w1ab[:, 1024 * fp + 128 * dc:1024 * fp + 128 * dc + 128],
                    xfm[:, 4096 * sp + 512 * dc:4096 * sp + 512 * dc + 512],
                    start=(dc == 0), stop=(dc == 7), skip_group_check=True), inc=sT)
            t_y1 = sT.n
            for dc in range(8):
                I_("tensor", lambda E, fj=fj, fp=fp, dc=dc, bB=bB, sp=sp: E.matmul(
                    pb[bB][:, 0:512],
                    w1bb[:, 1024 * fp + 128 * dc:1024 * fp + 128 * dc + 128],
                    xfm[:, 4096 * sp + 512 * dc:4096 * sp + 512 * dc + 512],
                    start=(dc == 0), stop=(dc == 7), skip_group_check=True), inc=sT)
            t_y2 = sT.n
            t_w1free[fp] = sT.n
            if fj + 2 < NFJ:
                w1mk[fj + 2] = emit_w1g(fj + 2, s, sp, v_idxw)
            W_("scalar", sT, t_y1)
            W_("scalar", sV, v_yfree[rot])
            s_sil = I_("scalar", lambda E, rot=rot, bA=bA: E.activation(
                silb[:, 512 * rot:512 * rot + 512], pb[bA][:, 0:512], AF.Silu), inc=sS)
            s_silfree[rot] = s_sil
            W_("vector", sS, s_sil); W_("vector", sT, t_y2)
            v_yfree[rot] = I_("vector", lambda E, fj=fj, rot=rot, bB=bB: E.tensor_tensor(
                g_sl[:, 512 * fj:512 * fj + 512],
                silb[:, 512 * rot:512 * rot + 512], pb[bB][:, 0:512], OP.mult), inc=sV)
        v_gdone = sV.n
        if s + 1 < NSLOT:
            emit_prep(s + 1)
        W_("tensor", sV, v_gdone)
        W_("tensor", sV, prep_ev[(s + 1) % 2][0] if s + 1 < NSLOT else 0)
        o_done = 0
        for hh in range(2):
            w2mk = {}

            def emit_w2g(fj, s=s, sp=sp, v_idxw=v_idxw):
                fp = fj % 2
                W_("gpsimd", sV, v_idxw)
                W_("gpsimd", sT, t_w2free[fp])
                return I_("gpsimd", lambda E, s=s, fj=fj, fp=fp, sp=sp: E.indirect_dma_start(
                    w2b[:, 1024 * fp:1024 * fp + 1024], None,
                    d_w2[0:W2ROWS, :], GIOA(idxw[:, 4 * sp + 1:4 * sp + 2], 0),
                    element_offset=128 * fj * 1024), inc=sW2p[fp], dma=True)
            w2mk[0] = emit_w2g(0)
            if NFJ > 1:
                w2mk[1] = emit_w2g(1)
            for fj in range(NFJ):
                fp = fj % 2
                W_("tensor", sW2p[fp], w2mk[fj])
                for tt in range(4):
                    bk = 4 + tt
                    if fj == 0:
                        W_("tensor", sV, v_oev[tt])
                    I_("tensor", lambda E, hh=hh, fj=fj, fp=fp, tt=tt, bk=bk: E.matmul(
                        pb[bk][:, 0:512],
                        g_sl[:, 512 * fj + 128 * tt:512 * fj + 128 * tt + 128],
                        w2b[:, 1024 * fp + 512 * hh:1024 * fp + 512 * hh + 512],
                        start=(fj == 0), stop=(fj == NFJ - 1), skip_group_check=True), inc=sT)
                t_w2free[fp] = sT.n
                if fj + 2 < NFJ:
                    w2mk[fj + 2] = emit_w2g(fj + 2)
            o_done = sT.n
            for tt in range(4):
                bk = 4 + tt
                W_("vector", sT, o_done)
                W_("vector", sLXp[0], ld_tok)
                W_("vector", sSXp[sp], st_out[sp])
                v_oev[tt] = I_("vector", lambda E, hh=hh, tt=tt, bk=bk, sp=sp: E.tensor_scalar_mul(
                    outb[:, 4096 * sp + 1024 * tt + 512 * hh:4096 * sp + 1024 * tt + 512 * hh + 512],
                    pb[bk][:, 0:512], pvs[:, 4 * sp + tt:4 * sp + tt + 1]), inc=sV)
        so_mark = 0
        for tt in range(4):
            W_("gpsimd", sV, v_oev[tt])
            so_mark = I_("gpsimd", lambda E, s=s, tt=tt, sp=sp: E.dma_start(
                out=d_oscr[SROWS * s + 128 * tt:SROWS * s + 128 * tt + 128, :],
                in_=outb[:, 4096 * sp + 1024 * tt:4096 * sp + 1024 * tt + 1024]),
                inc=sSXp[sp], dma=True)
        st_out[sp] = so_mark
        w1g_p[sp] = [(sW1p[0], sW1p[0].n), (sW1p[1], sW1p[1].n),
                     (sW2p[0], sW2p[0].n), (sW2p[1], sW2p[1].n)]

    oscr_all = [(sSXp[0], sSXp[0].n), (sSXp[1], sSXp[1].n)]

    g_cmb = {0: 0, 1: 0}
    v_ycmb = {0: 0, 1: 0}
    ld_pp = {0: 0, 1: 0}
    y_marks = {0: 0, 1: 0}
    for tt in range(NTIL):
        cp = tt % 2
        W_("gpsimd", sV, v_ycmb[cp])
        ld_pp[cp] = I_("gpsimd", lambda E, tt=tt, cp=cp: E.dma_start(
            out=pbuf[:, 1024 * cp:1024 * cp + 1024],
            in_=d_pscr[128 * tt:128 * tt + 128, :]), inc=sSPp[1], dma=True)
        W_("vector", sLQ, max(g_cmb.values()))
        vstep(lambda E, tt=tt: E.tensor_copy(posi1[:, 0:1], poscols[:, 2 * tt:2 * tt + 1]))
        v_pc = vstep(lambda E, tt=tt: E.tensor_copy(posi2[:, 0:1], poscols[:, 2 * tt + 1:2 * tt + 2]))
        W_("gpsimd", sV, v_pc)
        for sem, val in oscr_all:
            W_("gpsimd", sem, val)
        W_("gpsimd", sV, v_ycmb[cp])
        I_("gpsimd", lambda E, cp=cp: E.indirect_dma_start(
            o1b[:, 1024 * cp:1024 * cp + 1024], None,
            d_oscr[0:NROWS, :], GIOA(posi1[:, 0:1], 0)), inc=sLQ, dma=True)
        g_cmb[cp] = I_("gpsimd", lambda E, cp=cp: E.indirect_dma_start(
            o2b[:, 1024 * cp:1024 * cp + 1024], None,
            d_oscr[0:NROWS, :], GIOA(posi2[:, 0:1], 0)), inc=sLQ, dma=True)
        W_("vector", sLQ, g_cmb[cp])
        W_("vector", sSPp[1], ld_pp[cp])
        W_("vector", sYp[cp], y_marks[cp])
        vstep(lambda E, cp=cp: E.tensor_tensor(
            ybuf[:, 1024 * cp:1024 * cp + 1024], pbuf[:, 1024 * cp:1024 * cp + 1024],
            o1b[:, 1024 * cp:1024 * cp + 1024], OP.add))
        v_ycmb[cp] = vstep(lambda E, cp=cp: E.tensor_tensor(
            ybuf[:, 1024 * cp:1024 * cp + 1024], ybuf[:, 1024 * cp:1024 * cp + 1024],
            o2b[:, 1024 * cp:1024 * cp + 1024], OP.add))
        W_("gpsimd", sV, v_ycmb[cp])
        y_marks[cp] = I_("gpsimd", lambda E, tt=tt, cp=cp: E.dma_start(
            out=d_y[128 * tt:128 * tt + 128, :],
            in_=ybuf[:, 1024 * cp:1024 * cp + 1024]), inc=sYp[cp], dma=True)


    W_("gpsimd", sYp[0], sYp[0].n)
    W_("gpsimd", sYp[1], sYp[1].n)

    block = es.enter_context(nc.Block())

    def run_stream(eng_name):
        def body(E):
            for fn in P.streams[eng_name]:
                fn(E)
        return body

    block.sync(run_stream("sync"))
    block.tensor(run_stream("tensor"))
    block.vector(run_stream("vector"))
    block.scalar(run_stream("scalar"))
    block.gpsimd(run_stream("gpsimd"))

    return nc, es


def _host_inputs(inputs):
    x = np.ascontiguousarray(np.asarray(inputs["L0_attn_x"], dtype=np.float32))
    W = np.asarray(inputs["L0_attn_W"], dtype=np.float32)
    gam = np.asarray(inputs["L0_attn_ln_gamma"], dtype=np.float32).reshape(1, D)
    bet = np.asarray(inputs["L0_attn_ln_beta"], dtype=np.float32).reshape(1, D)
    eta = np.asarray(inputs["L0_attn_eta"], dtype=np.float32).reshape(1, 1)
    lml = np.asarray(inputs["L0_attn_lam_logit"], dtype=np.float32).reshape(1, 1)
    rw = np.asarray(inputs["L0_ffn_router_weight"], dtype=np.float32)
    w1 = np.asarray(inputs["L0_ffn_w1"], dtype=np.float32)
    w2 = np.asarray(inputs["L0_ffn_w2"], dtype=np.float32)

    wt = np.ascontiguousarray(W.T)
    w1t = np.ascontiguousarray(
        w1.reshape(E, 8, 128, 22, 128).transpose(0, 3, 2, 1, 4)).reshape(E * 22 * 128, 1024)
    w2r = np.ascontiguousarray(w2.reshape(E * F, 1024))
    rwt = np.ascontiguousarray(rw.reshape(8, 128, 8).transpose(1, 0, 2)).reshape(128, 64)

    ident = np.eye(128, dtype=np.float32)
    iotac = np.arange(1, 129, dtype=np.float32).reshape(128, 1)
    cval = np.zeros((128, 4), dtype=np.float32)
    cval[:, 0] = LNEPS
    tri8u = np.triu(np.ones((8, 8), dtype=np.float32), 1)
    th16 = np.tile((np.arange(16, dtype=np.float32) * SROWS).reshape(1, 16), (128, 1))
    zo = np.zeros((128, 2), dtype=np.float32)
    zo[:, 1] = 1.0
    ones1 = np.ones((1, 128), dtype=np.float32)
    revi = np.tile((8.0 - np.arange(8, dtype=np.float32)).reshape(1, 8), (128, 1))
    tri_s = np.triu(np.ones((128, 128), dtype=np.float32), 1)
    tri_i = np.triu(np.ones((128, 128), dtype=np.float32), 0)
    msks = np.ascontiguousarray(np.tile(tri_s, (1, 4)))
    mski = np.ascontiguousarray(np.tile(tri_i, (1, 4)))

    shared = {"wt": wt, "gam": gam, "bet": bet, "eta": eta, "lml": lml,
              "rwt": rwt, "w1t": w1t, "w2": w2r, "ident": ident, "iotac": iotac,
              "zo": zo, "ones1": ones1, "revi": revi, "msks": msks, "mski": mski,
              "cval": cval, "tri8": tri8u, "th16": th16}
    return x, shared


def kernel(**inputs):
    x, shared = _host_inputs(inputs)
    nc, es = build_nc()
    in_maps = [dict(shared, x=np.ascontiguousarray(x[c])) for c in range(B)]
    try:
        res = run_bass_kernel_spmd(nc, in_maps, core_ids=list(range(B)))
    finally:
        es.close()
    out = np.stack([np.asarray(res.results[c]["y"]) for c in range(B)], axis=0)
    return out.astype(np.float32)


# version 07 (final)
# speedup vs baseline: 1.0746x; 1.0746x over previous
import contextlib
import numpy as np
import concourse.bass as bass
import concourse.mybir as mybir
from concourse.bass import IndirectOffsetOnAxis
from concourse.bass_utils import run_bass_kernel_spmd

F32 = mybir.dt.float32
I32 = mybir.dt.int32
AF = mybir.ActivationFunctionType
OP = mybir.AluOpType
AX = mybir.AxisListType.X

B, L, D, H, DH = 8, 4096, 1024, 16, 64
C = 128
NCH = L // C
NEU = 4
E, F2, F = 8, 2816, 1408
NFJ = F // 128
QT = 1024
NQ = L // QT
W1ROWS = 8 * 22 * 128
W2ROWS = 8 * 1408
NTT = QT // 128
LNEPS = 1e-5
SROWS = 512
NSLOT = 2 * L // SROWS + E
NTIL = L // 128
DEBUG_TAPS = False

ENGS = ("sync", "tensor", "vector", "scalar", "gpsimd")


class _Sem:
    def __init__(self, name):
        self.name = name
        self.n = 0
        self.h = None


class _Prog:
    def __init__(self):
        self.streams = {k: [] for k in ENGS}
        self.hw = {k: {} for k in ENGS}

    def raw(self, eng, fn):
        self.streams[eng].append(fn)

    def wait(self, eng, sem, val):
        if sem is None or val is None or val <= 0:
            return
        if self.hw[eng].get(sem.name, 0) >= val:
            return
        self.hw[eng][sem.name] = val
        self.raw(eng, lambda E, s=sem, v=val: E.wait_ge(s.h, v))

    auto_sem = {}

    def ins(self, eng, build, inc=None, dma=False):
        if inc is None and not dma:
            inc = self.auto_sem.get(eng)
        if inc is not None:
            k = 16 if dma else 1
            inc.n += k
            v = inc.n
            self.raw(eng, lambda E, b=build, s=inc, kk=k: b(E).then_inc(s.h, kk))
            return v
        self.raw(eng, lambda E, b=build: b(E))
        return None


def build_nc(nch=NCH, nq=NQ, nexp=E, taps=None, do_router=True, do_sweep1=True, do_sweep2=True):
    nc = bass.Bass("TRN2", target_bir_lowering=False)
    es = contextlib.ExitStack()

    d_x = nc.dram_tensor("x", [L, D], F32, kind="ExternalInput").ap()
    d_wt = nc.dram_tensor("wt", [D, D], F32, kind="ExternalInput").ap()
    d_gam = nc.dram_tensor("gam", [1, D], F32, kind="ExternalInput").ap()
    d_bet = nc.dram_tensor("bet", [1, D], F32, kind="ExternalInput").ap()
    d_eta = nc.dram_tensor("eta", [1, 1], F32, kind="ExternalInput").ap()
    d_lml = nc.dram_tensor("lml", [1, 1], F32, kind="ExternalInput").ap()
    d_rwt = nc.dram_tensor("rwt", [128, 64], F32, kind="ExternalInput").ap()
    d_w1t = nc.dram_tensor("w1t", [E * 22 * 128, 1024], F32, kind="ExternalInput").ap()
    d_w2 = nc.dram_tensor("w2", [E * F, 1024], F32, kind="ExternalInput").ap()
    d_ident = nc.dram_tensor("ident", [128, 128], F32, kind="ExternalInput").ap()
    d_iota = nc.dram_tensor("iotac", [128, 1], F32, kind="ExternalInput").ap()
    d_zo = nc.dram_tensor("zo", [128, 2], F32, kind="ExternalInput").ap()
    d_ones1 = nc.dram_tensor("ones1", [1, 128], F32, kind="ExternalInput").ap()
    d_revi = nc.dram_tensor("revi", [128, 8], F32, kind="ExternalInput").ap()
    d_msks = nc.dram_tensor("msks", [128, 512], F32, kind="ExternalInput").ap()
    d_mski = nc.dram_tensor("mski", [128, 512], F32, kind="ExternalInput").ap()
    d_cval = nc.dram_tensor("cval", [128, 4], F32, kind="ExternalInput").ap()
    d_tri8 = nc.dram_tensor("tri8", [8, 8], F32, kind="ExternalInput").ap()
    d_th16 = nc.dram_tensor("th16", [128, 16], F32, kind="ExternalInput").ap()
    _taps = DEBUG_TAPS if taps is None else taps
    scr_kind = "ExternalOutput" if _taps else "Internal"
    d_xnscr = nc.dram_tensor("xnscr", [L, D], F32, kind=scr_kind).ap()
    d_pscr = nc.dram_tensor("pscr", [L, D], F32, kind=scr_kind).ap()
    NROWS = NSLOT * SROWS
    d_idxmap = nc.dram_tensor("idxmap", [NROWS, 1], I32, kind="Internal").ap()
    d_pvals = nc.dram_tensor("pvals", [NROWS, 1], F32, kind="Internal").ap()
    d_oscr = nc.dram_tensor("oscr", [NROWS, D], F32, kind="Internal").ap()
    d_dbg = nc.dram_tensor("dbg", [6 * 128, D], F32,
                           kind="ExternalOutput" if _taps else "Internal").ap()
    d_y = nc.dram_tensor("y", [L, D], F32, kind="ExternalOutput").ap()

    _probe = nc.alloc_sbuf_tensor("layout_base_probe", [128, 8], F32)
    SBUF_BASE = int(nc.lookup_mloc(_probe).addr)
    alloc_ptr = [SBUF_BASE]

    def sal(name, shape, at=None):
        if at is not None:
            alloc_ptr[0] = at
        off = alloc_ptr[0]
        t = nc.alloc_sbuf_tensor_at(name, shape, F32, offset=off)
        alloc_ptr[0] = off + ((shape[-1] * 4 + 31) // 32) * 32
        return t.ap(), off

    ident, _ = sal("ident", [128, 128])
    masks4, _ = sal("masks4", [128, 512])
    maski4, _ = sal("maski4", [128, 512])
    bmasks4, _ = sal("bmasks4", [128, 512])
    bmaski4, _ = sal("bmaski4", [128, 512])
    iotac, _ = sal("iotac", [128, 1])
    zo, _ = sal("zo", [128, 2])
    ones1, _ = sal("ones1", [1, 128])
    revi8, _ = sal("revi8", [128, 8])
    screw, _ = sal("screw", [1, 16])
    scv, _ = sal("scv", [128, 16])
    lampp, _ = sal("lampp", [128, 1])
    lampi, _ = sal("lampi", [128, 1])
    etain, _ = sal("etain", [1, 1])
    lmlin, _ = sal("lmlin", [1, 1])
    gamin, _ = sal("gamin", [1, D])
    betin, _ = sal("betin", [1, D])
    gbg, _ = sal("gbg", [128, D])
    gbb, _ = sal("gbb", [128, D])
    rwt, _ = sal("rwt", [128, 64])
    cval, _ = sal("cval", [128, 4])
    tri8, _ = sal("tri8", [8, 8])
    th16, _ = sal("th16", [128, 16])
    nc.const_aps.register(F32, LNEPS, cval[:, 0:1])

    def ial(name, shape):
        off = alloc_ptr[0]
        t = nc.alloc_sbuf_tensor_at(name, shape, I32, offset=off)
        alloc_ptr[0] = off + ((shape[-1] * 4 + 31) // 32) * 32
        return t.ap()

    e_sb, _ = sal("e_sb", [128, 256])
    pr_sb, _ = sal("pr_sb", [128, 256])
    sc1, _ = sal("sc1", [128, 256])
    sc2, _ = sal("sc2", [128, 256])
    m1_sb, _ = sal("m1_sb", [128, 256])
    pcol, _ = sal("pcol", [128, 256])
    lp_sb, _ = sal("lp_sb", [128, 256])
    poscols, _ = sal("poscols", [128, 64])
    cb_sb, _ = sal("cb_sb", [128, 8])
    tmp8, _ = sal("tmp8", [128, 8])
    m2tmp, _ = sal("m2tmp", [128, 8])
    junk8, _ = sal("junk8", [128, 8])
    pv12, _ = sal("pv12", [128, 4])
    tokf, _ = sal("tokf", [128, 1])
    cntc, _ = sal("cntc", [128, 16])
    rowsc, _ = sal("rowsc", [1, 16])
    ebas, _ = sal("ebas", [128, 48])
    s24, _ = sal("s24", [128, 40])
    rows64, _ = sal("rows64", [1, 64])
    iota0, _ = sal("iota0", [128, 1])
    toki = ial("toki", [128, 2])
    posi1 = ial("posi1", [128, 2])
    posi2 = ial("posi2", [128, 2])
    tok32 = ial("tok32", [128, 8])
    idxw = ial("idxw", [128, 8])
    zeri = ial("zeri", [128, 48])
    pvs, _ = sal("pvs", [128, 8])
    COMMON_END = alloc_ptr[0]
    ARENA = (COMMON_END + 255) // 256 * 256

    lamc_col = scv[:, 3:4]
    blamc_col = scv[:, 4:5]
    beta_col = scv[:, 2:3]
    zcol = zo[:, 0:1]
    onecol = zo[:, 1:2]

    wt_sb, _ = sal("wt_sb", [128, 8192], at=ARENA)
    x_tok, _ = sal("x_tok", [128, 2048])
    xn_tok, _ = sal("xn_tok", [128, 2048])
    xa_fm, _ = sal("xa_fm", [128, 2048])
    kodd, _ = sal("kodd", [128, 2048])
    static_t, _ = sal("static_t", [128, 2048])
    vd_yn, _ = sal("vd_yn", [128, 1024])
    bb, _ = sal("bb", [128, 1024])
    u0, _ = sal("u0", [128, 1024])
    u1, _ = sal("u1", [128, 1024])
    gus, _ = sal("gus", [128, 2048])
    gui, _ = sal("gui", [128, 2048])
    ktok, _ = sal("ktok", [128, 1024])
    s_sb, _ = sal("s_sb", [128, 1024])
    part2, _ = sal("part2", [128, 2048])
    junk, _ = sal("junk", [128, 2048])
    xc, _ = sal("xc", [128, 2048])
    lncol, _ = sal("lncol", [128, 32])
    P1_END = alloc_ptr[0]

    xn_ld, _ = sal("xn_ld", [128, 2048], at=ARENA)
    fm_tmp, _ = sal("fm_tmp", [128, 1024])
    silb, _ = sal("silb", [128, 1024])
    xng, _ = sal("xng", [128, 2048])
    xfm, _ = sal("xfm", [128, 8192])
    g_sl, _ = sal("g_sl", [128, 11 * 512])
    w1ab, _ = sal("w1ab", [128, 2048])
    w1bb, _ = sal("w1bb", [128, 2048])
    w2b, _ = sal("w2b", [128, 2048])
    outb, _ = sal("outb", [128, 8192])
    pbuf, _ = sal("pbuf", [128, 2048])
    o1b, _ = sal("o1b", [128, 2048])
    o2b, _ = sal("o2b", [128, 2048])
    ybuf, _ = sal("ybuf", [128, 2048])
    P2_END = alloc_ptr[0]

    P2_END = alloc_ptr[0]
    xn_ld2 = xn_ld
    assert P1_END <= 229000 and P2_END <= 229000, (P1_END, P2_END)

    pb = [es.enter_context(nc.psum_tensor(f"pb{i}", [128, 512], F32)) for i in range(8)]

    sems = {}
    for name in ("sLC", "sLX0", "sLX1", "sW1P0", "sW1P1", "sW2", "sSX0", "sSX1", "sSP0", "sSP1", "sTAP", "sLQ", "sGX0", "sGX1", "sW2B", "sY0", "sY1", "sSC0", "sSC1", "sOUT", "sT", "sV", "sS"):
        s = _Sem(name)
        s.h = es.enter_context(nc.semaphore(name))
        sems[name] = s
    sLC, sW2, sTAP = sems["sLC"], sems["sW2"], sems["sTAP"]
    sLXp = [sems["sLX0"], sems["sLX1"]]
    sW1p = [sems["sW1P0"], sems["sW1P1"]]
    sSXp = [sems["sSX0"], sems["sSX1"]]
    sSPp = [sems["sSP0"], sems["sSP1"]]
    sLQ, sOUT = sems["sLQ"], sems["sOUT"]
    sGXp = [sems["sGX0"], sems["sGX1"]]
    sW2p = [sems["sW2"], sems["sW2B"]]
    sYp = [sems["sY0"], sems["sY1"]]
    sSCp = [sems["sSC0"], sems["sSC1"]]
    sT, sV, sS = sems["sT"], sems["sV"], sems["sS"]
    P = _Prog()
    P.auto_sem = {"tensor": sT, "vector": sV, "scalar": sS}
    W_ = P.wait
    I_ = P.ins

    for i in range(8):
        I_("sync", lambda E, i=i: E.dma_start(out=wt_sb[:, 1024 * i:1024 * i + 1024],
                                              in_=d_wt[128 * i:128 * i + 128, :]), inc=sLC, dma=True)
    for dst, src in ((ident, d_ident), (masks4, d_msks), (maski4, d_mski),
                     (iotac, d_iota), (zo, d_zo), (ones1, d_ones1), (revi8, d_revi),
                     (etain, d_eta), (lmlin, d_lml), (gamin, d_gam), (betin, d_bet),
                     (rwt, d_rwt), (cval, d_cval), (tri8, d_tri8), (th16, d_th16)):
        I_("sync", lambda E, d=dst, s=src: E.dma_start(out=d[:, :], in_=s[:, :]), inc=sLC, dma=True)
    ld_done = sLC.n

    v_init = I_("vector", lambda E: E.memset(s_sb[:, :], 0.0), inc=sV)
    v_scr0 = I_("vector", lambda E: E.memset(screw[0:1, :], 0.0), inc=sV)
    W_("vector", sLC, ld_done)

    def sync_step(eng, build, sem):
        v = I_(eng, build, inc=sem)
        W_(eng, sem, v)
        return v

    W_("scalar", sLC, ld_done)
    W_("scalar", sV, v_scr0)
    sync_step("scalar", lambda E: E.activation(screw[0:1, 0:1], lmlin[0:1, 0:1], AF.Sigmoid), sS)
    sync_step("scalar", lambda E: E.activation(screw[0:1, 1:2], screw[0:1, 0:1], AF.Ln), sS)
    s_ll = sync_step("scalar", lambda E: E.activation(screw[0:1, 3:4], screw[0:1, 1:2], AF.Exp,
                                                      bias=0.0, scale=128.0), sS)
    W_("vector", sS, s_ll)
    sync_step("vector", lambda E: E.tensor_scalar(screw[0:1, 5:6], screw[0:1, 0:1], -1.0, 1.0,
                                                  OP.mult, OP.add), sV)
    sync_step("vector", lambda E: E.tensor_tensor(screw[0:1, 6:7], screw[0:1, 5:6], etain[0:1, 0:1],
                                                  OP.mult), sV)
    sync_step("vector", lambda E: E.tensor_scalar_mul(screw[0:1, 2:3], screw[0:1, 6:7], 1.0 / DH), sV)
    v_beta = sync_step("vector", lambda E: E.tensor_tensor(screw[0:1, 4:5], screw[0:1, 2:3],
                                                           screw[0:1, 3:4], OP.mult), sV)
    W_("tensor", sV, v_beta)
    t_bc = I_("tensor", lambda E: E.matmul(pb[0][:, 0:16], ones1[0:1, :], screw[0:1, :],
                                           start=True, stop=True), inc=sT)
    W_("vector", sT, t_bc)
    v_scv = sync_step("vector", lambda E: E.tensor_copy(scv[:, :], pb[0][:, 0:16]), sV)
    sync_step("vector", lambda E: E.tensor_tensor(lncol[:, 0:1], iotac[:, 0:1], scv[:, 1:2],
                                                  OP.mult), sV)
    v_l2 = sync_step("vector", lambda E: E.tensor_scalar_mul(lncol[:, 1:2], lncol[:, 0:1], -1.0), sV)
    W_("scalar", sV, v_l2)
    I_("scalar", lambda E: E.activation(lampp[:, 0:1], lncol[:, 0:1], AF.Exp,
                                        bias=zcol, scale=onecol), inc=sS)
    s_lamp = I_("scalar", lambda E: E.activation(lampi[:, 0:1], lncol[:, 1:2], AF.Exp,
                                                 bias=zcol, scale=onecol), inc=sS)
    I_("vector", lambda E: E.tensor_scalar_mul(bmasks4[:, :], masks4[:, :], scv[:, 2:3]), inc=sV)
    v_bm = I_("vector", lambda E: E.tensor_scalar_mul(bmaski4[:, :], maski4[:, :], scv[:, 2:3]), inc=sV)
    W_("tensor", sLC, ld_done)
    v_gb = 0
    for i in range(2):
        W_("tensor", sV, v_gb)
        I_("tensor", lambda E, i=i: E.matmul(pb[1][:, 0:512], ones1[0:1, :],
                                             gamin[0:1, 512 * i:512 * i + 512],
                                             start=True, stop=True, skip_group_check=True), inc=sT)
        t_g1 = I_("tensor", lambda E, i=i: E.matmul(pb[2][:, 0:512], ones1[0:1, :],
                                                    betin[0:1, 512 * i:512 * i + 512],
                                                    start=True, stop=True, skip_group_check=True), inc=sT)
        W_("vector", sT, t_g1)
        I_("vector", lambda E, i=i: E.tensor_copy(gbg[:, 512 * i:512 * i + 512], pb[1][:, 0:512]), inc=sV)
        v_gb = I_("vector", lambda E, i=i: E.tensor_copy(gbb[:, 512 * i:512 * i + 512], pb[2][:, 0:512]), inc=sV)
    prolog_v = sV.n
    prolog_s = sS.n
    prolog_t = sT.n

    vPar = {}
    swCnt = {}
    M = {}

    def PAB(buf, cc):
        pa = cc % 2
        return buf[:, 1024 * pa:1024 * pa + 1024]

    def K_ap(h, cc):
        j = h // 2
        if h % 2 == 0:
            return PAB(xa_fm, cc)[0:64, 128 * j:128 * j + 128]
        return PAB(kodd, cc)[0:64, 128 * j:128 * j + 128]

    def LNC(cc, k):
        return lncol[:, 16 * (cc % 2) + k:16 * (cc % 2) + k + 1]

    def emit_load(cc):
        if cc >= nch:
            return
        W_("sync", sV, vPar.get(cc - 2, 0))
        M.setdefault(cc, {})["lx"] = I_("sync", lambda E, cc=cc: E.dma_start(
            out=PAB(x_tok, cc), in_=d_x[C * cc:C * cc + C, :]), inc=sLXp[cc % 2], dma=True)

    def emit_B1(cc):
        m = M.setdefault(cc, {})
        pa = cc % 2
        xtk, xntk = PAB(x_tok, cc), PAB(xn_tok, cc)
        xcb, jkb = PAB(xc, cc), PAB(junk, cc)
        p2 = M.get(cc - 2, {})
        W_("vector", sLXp[pa], m["lx"])
        if cc == 0:
            W_("vector", sS, s_lamp)
        W_("vector", sS, p2.get("s8", 0))
        v1 = I_("vector", lambda E, x=xtk: E.reduce_sum(LNC(cc, 2), x, axis=AX), inc=sV)
        W_("scalar", sV, v1)
        s1 = I_("scalar", lambda E: E.mul(LNC(cc, 3), LNC(cc, 2), -1.0 / D), inc=sS)
        W_("vector", sS, s1)
        v2 = I_("vector", lambda E, x=xtk, xc_=xcb: E.tensor_scalar_add(xc_, x, LNC(cc, 3)), inc=sV)
        W_("vector", sV, v2)
        v3 = I_("vector", lambda E, xc_=xcb, jk=jkb: E.scalar_tensor_tensor(
            jk, xc_, 0.0, xc_, OP.add, OP.mult, accum_out=LNC(cc, 4)), inc=sV)
        W_("scalar", sV, v3)
        s2a = I_("scalar", lambda E: E.mul(LNC(cc, 5), LNC(cc, 4), 1.0 / D), inc=sS)
        W_("scalar", sS, s2a)
        s2 = I_("scalar", lambda E: E.add(LNC(cc, 5), LNC(cc, 5), LNEPS), inc=sS)
        W_("vector", sS, s2)
        v4 = I_("vector", lambda E: E.reciprocal(LNC(cc, 6), LNC(cc, 5)), inc=sV)
        W_("scalar", sV, v4)
        s_sq1 = I_("scalar", lambda E: E.activation(LNC(cc, 7), LNC(cc, 6), AF.Sqrt), inc=sS)
        W_("scalar", sS, s_sq1)
        W_("scalar", sSXp[pa], swCnt.get(("x", cc - 2), 0))
        W_("scalar", sT, p2.get("t_tp_last", 0))
        m["s_xn"] = I_("scalar", lambda E, xn=xntk, xc_=xcb: E.activation(
            xn, xc_, AF.Identity, bias=zcol, scale=LNC(cc, 7)), inc=sS)
        W_("gpsimd", sS, m["s_xn"])
        swCnt[("x", cc)] = I_("gpsimd", lambda E, cc=cc, xn=xntk: E.dma_start(
            out=d_xnscr[C * cc:C * cc + C, :], in_=xn), inc=sSXp[pa], dma=True)

    def emit_B1t(cc):
        m = M[cc]
        pa = cc % 2
        xntk = PAB(xn_tok, cc)
        p2 = M.get(cc - 2, {})
        p1m = M.get(cc - 1, {})
        W_("tensor", sS, m["s_xn"])
        if cc == 0:
            W_("tensor", sV, v_bm)
        W_("tensor", sV, p1m.get("v_n_last", 0))
        W_("tensor", sV, p1m.get("v_k1", 0))
        t_tp = []
        for bk in range(2):
            for jj in range(4):
                j = 4 * bk + jj
                t = I_("tensor", lambda E, bk=bk, jj=jj, j=j, xn=xntk: E.matmul(
                    pb[6 + bk][:, 128 * jj:128 * jj + 128], xn[:, 128 * j:128 * j + 128],
                    ident[:, :], is_transpose=True, start=(jj == 0), stop=(jj == 3),
                    skip_group_check=True), inc=sT)
            t_tp.append(t)
        for bk in range(2):
            for jj in range(4):
                j = 4 * bk + jj
                t = I_("tensor", lambda E, bk=bk, jj=jj, j=j, xn=xntk: E.matmul(
                    pb[bk][0:64, 128 * jj:128 * jj + 128],
                    xn[:, 128 * j + 64:128 * j + 128],
                    ident[:, :], is_transpose=True, start=(jj == 0), stop=(jj == 3),
                    skip_group_check=True), inc=sT)
            t_tp.append(t)
        m["t_tp_last"] = sT.n
        s_tanh = []
        W_("scalar", sT, p2.get("t_sd_last", 0))
        for bk in range(2):
            W_("scalar", sT, t_tp[bk])
            s_tanh.append(I_("scalar", lambda E, bk=bk, cc=cc: E.activation(
                PAB(xa_fm, cc)[:, 512 * bk:512 * bk + 512], pb[6 + bk][:, 0:512],
                AF.Tanh), inc=sS))
        for bk in range(2):
            W_("scalar", sT, t_tp[2 + bk])
            s_tanh.append(I_("scalar", lambda E, bk=bk, cc=cc: E.activation(
                PAB(kodd, cc)[0:64, 512 * bk:512 * bk + 512], pb[bk][0:64, 0:512],
                AF.Tanh), inc=sS))
        m["s_tanh"] = s_tanh

    def emit_B2(cc):
        m = M[cc]
        p1m = M.get(cc - 1, {})
        p2 = M.get(cc - 2, {})
        W_("tensor", sS, m["s_tanh"][1])
        W_("tensor", sV, p1m.get("v_o1", 0))
        t_st = []
        for eb in range(2):
            for dc in range(8):
                t = I_("tensor", lambda E, eb=eb, dc=dc, cc=cc: E.matmul(
                    pb[4 + eb][:, 0:512],
                    PAB(xa_fm, cc)[:, 128 * dc:128 * dc + 128],
                    wt_sb[:, 1024 * dc + 512 * eb:1024 * dc + 512 * eb + 512],
                    start=(dc == 0), stop=(dc == 7), skip_group_check=True), inc=sT)
            t_st.append(t)
        v_st = []
        W_("vector", sS, p2.get("s_vd", 0))
        for eb in range(2):
            W_("vector", sT, t_st[eb])
            v_st.append(I_("vector", lambda E, eb=eb, cc=cc: E.tensor_copy(
                PAB(static_t, cc)[:, 512 * eb:512 * eb + 512], pb[4 + eb][:, 0:512]),
                inc=sV))
        m["v_st"] = v_st
        if _taps and cc == 0:
            W_("gpsimd", sV, v_st[1])
            M[cc]["tap_static"] = I_("gpsimd", lambda E, cc=cc: E.dma_start(
                out=d_dbg[0:128, :], in_=PAB(static_t, cc)), inc=sTAP, dma=True)

    def emit_C1(cc):
        m = M[cc]
        p1m = M.get(cc - 1, {})
        s_tanh = m["s_tanh"]
        W_("tensor", sS, s_tanh[3])
        W_("tensor", sV, p1m.get("v_su", 0))
        t_g = []
        for bk in range(4):
            for qq in range(4):
                h = 4 * bk + qq
                t = I_("tensor", lambda E, bk=bk, qq=qq, h=h, cc=cc: E.matmul(
                    pb[bk][:, 128 * qq:128 * qq + 128], K_ap(h, cc), K_ap(h, cc),
                    start=(qq == 0), stop=(qq == 3), skip_group_check=True), inc=sT)
            t_g.append(t)
        W_("vector", sT, p1m.get("t_sd_last", 0))
        v_g = []
        for bk in range(4):
            W_("vector", sT, t_g[bk])
            I_("vector", lambda E, bk=bk: E.tensor_tensor(
                gus[:, 512 * bk:512 * bk + 512], pb[bk][:, 0:512], bmasks4[:, :],
                OP.mult), inc=sV)
            v_g.append(I_("vector", lambda E, bk=bk: E.tensor_tensor(
                gui[:, 512 * bk:512 * bk + 512], pb[bk][:, 0:512], bmaski4[:, :],
                OP.mult), inc=sV))
        m["v_g"] = v_g
        t_tk = []
        for bk in range(2):
            W_("tensor", sV, v_g[bk])
            for qq in range(8):
                h = 8 * bk + qq
                t = I_("tensor", lambda E, bk=bk, qq=qq, h=h, cc=cc: E.matmul(
                    pb[bk][:, 64 * qq:64 * qq + 64], K_ap(h, cc), ident[0:64, 0:64],
                    is_transpose=True, start=(qq == 0), stop=(qq == 7),
                    skip_group_check=True), inc=sT)
            t_tk.append(t)
        v_k = []
        for bk in range(2):
            W_("vector", sT, t_tk[bk])
            v_k.append(I_("vector", lambda E, bk=bk: E.tensor_copy(
                ktok[:, 512 * bk:512 * bk + 512], pb[bk][:, 0:512]), inc=sV))
        m["v_k1"] = v_k[1]
        m["v_k"] = v_k
        W_("scalar", sV, p1m.get("vg1", 0))
        W_("scalar", sV, m["v_st"][1])
        m["s_vd"] = I_("scalar", lambda E, cc=cc: E.activation(
            vd_yn[:, :], PAB(static_t, cc), AF.Identity, bias=zcol,
            scale=lampi[:, 0:1]), inc=sS)
        if cc == 0:
            W_("tensor", sV, v_init)
        t_ks = []
        for bk in range(2):
            W_("tensor", sV, m["v_st"][bk])
            for qq in range(8):
                h = 8 * bk + qq
                t = I_("tensor", lambda E, bk=bk, qq=qq, h=h, cc=cc: E.matmul(
                    pb[4 + bk][:, 64 * qq:64 * qq + 64], K_ap(h, cc),
                    s_sb[0:64, 64 * h:64 * h + 64],
                    start=(qq == 0), stop=(qq == 7), skip_group_check=True), inc=sT)
            t_ks.append(t)
        v_b = []
        for bk in range(2):
            W_("vector", sT, t_ks[bk]); W_("vector", sS, m["s_vd"])
            v_b.append(I_("vector", lambda E, bk=bk: E.tensor_tensor(
                bb[:, 512 * bk:512 * bk + 512], vd_yn[:, 512 * bk:512 * bk + 512],
                pb[4 + bk][:, 0:512], OP.subtract), inc=sV))
        m["v_b"] = v_b
        if _taps and cc == 0:
            W_("gpsimd", sV, v_b[1])
            I_("gpsimd", lambda E: E.dma_start(out=d_dbg[128:256, :], in_=bb[:, :]),
               inc=sTAP, dma=True)
            M[cc]["tap_vd"] = I_("gpsimd", lambda E: E.dma_start(
                out=d_dbg[256:384, :], in_=vd_yn[:, :]), inc=sTAP, dma=True)
        ubufs = [u0, u1]
        v_n = [0, 0]
        if _taps and cc == 1:
            W_("vector", sTAP, sTAP.n)
        W_("vector", sV, v_b[1])
        for it in range(1, NEU + 1):
            src = bb if it == 1 else ubufs[1 - it % 2]
            dst = ubufs[it % 2]
            t_n = []
            for bk in range(2):
                W_("tensor", sV, v_g[3])
                W_("tensor", sV, v_b[bk] if it == 1 else v_n[bk])
                if it == 1:
                    W_("tensor", sS, s_tanh[bk])
                for qq in range(8):
                    h = 8 * bk + qq
                    t = I_("tensor", lambda E, bk=bk, qq=qq, h=h, src=src: E.matmul(
                        pb[6 + bk][:, 64 * qq:64 * qq + 64],
                        gus[:, 128 * h:128 * h + 128], src[:, 64 * h:64 * h + 64],
                        start=(qq == 0), stop=(qq == 7), skip_group_check=True), inc=sT)
                t_n.append(t)
            for bk in range(2):
                W_("vector", sT, t_n[bk])
                v_n[bk] = I_("vector", lambda E, bk=bk, dst=dst: E.tensor_tensor(
                    dst[:, 512 * bk:512 * bk + 512], bb[:, 512 * bk:512 * bk + 512],
                    pb[6 + bk][:, 0:512], OP.subtract), inc=sV)
        m["v_n"] = list(v_n)
        m["v_n_last"] = max(v_n)
        m["ufin"] = ubufs[NEU % 2]
        if _taps and cc == 0:
            W_("gpsimd", sV, m["v_n_last"])
            M[cc]["tap_u"] = I_("gpsimd", lambda E, u=m["ufin"]: E.dma_start(
                out=d_dbg[384:512, :], in_=u[:, :]), inc=sTAP, dma=True)

    def emit_C2a(cc):
        m = M[cc]
        ufin = m["ufin"]
        t_o = []
        for bk in range(2):
            W_("tensor", sV, m["v_n"][bk])
            W_("tensor", sV, m["v_b"][bk])
            for qq in range(8):
                h = 8 * bk + qq
                t = I_("tensor", lambda E, bk=bk, qq=qq, h=h, u=ufin: E.matmul(
                    pb[4 + bk][:, 64 * qq:64 * qq + 64],
                    gui[:, 128 * h:128 * h + 128], u[:, 64 * h:64 * h + 64],
                    start=False, stop=(qq == 7), skip_group_check=True), inc=sT)
            t_o.append(t)
        m["t_o"] = t_o
        v_o = []
        if _taps and cc == 0 and "tap_static" in m:
            W_("vector", sTAP, sTAP.n)
        for bk in range(2):
            W_("vector", sT, t_o[bk])
            v_o.append(I_("vector", lambda E, bk=bk, cc=cc: E.scalar_tensor_tensor(
                PAB(static_t, cc)[:, 512 * bk:512 * bk + 512], pb[4 + bk][:, 0:512],
                lampp[:, 0:1], PAB(static_t, cc)[:, 512 * bk:512 * bk + 512],
                OP.mult, OP.add), inc=sV))
        m["v_o"] = v_o
        m["v_o1"] = v_o[1]

    def emit_C2b(cc):
        m = M[cc]
        ufin = m["ufin"]
        t_sd = []
        for bk in range(2):
            W_("tensor", sV, m["v_k"][bk]); W_("tensor", sV, m["v_g"][2 + bk])
            for qq in range(8):
                h = 8 * bk + qq
                t = I_("tensor", lambda E, bk=bk, qq=qq, h=h, u=ufin: E.matmul(
                    pb[2 + bk][0:64, 64 * qq:64 * qq + 64],
                    ktok[:, 64 * h:64 * h + 64], u[:, 64 * h:64 * h + 64],
                    start=(qq == 0), stop=(qq == 7), skip_group_check=True), inc=sT)
            t_sd.append(t)
        m["t_sd_last"] = sT.n
        W_("vector", sT, t_sd[0]); W_("vector", sT, t_sd[1])
        v_sm2 = 0
        for bk in range(2):
            v_sm2 = I_("vector", lambda E, bk=bk: E.tensor_scalar_mul(
                s_sb[0:64, 512 * bk:512 * bk + 512],
                s_sb[0:64, 512 * bk:512 * bk + 512], lamc_col[0:64, :]), inc=sV)
        W_("vector", sV, v_sm2)
        v_su = 0
        for bk in range(2):
            v_su = I_("vector", lambda E, bk=bk: E.scalar_tensor_tensor(
                s_sb[0:64, 512 * bk:512 * bk + 512], pb[2 + bk][0:64, 0:512],
                blamc_col[0:64, :], s_sb[0:64, 512 * bk:512 * bk + 512],
                OP.mult, OP.add), inc=sV)
        m["v_su"] = v_su

    def emit_D(cc):
        m = M[cc]
        pa = cc % 2
        xtk, ptk = PAB(x_tok, cc), PAB(part2, cc)
        xcb, jkb = PAB(xc, cc), PAB(junk, cc)
        stt_ = PAB(static_t, cc)
        if _taps and cc == 0:
            W_("gpsimd", sV, m["v_o1"])
            I_("gpsimd", lambda E, s=stt_: E.dma_start(out=d_dbg[512:640, :], in_=s),
               inc=sTAP, dma=True)
            W_("gpsimd", sS, s_lamp)
            I_("gpsimd", lambda E: E.dma_start(out=d_dbg[640:641, 0:1], in_=lampp[0:1, 0:1]),
               inc=sTAP, dma=True)
            I_("gpsimd", lambda E: E.dma_start(out=d_dbg[641:642, 0:16], in_=scv[0:1, :]),
               inc=sTAP, dma=True)
            I_("gpsimd", lambda E: E.dma_start(out=d_dbg[642:643, 0:1], in_=lampi[0:1, 0:1]),
               inc=sTAP, dma=True)
        W_("vector", sV, m["v_o1"])
        v5 = I_("vector", lambda E, s=stt_: E.reduce_sum(LNC(cc, 8), s, axis=AX), inc=sV)
        W_("scalar", sV, v5)
        s5 = I_("scalar", lambda E: E.mul(LNC(cc, 9), LNC(cc, 8), -1.0 / D), inc=sS)
        W_("vector", sS, s5)
        v6 = I_("vector", lambda E, s=stt_, xc_=xcb: E.tensor_scalar_add(
            xc_, s, LNC(cc, 9)), inc=sV)
        W_("vector", sV, v6)
        v7 = I_("vector", lambda E, xc_=xcb, jk=jkb: E.scalar_tensor_tensor(
            jk, xc_, 0.0, xc_, OP.add, OP.mult, accum_out=LNC(cc, 10)), inc=sV)
        W_("scalar", sV, v7)
        s6a = I_("scalar", lambda E: E.mul(LNC(cc, 11), LNC(cc, 10), 1.0 / D), inc=sS)
        W_("scalar", sS, s6a)
        s6 = I_("scalar", lambda E: E.add(LNC(cc, 11), LNC(cc, 11), LNEPS), inc=sS)
        W_("vector", sS, s6)
        v8 = I_("vector", lambda E: E.reciprocal(LNC(cc, 12), LNC(cc, 11)), inc=sV)
        W_("scalar", sV, v8)
        s_sq2 = I_("scalar", lambda E: E.activation(LNC(cc, 13), LNC(cc, 12), AF.Sqrt), inc=sS)
        W_("scalar", sS, s_sq2)
        if _taps and cc == 0 and "tap_vd" in m:
            W_("scalar", sTAP, sTAP.n)
        W_("scalar", sV, m["v_b"][1])
        s8 = I_("scalar", lambda E, xc_=xcb: E.activation(
            vd_yn[:, :], xc_, AF.Identity, bias=zcol, scale=LNC(cc, 13)), inc=sS)
        m["s8"] = s8
        W_("vector", sS, s8)
        W_("vector", sSPp[pa], swCnt.get(("p", cc - 2), 0))
        vg1 = I_("vector", lambda E, jk=jkb: E.tensor_mul(jk, vd_yn[:, :], gbg[:, :]), inc=sV)
        m["vg1"] = vg1
        W_("vector", sV, vg1)
        vg2 = I_("vector", lambda E, jk=jkb: E.tensor_add(jk, jk, gbb[:, :]), inc=sV)
        W_("vector", sV, vg2)
        vP = I_("vector", lambda E, x=xtk, p=ptk, jk=jkb: E.tensor_add(p, jk, x), inc=sV)
        vPar[cc] = vP
        W_("gpsimd", sV, vP)
        swCnt[("p", cc)] = I_("gpsimd", lambda E, cc=cc, p=ptk: E.dma_start(
            out=d_pscr[C * cc:C * cc + C, :], in_=p), inc=sSPp[pa], dma=True)

    emit_load(0)
    emit_load(1)
    if nch > 0:
        emit_B1(0)
        emit_B1t(0)
        emit_B2(0)
    for cc in range(nch):
        emit_C1(cc)
        if cc + 1 < nch:
            emit_B1(cc + 1)
        emit_C2a(cc)
        emit_C2b(cc)
        if cc + 1 < nch:
            emit_B1t(cc + 1)
            emit_B2(cc + 1)
        emit_D(cc)
        emit_load(cc + 2)


    fins = ((sT, sT.n), (sV, sV.n), (sS, sS.n),
            (sSXp[0], sSXp[0].n), (sSXp[1], sSXp[1].n),
            (sSPp[0], sSPp[0].n), (sSPp[1], sSPp[1].n),
            (sLXp[0], sLXp[0].n), (sLXp[1], sLXp[1].n),
            (sLC, sLC.n), (sTAP, sTAP.n))
    for eng in ENGS:
        for sem, val in fins:
            W_(eng, sem, val)

    GIOA = IndirectOffsetOnAxis

    v_i0 = I_("vector", lambda E: E.tensor_scalar_add(iota0[:, 0:1], iotac[:, 0:1], -1.0), inc=sV)

    z_ms = I_("vector", lambda E: E.memset(zeri[:, :], 0), inc=sV)
    W_("gpsimd", sV, z_ms)
    for i in range(NROWS // 4096):
        I_("gpsimd", lambda E, i=i: E.dma_start(
            out=d_idxmap[4096 * i:4096 * i + 4096, 0:1].rearrange("(p f) o -> p (f o)", p=128),
            in_=zeri[:, 0:32]), inc=sLQ, dma=True)
        I_("gpsimd", lambda E, i=i: E.dma_start(
            out=d_pvals[4096 * i:4096 * i + 4096, 0:1].rearrange("(p f) o -> p (f o)", p=128),
            in_=zeri[:, 0:32]), inc=sLQ, dma=True)
    zinit_mark = sLQ.n

    xnld_p = {0: 0, 1: 0}
    t_tr = {}
    t_pl = 0
    for t8 in range(NTIL):
        W_("sync", sT, xnld_p[t8 % 2])
        lxv = I_("sync", lambda E, t8=t8: E.dma_start(
            out=xn_ld[:, 1024 * (t8 % 2):1024 * (t8 % 2) + 1024],
            in_=d_xnscr[128 * t8:128 * t8 + 128, :]), inc=sLXp[t8 % 2], dma=True)
        W_("tensor", sLXp[t8 % 2], lxv)
        for bk in range(2):
            if t8 > 0:
                W_("tensor", sV, t_tr.get("evac%d_%d" % (bk, t8 - 1), 0))
            for jj in range(4):
                dc = 4 * bk + jj
                t = I_("tensor", lambda E, t8=t8, bk=bk, jj=jj, dc=dc: E.matmul(
                    pb[bk][:, 128 * jj:128 * jj + 128],
                    xn_ld[:, 1024 * (t8 % 2) + 128 * dc:1024 * (t8 % 2) + 128 * dc + 128],
                    ident[:, :], is_transpose=True, start=(jj == 0), stop=(jj == 3),
                    skip_group_check=True), inc=sT)
            t_tr["tp%d_%d" % (bk, t8)] = t
        xnld_p[t8 % 2] = sT.n
        ev_last = 0
        for bk in range(2):
            W_("vector", sT, t_tr["tp%d_%d" % (bk, t8)])
            for jj in range(4):
                dc = 4 * bk + jj
                ev_last = I_("vector", lambda E, bk=bk, jj=jj, dc=dc: E.tensor_copy(
                    fm_tmp[:, 128 * dc:128 * dc + 128],
                    pb[bk][:, 128 * jj:128 * jj + 128]), inc=sV)
            t_tr["evac%d_%d" % (bk, t8)] = ev_last
        W_("tensor", sV, ev_last)
        for dc in range(8):
            t_pl = I_("tensor", lambda E, t8=t8, dc=dc: E.matmul(
                pb[2][:, 8 * t8:8 * t8 + 8],
                fm_tmp[:, 128 * dc:128 * dc + 128], rwt[:, 8 * dc:8 * dc + 8],
                start=(t8 == 0 and dc == 0), stop=(dc == 7),
                skip_group_check=True), inc=sT)

    def vgroup(build_fn):
        last = 0
        for tt in range(NTIL):
            last = I_("vector", build_fn(tt), inc=sV if tt == NTIL - 1 else None)
        W_("vector", sV, last)
        return last

    def sgroup(build_fn):
        last = 0
        for tt in range(NTIL):
            last = I_("scalar", build_fn(tt), inc=sS if tt == NTIL - 1 else None)
        W_("scalar", sS, last)
        return last

    W_("vector", sT, t_pl)
    v_m1 = vgroup(lambda tt: lambda E, tt=tt: E.reduce_max(
        pcol[:, 8 * tt:8 * tt + 1], pb[2][:, 8 * tt:8 * tt + 8], axis=AX))
    W_("scalar", sV, v_m1)
    sgroup(lambda tt: lambda E, tt=tt: E.mul(
        pcol[:, 8 * tt + 1:8 * tt + 2], pcol[:, 8 * tt:8 * tt + 1], -1.0))
    s_exp = sgroup(lambda tt: lambda E, tt=tt: E.activation(
        e_sb[:, 8 * tt:8 * tt + 8], pb[2][:, 8 * tt:8 * tt + 8], AF.Exp,
        bias=pcol[:, 8 * tt + 1:8 * tt + 2], scale=onecol))
    W_("vector", sS, s_exp)
    vgroup(lambda tt: lambda E, tt=tt: E.reduce_sum(
        pcol[:, 8 * tt + 2:8 * tt + 3], e_sb[:, 8 * tt:8 * tt + 8], axis=AX))
    v_ri = vgroup(lambda tt: lambda E, tt=tt: E.reciprocal(
        pcol[:, 8 * tt + 3:8 * tt + 4], pcol[:, 8 * tt + 2:8 * tt + 3]))
    v_sm = vgroup(lambda tt: lambda E, tt=tt: E.reduce_max(
        pcol[:, 8 * tt + 7:8 * tt + 8], e_sb[:, 8 * tt:8 * tt + 8], axis=AX))
    W_("scalar", sV, v_ri)
    s_pr = sgroup(lambda tt: lambda E, tt=tt: E.activation(
        pr_sb[:, 8 * tt:8 * tt + 8], e_sb[:, 8 * tt:8 * tt + 8], AF.Identity,
        bias=zcol, scale=pcol[:, 8 * tt + 3:8 * tt + 4]))
    W_("vector", sV, v_sm)
    vgroup(lambda tt: lambda E, tt=tt: E.tensor_single_scalar(
        sc1[:, 8 * tt:8 * tt + 8], e_sb[:, 8 * tt:8 * tt + 8],
        pcol[:, 8 * tt + 7:8 * tt + 8], OP.is_equal))
    vgroup(lambda tt: lambda E, tt=tt: E.tensor_tensor(
        sc1[:, 8 * tt:8 * tt + 8], sc1[:, 8 * tt:8 * tt + 8], revi8[:, :], OP.mult))
    vgroup(lambda tt: lambda E, tt=tt: E.reduce_max(
        pcol[:, 8 * tt + 4:8 * tt + 5], sc1[:, 8 * tt:8 * tt + 8], axis=AX))
    vgroup(lambda tt: lambda E, tt=tt: E.tensor_single_scalar(
        sc1[:, 8 * tt:8 * tt + 8], sc1[:, 8 * tt:8 * tt + 8],
        pcol[:, 8 * tt + 4:8 * tt + 5], OP.is_equal))
    vgroup(lambda tt: lambda E, tt=tt: E.tensor_copy(
        m1_sb[:, 8 * tt:8 * tt + 8], sc1[:, 8 * tt:8 * tt + 8]))
    W_("vector", sS, s_pr)
    vgroup(lambda tt: lambda E, tt=tt: E.scalar_tensor_tensor(
        e_sb[:, 8 * tt:8 * tt + 8], sc1[:, 8 * tt:8 * tt + 8], -2.0,
        e_sb[:, 8 * tt:8 * tt + 8], OP.mult, OP.add))
    vgroup(lambda tt: lambda E, tt=tt: E.reduce_max(
        pcol[:, 8 * tt + 5:8 * tt + 6], e_sb[:, 8 * tt:8 * tt + 8], axis=AX))
    vgroup(lambda tt: lambda E, tt=tt: E.tensor_single_scalar(
        sc2[:, 8 * tt:8 * tt + 8], e_sb[:, 8 * tt:8 * tt + 8],
        pcol[:, 8 * tt + 5:8 * tt + 6], OP.is_equal))
    vgroup(lambda tt: lambda E, tt=tt: E.tensor_tensor(
        sc2[:, 8 * tt:8 * tt + 8], sc2[:, 8 * tt:8 * tt + 8], revi8[:, :], OP.mult))
    vgroup(lambda tt: lambda E, tt=tt: E.reduce_max(
        pcol[:, 8 * tt + 6:8 * tt + 7], sc2[:, 8 * tt:8 * tt + 8], axis=AX))
    vgroup(lambda tt: lambda E, tt=tt: E.tensor_single_scalar(
        sc2[:, 8 * tt:8 * tt + 8], sc2[:, 8 * tt:8 * tt + 8],
        pcol[:, 8 * tt + 6:8 * tt + 7], OP.is_equal))
    vgroup(lambda tt: lambda E, tt=tt: E.tensor_add(
        sc1[:, 8 * tt:8 * tt + 8], sc1[:, 8 * tt:8 * tt + 8], sc2[:, 8 * tt:8 * tt + 8]))
    v_lp = vgroup(lambda tt: lambda E, tt=tt: E.tensor_mul(
        lp_sb[:, 8 * tt:8 * tt + 8], sc1[:, 8 * tt:8 * tt + 8],
        pr_sb[:, 8 * tt:8 * tt + 8]))

    t_cnt = 0
    W_("tensor", sV, v_lp)
    for tt in range(NTIL):
        t_cnt = I_("tensor", lambda E, tt=tt: E.matmul(
            pb[3][0:8, 0:1], sc1[:, 8 * tt:8 * tt + 8], onecol[:, 0:1],
            start=(tt == 0), stop=(tt == NTIL - 1), skip_group_check=True), inc=sT)
    W_("vector", sT, t_cnt)

    def vstep(build, eng="vector"):
        sem = {"vector": sV, "scalar": sS}[eng]
        v = I_(eng, build, inc=sem)
        W_(eng, sem, v)
        return v

    vstep(lambda E: E.tensor_copy(cntc[0:8, 0:1], pb[3][0:8, 0:1]))
    vstep(lambda E: E.tensor_single_scalar(s24[0:8, 24:24 + 16], th16[0:8, :],
                                           cntc[0:8, 0:1], OP.is_lt))
    vstep(lambda E: E.reduce_sum(cntc[0:8, 3:4], s24[0:8, 24:24 + 16], axis=AX))
    v_pad = vstep(lambda E: E.tensor_scalar_mul(cntc[0:8, 4:5], cntc[0:8, 3:4], float(SROWS)))
    W_("tensor", sV, v_pad)
    t_po = I_("tensor", lambda E: E.matmul(pb[3][0:8, 8:9], tri8[:, :], cntc[0:8, 4:5],
                                           start=False, stop=True, skip_group_check=True), inc=sT)
    W_("vector", sT, t_po)
    v_poc = vstep(lambda E: E.tensor_copy(cntc[0:8, 5:6], pb[3][0:8, 8:9]))
    W_("tensor", sV, v_poc)
    t_por = I_("tensor", lambda E: E.matmul(pb[3][0:1, 16:24], cntc[0:8, 5:6], ident[0:8, 0:8],
                                            is_transpose=True, start=False, stop=True,
                                            skip_group_check=True), inc=sT)
    W_("vector", sT, t_por)
    v_porow = vstep(lambda E: E.tensor_copy(rows64[0:1, 0:8], pb[3][0:1, 16:24]))
    W_("tensor", sV, v_porow)
    t_pob = I_("tensor", lambda E: E.matmul(pb[1][:, 0:8], ones1[0:1, :], rows64[0:1, 0:8],
                                            start=True, stop=True, skip_group_check=True), inc=sT)
    W_("vector", sT, t_pob)
    vstep(lambda E: E.tensor_copy(s24[:, 0:8], pb[1][:, 0:8]))
    vstep(lambda E: E.tensor_scalar_mul(cntc[:, 6:7], iota0[:, 0:1], float(SROWS)))
    vstep(lambda E: E.tensor_single_scalar(s24[:, 8:16], s24[:, 0:8], cntc[:, 6:7], OP.is_le))
    vstep(lambda E: E.reduce_sum(cntc[:, 8:9], s24[:, 8:16], axis=AX))
    vstep(lambda E: E.tensor_scalar_add(cntc[:, 9:10], cntc[:, 8:9], -1.0))
    vstep(lambda E: E.tensor_scalar_mul(cntc[:, 10:11], cntc[:, 9:10], float(22 * 128)))
    v_eb = vstep(lambda E: E.tensor_scalar_mul(cntc[:, 11:12], cntc[:, 9:10], float(F)))
    W_("tensor", sV, v_eb)
    t_e1r = I_("tensor", lambda E: E.matmul(pb[3][0:1, 32:32 + NSLOT], cntc[0:NSLOT, 10:11],
                                            ident[0:NSLOT, 0:NSLOT], is_transpose=True,
                                            start=False, stop=True, skip_group_check=True), inc=sT)
    t_e2r = I_("tensor", lambda E: E.matmul(pb[3][0:1, 64:64 + NSLOT], cntc[0:NSLOT, 11:12],
                                            ident[0:NSLOT, 0:NSLOT], is_transpose=True,
                                            start=False, stop=True, skip_group_check=True), inc=sT)
    W_("vector", sT, t_e2r)
    vstep(lambda E: E.tensor_copy(rows64[0:1, 8:8 + NSLOT], pb[3][0:1, 32:32 + NSLOT]))
    v_ebr = vstep(lambda E: E.tensor_copy(rows64[0:1, 32:32 + NSLOT], pb[3][0:1, 64:64 + NSLOT]))
    W_("tensor", sV, v_ebr)
    t_eb1 = I_("tensor", lambda E: E.matmul(pb[1][:, 8:8 + NSLOT], ones1[0:1, :],
                                            rows64[0:1, 8:8 + NSLOT], start=False, stop=True,
                                            skip_group_check=True), inc=sT)
    t_eb2 = I_("tensor", lambda E: E.matmul(pb[1][:, 64:64 + NSLOT], ones1[0:1, :],
                                            rows64[0:1, 32:32 + NSLOT], start=False, stop=True,
                                            skip_group_check=True), inc=sT)
    W_("vector", sT, t_eb2)
    vstep(lambda E: E.tensor_copy(ebas[:, 0:NSLOT], pb[1][:, 8:8 + NSLOT]))
    v_ebas = vstep(lambda E: E.tensor_copy(ebas[:, 24:24 + NSLOT], pb[1][:, 64:64 + NSLOT]))

    prev_vcu = vstep(lambda E: E.tensor_copy(rows64[0:1, 56:64], rows64[0:1, 0:8]))
    sc_p = {0: 0, 1: 0}
    for tt in range(NTIL):
        W_("tensor", sV, prev_vcu)
        t_rk = I_("tensor", lambda E, tt=tt: E.matmul(
            pb[0][:, 0:8], masks4[:, 0:128], sc1[:, 8 * tt:8 * tt + 8],
            start=True, stop=True, skip_group_check=True), inc=sT)
        t_cb = I_("tensor", lambda E: E.matmul(
            pb[0][:, 8:16], ones1[0:1, :], rows64[0:1, 56:64],
            start=False, stop=True, skip_group_check=True), inc=sT)
        t_tsum = I_("tensor", lambda E, tt=tt: E.matmul(
            pb[0][0:1, 16:24], onecol[:, 0:1], sc1[:, 8 * tt:8 * tt + 8],
            start=False, stop=True, skip_group_check=True), inc=sT)
        W_("vector", sT, t_tsum)
        W_("vector", sT, t_cb)
        v_t8 = vstep(lambda E: E.tensor_copy(cb_sb[:, 0:8], pb[0][:, 8:16]))
        v_t9 = vstep(lambda E: E.tensor_tensor(tmp8[:, 0:8], pb[0][:, 0:8], cb_sb[:, 0:8], OP.add))
        vstep(lambda E, tt=tt: E.tensor_tensor(
            m2tmp[:, 0:8], sc1[:, 8 * tt:8 * tt + 8], m1_sb[:, 8 * tt:8 * tt + 8], OP.subtract))
        vstep(lambda E, tt=tt: E.scalar_tensor_tensor(
            junk8[:, 0:8], tmp8[:, 0:8], 0.0, m1_sb[:, 8 * tt:8 * tt + 8],
            OP.add, OP.mult, accum_out=poscols[:, 2 * tt:2 * tt + 1]))
        vstep(lambda E, tt=tt: E.scalar_tensor_tensor(
            junk8[:, 0:8], tmp8[:, 0:8], 0.0, m2tmp[:, 0:8],
            OP.add, OP.mult, accum_out=poscols[:, 2 * tt + 1:2 * tt + 2]))
        tp2 = tt % 2
        W_("vector", sSCp[tp2], sc_p[tp2])
        vstep(lambda E, tt=tt, tp2=tp2: E.scalar_tensor_tensor(
            junk8[:, 0:8], lp_sb[:, 8 * tt:8 * tt + 8], 0.0, m1_sb[:, 8 * tt:8 * tt + 8],
            OP.add, OP.mult, accum_out=pv12[:, 2 * tp2:2 * tp2 + 1]))
        vstep(lambda E, tt=tt, tp2=tp2: E.scalar_tensor_tensor(
            junk8[:, 0:8], lp_sb[:, 8 * tt:8 * tt + 8], 0.0, m2tmp[:, 0:8],
            OP.add, OP.mult, accum_out=pv12[:, 2 * tp2 + 1:2 * tp2 + 2]))
        vstep(lambda E, tt=tt: E.tensor_scalar_add(tokf[:, 0:1], iota0[:, 0:1], float(128 * tt)))
        vstep(lambda E, tp2=tp2: E.tensor_copy(toki[:, tp2:tp2 + 1], tokf[:, 0:1]))
        vstep(lambda E, tt=tt, tp2=tp2: E.tensor_copy(posi1[:, tp2:tp2 + 1],
                                                      poscols[:, 2 * tt:2 * tt + 1]))
        v_cast = vstep(lambda E, tt=tt, tp2=tp2: E.tensor_copy(
            posi2[:, tp2:tp2 + 1], poscols[:, 2 * tt + 1:2 * tt + 2]))
        v_cu = vstep(lambda E: E.tensor_tensor(
            rows64[0:1, 56:64], rows64[0:1, 56:64], pb[0][0:1, 16:24], OP.add))
        prev_vcu = v_cu
        W_("gpsimd", sLQ, zinit_mark)
        W_("gpsimd", sV, v_cast)
        I_("gpsimd", lambda E, tp2=tp2: E.indirect_dma_start(
            d_idxmap[0:NROWS, 0:1], GIOA(posi1[:, tp2:tp2 + 1], 0),
            toki[:, tp2:tp2 + 1], None), inc=sSCp[tp2], dma=True)
        I_("gpsimd", lambda E, tp2=tp2: E.indirect_dma_start(
            d_idxmap[0:NROWS, 0:1], GIOA(posi2[:, tp2:tp2 + 1], 0),
            toki[:, tp2:tp2 + 1], None), inc=sSCp[tp2], dma=True)
        I_("gpsimd", lambda E, tp2=tp2: E.indirect_dma_start(
            d_pvals[0:NROWS, 0:1], GIOA(posi1[:, tp2:tp2 + 1], 0),
            pv12[:, 2 * tp2:2 * tp2 + 1], None), inc=sSCp[tp2], dma=True)
        sc_p[tp2] = I_("gpsimd", lambda E, tp2=tp2: E.indirect_dma_start(
            d_pvals[0:NROWS, 0:1], GIOA(posi2[:, tp2:tp2 + 1], 0),
            pv12[:, 2 * tp2 + 1:2 * tp2 + 2], None), inc=sSCp[tp2], dma=True)

    w1g_p = {0: [], 1: []}
    w2g_p = {0: 0, 1: 0}
    t_w1free = {0: 0, 1: 0}
    t_w2free = {0: 0, 1: 0}
    v_idxw_free = {0: 0, 1: 0}
    g_xng = {0: 0, 1: 0}
    t_xng_free = {0: 0, 1: 0}
    v_yfree = [0, 0]
    v_oev = [0, 0, 0, 0]
    s_silfree = [0, 0]
    st_out = {0: 0, 1: 0}
    W_("tensor", sV, sV.n)
    W_("tensor", sS, sS.n)

    prep_ev = {0: (0, 0), 1: (0, 0)}
    w1mk_all = {}

    def emit_prep(s):
        sp = s % 2
        W_("vector", sV, v_ebas)
        for sem, val in w1g_p[sp]:
            W_("vector", sem, val)
        vstep(lambda E, s=s: E.tensor_tensor(tokf[:, 0:1], ebas[:, s:s + 1], iota0[:, 0:1], OP.add))
        vstep(lambda E, s=s, sp=sp: E.tensor_copy(idxw[:, 4 * sp:4 * sp + 1], tokf[:, 0:1]))
        vstep(lambda E, s=s: E.tensor_tensor(tokf[:, 0:1], ebas[:, 24 + s:24 + s + 1],
                                             iota0[:, 0:1], OP.add))
        v_idxw = vstep(lambda E, s=s, sp=sp: E.tensor_copy(idxw[:, 4 * sp + 1:4 * sp + 2],
                                                           tokf[:, 0:1]))
        M_idxw[sp] = v_idxw
        W_("sync", sSCp[0], sc_p[0]); W_("sync", sSCp[1], sc_p[1])
        W_("sync", sGXp[0], g_xng[0]); W_("sync", sGXp[1], g_xng[1])
        W_("sync", sV, max(v_oev))
        ld_tok = 0
        for tt in range(4):
            I_("sync", lambda E, s=s, tt=tt, sp=sp: E.dma_start(
                out=tok32[:, 4 * sp + tt:4 * sp + tt + 1],
                in_=d_idxmap[SROWS * s + 128 * tt:SROWS * s + 128 * tt + 128, 0:1]),
                inc=sLXp[0], dma=True)
            ld_tok = I_("sync", lambda E, s=s, tt=tt, sp=sp: E.dma_start(
                out=pvs[:, 4 * sp + tt:4 * sp + tt + 1],
                in_=d_pvals[SROWS * s + 128 * tt:SROWS * s + 128 * tt + 128, 0:1]),
                inc=sLXp[0], dma=True)
        ev = 0
        for tt in range(4):
            gp = tt % 2
            W_("gpsimd", sLXp[0], ld_tok)
            W_("gpsimd", sT, t_xng_free[gp])
            g_xng[gp] = I_("gpsimd", lambda E, tt=tt, gp=gp, sp=sp: E.indirect_dma_start(
                xng[:, 1024 * gp:1024 * gp + 1024], None,
                d_xnscr[0:L, :], GIOA(tok32[:, 4 * sp + tt:4 * sp + tt + 1], 0)),
                inc=sGXp[gp], dma=True)
            W_("tensor", sGXp[gp], g_xng[gp])
            W_("tensor", sV, v_oev[0]); W_("tensor", sV, v_oev[1])
            for bk in range(2):
                W_("tensor", sV, t_tr.get("sev%d" % bk, 0))
                for jj in range(4):
                    dc = 4 * bk + jj
                    t = I_("tensor", lambda E, tt=tt, gp=gp, bk=bk, jj=jj, dc=dc: E.matmul(
                        pb[4 + bk][:, 128 * jj:128 * jj + 128],
                        xng[:, 1024 * gp + 128 * dc:1024 * gp + 128 * dc + 128],
                        ident[:, :], is_transpose=True, start=(jj == 0), stop=(jj == 3),
                        skip_group_check=True), inc=sT)
                t_tr["stp%d" % bk] = t
            t_xng_free[gp] = sT.n
            for bk in range(2):
                W_("vector", sT, t_tr["stp%d" % bk])
                for jj in range(4):
                    dc = 4 * bk + jj
                    ev = I_("vector", lambda E, tt=tt, bk=bk, jj=jj, dc=dc, sp=sp: E.tensor_copy(
                        xfm[:, 4096 * sp + 512 * dc + 128 * tt:4096 * sp + 512 * dc + 128 * tt + 128],
                        pb[4 + bk][:, 128 * jj:128 * jj + 128]), inc=sV)
                t_tr["sev%d" % bk] = ev
        prep_ev[sp] = (ev, ld_tok)
        w1mk_all[s] = {}
        for fj in (0, 1):
            if fj < NFJ:
                w1mk_all[s][fj] = emit_w1g(fj, s, sp, v_idxw)

    def emit_w1g(fj, s, sp, v_idxw):
        fp = fj % 2
        W_("gpsimd", sV, v_idxw)
        W_("gpsimd", sT, t_w1free[fp])
        mk = 0
        for ab, buf in ((0, w1ab), (1, w1bb)):
            mk = I_("gpsimd", lambda E, s=s, fj=fj, fp=fp, ab=ab, buf=buf, sp=sp: E.indirect_dma_start(
                buf[:, 1024 * fp:1024 * fp + 1024], None,
                d_w1t[0:W1ROWS, :], GIOA(idxw[:, 4 * sp:4 * sp + 1], 0),
                element_offset=128 * (fj + 11 * ab) * 1024), inc=sW1p[fp], dma=True)
        return mk

    M_idxw = {0: 0, 1: 0}
    emit_prep(0)
    for s in range(NSLOT):
        sp = s % 2
        v_idxw = M_idxw[sp]
        w1mk = w1mk_all[s]
        v_xfm, ld_tok = prep_ev[sp]
        W_("tensor", sV, v_xfm)
        for fj in range(NFJ):
            fp = fj % 2
            rot = fj % 2
            bA, bB = 2 * rot, 2 * rot + 1
            W_("tensor", sW1p[fp], w1mk[fj])
            W_("tensor", sV, v_yfree[rot])
            W_("tensor", sS, s_silfree[rot])
            for dc in range(8):
                I_("tensor", lambda E, fj=fj, fp=fp, dc=dc, bA=bA, sp=sp: E.matmul(
                    pb[bA][:, 0:512],
                    w1ab[:, 1024 * fp + 128 * dc:1024 * fp + 128 * dc + 128],
                    xfm[:, 4096 * sp + 512 * dc:4096 * sp + 512 * dc + 512],
                    start=(dc == 0), stop=(dc == 7), skip_group_check=True), inc=sT)
            t_y1 = sT.n
            for dc in range(8):
                I_("tensor", lambda E, fj=fj, fp=fp, dc=dc, bB=bB, sp=sp: E.matmul(
                    pb[bB][:, 0:512],
                    w1bb[:, 1024 * fp + 128 * dc:1024 * fp + 128 * dc + 128],
                    xfm[:, 4096 * sp + 512 * dc:4096 * sp + 512 * dc + 512],
                    start=(dc == 0), stop=(dc == 7), skip_group_check=True), inc=sT)
            t_y2 = sT.n
            t_w1free[fp] = sT.n
            if fj + 2 < NFJ:
                w1mk[fj + 2] = emit_w1g(fj + 2, s, sp, v_idxw)
            W_("scalar", sT, t_y1)
            W_("scalar", sV, v_yfree[rot])
            s_sil = I_("scalar", lambda E, rot=rot, bA=bA: E.activation(
                silb[:, 512 * rot:512 * rot + 512], pb[bA][:, 0:512], AF.Silu), inc=sS)
            s_silfree[rot] = s_sil
            W_("vector", sS, s_sil); W_("vector", sT, t_y2)
            v_yfree[rot] = I_("vector", lambda E, fj=fj, rot=rot, bB=bB: E.tensor_tensor(
                g_sl[:, 512 * fj:512 * fj + 512],
                silb[:, 512 * rot:512 * rot + 512], pb[bB][:, 0:512], OP.mult), inc=sV)
        v_gdone = sV.n
        if s + 1 < NSLOT:
            emit_prep(s + 1)
        W_("tensor", sV, v_gdone)
        W_("tensor", sV, prep_ev[(s + 1) % 2][0] if s + 1 < NSLOT else 0)
        o_done = 0
        for hh in range(2):
            w2mk = {}

            def emit_w2g(fj, s=s, sp=sp, v_idxw=v_idxw):
                fp = fj % 2
                W_("gpsimd", sV, v_idxw)
                W_("gpsimd", sT, t_w2free[fp])
                return I_("gpsimd", lambda E, s=s, fj=fj, fp=fp, sp=sp: E.indirect_dma_start(
                    w2b[:, 1024 * fp:1024 * fp + 1024], None,
                    d_w2[0:W2ROWS, :], GIOA(idxw[:, 4 * sp + 1:4 * sp + 2], 0),
                    element_offset=128 * fj * 1024), inc=sW2p[fp], dma=True)
            w2mk[0] = emit_w2g(0)
            if NFJ > 1:
                w2mk[1] = emit_w2g(1)
            for fj in range(NFJ):
                fp = fj % 2
                W_("tensor", sW2p[fp], w2mk[fj])
                for tt in range(4):
                    bk = 4 + tt
                    if fj == 0:
                        W_("tensor", sV, v_oev[tt])
                    I_("tensor", lambda E, hh=hh, fj=fj, fp=fp, tt=tt, bk=bk: E.matmul(
                        pb[bk][:, 0:512],
                        g_sl[:, 512 * fj + 128 * tt:512 * fj + 128 * tt + 128],
                        w2b[:, 1024 * fp + 512 * hh:1024 * fp + 512 * hh + 512],
                        start=(fj == 0), stop=(fj == NFJ - 1), skip_group_check=True), inc=sT)
                t_w2free[fp] = sT.n
                if fj + 2 < NFJ:
                    w2mk[fj + 2] = emit_w2g(fj + 2)
            o_done = sT.n
            for tt in range(4):
                bk = 4 + tt
                W_("vector", sT, o_done)
                W_("vector", sLXp[0], ld_tok)
                W_("vector", sSXp[sp], st_out[sp])
                v_oev[tt] = I_("vector", lambda E, hh=hh, tt=tt, bk=bk, sp=sp: E.tensor_scalar_mul(
                    outb[:, 4096 * sp + 1024 * tt + 512 * hh:4096 * sp + 1024 * tt + 512 * hh + 512],
                    pb[bk][:, 0:512], pvs[:, 4 * sp + tt:4 * sp + tt + 1]), inc=sV)
        so_mark = 0
        for tt in range(4):
            W_("gpsimd", sV, v_oev[tt])
            so_mark = I_("gpsimd", lambda E, s=s, tt=tt, sp=sp: E.dma_start(
                out=d_oscr[SROWS * s + 128 * tt:SROWS * s + 128 * tt + 128, :],
                in_=outb[:, 4096 * sp + 1024 * tt:4096 * sp + 1024 * tt + 1024]),
                inc=sSXp[sp], dma=True)
        st_out[sp] = so_mark
        w1g_p[sp] = [(sW1p[0], sW1p[0].n), (sW1p[1], sW1p[1].n),
                     (sW2p[0], sW2p[0].n), (sW2p[1], sW2p[1].n)]

    oscr_all = [(sSXp[0], sSXp[0].n), (sSXp[1], sSXp[1].n)]

    g_cmb = {0: 0, 1: 0}
    v_ycmb = {0: 0, 1: 0}
    ld_pp = {0: 0, 1: 0}
    y_marks = {0: 0, 1: 0}
    for tt in range(NTIL):
        cp = tt % 2
        W_("gpsimd", sV, v_ycmb[cp])
        ld_pp[cp] = I_("gpsimd", lambda E, tt=tt, cp=cp: E.dma_start(
            out=pbuf[:, 1024 * cp:1024 * cp + 1024],
            in_=d_pscr[128 * tt:128 * tt + 128, :]), inc=sSPp[1], dma=True)
        W_("vector", sLQ, max(g_cmb.values()))
        W_("vector", sSCp[0], sc_p[0]); W_("vector", sSCp[1], sc_p[1])
        vstep(lambda E, tt=tt: E.tensor_copy(posi1[:, 0:1], poscols[:, 2 * tt:2 * tt + 1]))
        v_pc = vstep(lambda E, tt=tt: E.tensor_copy(posi2[:, 0:1], poscols[:, 2 * tt + 1:2 * tt + 2]))
        W_("gpsimd", sV, v_pc)
        for sem, val in oscr_all:
            W_("gpsimd", sem, val)
        W_("gpsimd", sV, v_ycmb[cp])
        I_("gpsimd", lambda E, cp=cp: E.indirect_dma_start(
            o1b[:, 1024 * cp:1024 * cp + 1024], None,
            d_oscr[0:NROWS, :], GIOA(posi1[:, 0:1], 0)), inc=sLQ, dma=True)
        g_cmb[cp] = I_("gpsimd", lambda E, cp=cp: E.indirect_dma_start(
            o2b[:, 1024 * cp:1024 * cp + 1024], None,
            d_oscr[0:NROWS, :], GIOA(posi2[:, 0:1], 0)), inc=sLQ, dma=True)
        W_("vector", sLQ, g_cmb[cp])
        W_("vector", sSPp[1], ld_pp[cp])
        W_("vector", sYp[cp], y_marks[cp])
        vstep(lambda E, cp=cp: E.tensor_tensor(
            ybuf[:, 1024 * cp:1024 * cp + 1024], pbuf[:, 1024 * cp:1024 * cp + 1024],
            o1b[:, 1024 * cp:1024 * cp + 1024], OP.add))
        v_ycmb[cp] = vstep(lambda E, cp=cp: E.tensor_tensor(
            ybuf[:, 1024 * cp:1024 * cp + 1024], ybuf[:, 1024 * cp:1024 * cp + 1024],
            o2b[:, 1024 * cp:1024 * cp + 1024], OP.add))
        W_("gpsimd", sV, v_ycmb[cp])
        y_marks[cp] = I_("gpsimd", lambda E, tt=tt, cp=cp: E.dma_start(
            out=d_y[128 * tt:128 * tt + 128, :],
            in_=ybuf[:, 1024 * cp:1024 * cp + 1024]), inc=sYp[cp], dma=True)


    W_("gpsimd", sYp[0], sYp[0].n)
    W_("gpsimd", sYp[1], sYp[1].n)

    block = es.enter_context(nc.Block())

    def run_stream(eng_name):
        def body(E):
            for fn in P.streams[eng_name]:
                fn(E)
        return body

    block.sync(run_stream("sync"))
    block.tensor(run_stream("tensor"))
    block.vector(run_stream("vector"))
    block.scalar(run_stream("scalar"))
    block.gpsimd(run_stream("gpsimd"))

    return nc, es


def _host_inputs(inputs):
    x = np.ascontiguousarray(np.asarray(inputs["L0_attn_x"], dtype=np.float32))
    W = np.asarray(inputs["L0_attn_W"], dtype=np.float32)
    gam = np.asarray(inputs["L0_attn_ln_gamma"], dtype=np.float32).reshape(1, D)
    bet = np.asarray(inputs["L0_attn_ln_beta"], dtype=np.float32).reshape(1, D)
    eta = np.asarray(inputs["L0_attn_eta"], dtype=np.float32).reshape(1, 1)
    lml = np.asarray(inputs["L0_attn_lam_logit"], dtype=np.float32).reshape(1, 1)
    rw = np.asarray(inputs["L0_ffn_router_weight"], dtype=np.float32)
    w1 = np.asarray(inputs["L0_ffn_w1"], dtype=np.float32)
    w2 = np.asarray(inputs["L0_ffn_w2"], dtype=np.float32)

    wt = np.ascontiguousarray(W.T)
    w1t = np.ascontiguousarray(
        w1.reshape(E, 8, 128, 22, 128).transpose(0, 3, 2, 1, 4)).reshape(E * 22 * 128, 1024)
    w2r = np.ascontiguousarray(w2.reshape(E * F, 1024))
    rwt = np.ascontiguousarray(rw.reshape(8, 128, 8).transpose(1, 0, 2)).reshape(128, 64)

    ident = np.eye(128, dtype=np.float32)
    iotac = np.arange(1, 129, dtype=np.float32).reshape(128, 1)
    cval = np.zeros((128, 4), dtype=np.float32)
    cval[:, 0] = LNEPS
    tri8u = np.triu(np.ones((8, 8), dtype=np.float32), 1)
    th16 = np.tile((np.arange(16, dtype=np.float32) * SROWS).reshape(1, 16), (128, 1))
    zo = np.zeros((128, 2), dtype=np.float32)
    zo[:, 1] = 1.0
    ones1 = np.ones((1, 128), dtype=np.float32)
    revi = np.tile((8.0 - np.arange(8, dtype=np.float32)).reshape(1, 8), (128, 1))
    tri_s = np.triu(np.ones((128, 128), dtype=np.float32), 1)
    tri_i = np.triu(np.ones((128, 128), dtype=np.float32), 0)
    msks = np.ascontiguousarray(np.tile(tri_s, (1, 4)))
    mski = np.ascontiguousarray(np.tile(tri_i, (1, 4)))

    shared = {"wt": wt, "gam": gam, "bet": bet, "eta": eta, "lml": lml,
              "rwt": rwt, "w1t": w1t, "w2": w2r, "ident": ident, "iotac": iotac,
              "zo": zo, "ones1": ones1, "revi": revi, "msks": msks, "mski": mski,
              "cval": cval, "tri8": tri8u, "th16": th16}
    return x, shared


def kernel(**inputs):
    x, shared = _host_inputs(inputs)
    nc, es = build_nc()
    in_maps = [dict(shared, x=np.ascontiguousarray(x[c])) for c in range(B)]
    try:
        res = run_bass_kernel_spmd(nc, in_maps, core_ids=list(range(B)))
    finally:
        es.close()
    out = np.stack([np.asarray(res.results[c]["y"]) for c in range(B)], axis=0)
    return out.astype(np.float32)


# version 08
# speedup vs baseline: 1.0912x; 1.0154x over previous
import contextlib
import numpy as np
import concourse.bass as bass
import concourse.mybir as mybir
from concourse.bass import IndirectOffsetOnAxis
from concourse.bass_utils import run_bass_kernel_spmd

F32 = mybir.dt.float32
I32 = mybir.dt.int32
AF = mybir.ActivationFunctionType
OP = mybir.AluOpType
AX = mybir.AxisListType.X

B, L, D, H, DH = 8, 4096, 1024, 16, 64
C = 128
NCH = L // C
NEU = 4
E, F2, F = 8, 2816, 1408
NFJ = F // 128
QT = 1024
NQ = L // QT
W1ROWS = 8 * 22 * 128
W2ROWS = 8 * 1408
NTT = QT // 128
LNEPS = 1e-5
SROWS = 384
NSLOT = 2 * L // SROWS + E
TPS = SROWS // 128
XFW = 1024 * TPS
EB2 = 32
NTIL = L // 128
DEBUG_TAPS = False

ENGS = ("sync", "tensor", "vector", "scalar", "gpsimd")


class _Sem:
    def __init__(self, name):
        self.name = name
        self.n = 0
        self.h = None


class _Prog:
    def __init__(self):
        self.streams = {k: [] for k in ENGS}
        self.hw = {k: {} for k in ENGS}

    def raw(self, eng, fn):
        self.streams[eng].append(fn)

    def wait(self, eng, sem, val):
        if sem is None or val is None or val <= 0:
            return
        if self.hw[eng].get(sem.name, 0) >= val:
            return
        self.hw[eng][sem.name] = val
        self.raw(eng, lambda E, s=sem, v=val: E.wait_ge(s.h, v))

    auto_sem = {}

    def ins(self, eng, build, inc=None, dma=False):
        if inc is None and not dma:
            inc = self.auto_sem.get(eng)
        if inc is not None:
            k = 16 if dma else 1
            inc.n += k
            v = inc.n
            self.raw(eng, lambda E, b=build, s=inc, kk=k: b(E).then_inc(s.h, kk))
            return v
        self.raw(eng, lambda E, b=build: b(E))
        return None


def build_nc(nch=NCH, nq=NQ, nexp=E, taps=None, do_router=True, do_sweep1=True, do_sweep2=True):
    nc = bass.Bass("TRN2", target_bir_lowering=False)
    es = contextlib.ExitStack()

    d_x = nc.dram_tensor("x", [L, D], F32, kind="ExternalInput").ap()
    d_wt = nc.dram_tensor("wt", [D, D], F32, kind="ExternalInput").ap()
    d_gam = nc.dram_tensor("gam", [1, D], F32, kind="ExternalInput").ap()
    d_bet = nc.dram_tensor("bet", [1, D], F32, kind="ExternalInput").ap()
    d_eta = nc.dram_tensor("eta", [1, 1], F32, kind="ExternalInput").ap()
    d_lml = nc.dram_tensor("lml", [1, 1], F32, kind="ExternalInput").ap()
    d_rwt = nc.dram_tensor("rwt", [128, 64], F32, kind="ExternalInput").ap()
    d_w1t = nc.dram_tensor("w1t", [E * 22 * 128, 1024], F32, kind="ExternalInput").ap()
    d_w2 = nc.dram_tensor("w2", [E * F, 1024], F32, kind="ExternalInput").ap()
    d_ident = nc.dram_tensor("ident", [128, 128], F32, kind="ExternalInput").ap()
    d_iota = nc.dram_tensor("iotac", [128, 1], F32, kind="ExternalInput").ap()
    d_zo = nc.dram_tensor("zo", [128, 2], F32, kind="ExternalInput").ap()
    d_ones1 = nc.dram_tensor("ones1", [1, 128], F32, kind="ExternalInput").ap()
    d_revi = nc.dram_tensor("revi", [128, 8], F32, kind="ExternalInput").ap()
    d_msks = nc.dram_tensor("msks", [128, 512], F32, kind="ExternalInput").ap()
    d_mski = nc.dram_tensor("mski", [128, 512], F32, kind="ExternalInput").ap()
    d_cval = nc.dram_tensor("cval", [128, 4], F32, kind="ExternalInput").ap()
    d_tri8 = nc.dram_tensor("tri8", [8, 8], F32, kind="ExternalInput").ap()
    d_th16 = nc.dram_tensor("th16", [128, 16], F32, kind="ExternalInput").ap()
    _taps = DEBUG_TAPS if taps is None else taps
    scr_kind = "ExternalOutput" if _taps else "Internal"
    d_xnscr = nc.dram_tensor("xnscr", [L, D], F32, kind=scr_kind).ap()
    d_pscr = nc.dram_tensor("pscr", [L, D], F32, kind=scr_kind).ap()
    NROWS = NSLOT * SROWS
    d_idxmap = nc.dram_tensor("idxmap", [NROWS, 1], I32, kind="Internal").ap()
    d_pvals = nc.dram_tensor("pvals", [NROWS, 1], F32, kind="Internal").ap()
    d_oscr = nc.dram_tensor("oscr", [NROWS, D], F32, kind="Internal").ap()
    d_dbg = nc.dram_tensor("dbg", [6 * 128, D], F32,
                           kind="ExternalOutput" if _taps else "Internal").ap()
    d_y = nc.dram_tensor("y", [L, D], F32, kind="ExternalOutput").ap()

    _probe = nc.alloc_sbuf_tensor("layout_base_probe", [128, 8], F32)
    SBUF_BASE = int(nc.lookup_mloc(_probe).addr)
    alloc_ptr = [SBUF_BASE]

    def sal(name, shape, at=None):
        if at is not None:
            alloc_ptr[0] = at
        off = alloc_ptr[0]
        t = nc.alloc_sbuf_tensor_at(name, shape, F32, offset=off)
        alloc_ptr[0] = off + ((shape[-1] * 4 + 31) // 32) * 32
        return t.ap(), off

    ident, _ = sal("ident", [128, 128])
    masks4, _ = sal("masks4", [128, 512])
    maski4, _ = sal("maski4", [128, 512])
    bmasks4, _ = sal("bmasks4", [128, 512])
    bmaski4, _ = sal("bmaski4", [128, 512])
    iotac, _ = sal("iotac", [128, 1])
    zo, _ = sal("zo", [128, 2])
    ones1, _ = sal("ones1", [1, 128])
    revi8, _ = sal("revi8", [128, 8])
    screw, _ = sal("screw", [1, 16])
    scv, _ = sal("scv", [128, 16])
    lampp, _ = sal("lampp", [128, 1])
    lampi, _ = sal("lampi", [128, 1])
    etain, _ = sal("etain", [1, 1])
    lmlin, _ = sal("lmlin", [1, 1])
    gamin, _ = sal("gamin", [1, D])
    betin, _ = sal("betin", [1, D])
    gbg, _ = sal("gbg", [128, D])
    gbb, _ = sal("gbb", [128, D])
    rwt, _ = sal("rwt", [128, 64])
    cval, _ = sal("cval", [128, 4])
    tri8, _ = sal("tri8", [8, 8])
    th16, _ = sal("th16", [128, 16])
    nc.const_aps.register(F32, LNEPS, cval[:, 0:1])

    def ial(name, shape):
        off = alloc_ptr[0]
        t = nc.alloc_sbuf_tensor_at(name, shape, I32, offset=off)
        alloc_ptr[0] = off + ((shape[-1] * 4 + 31) // 32) * 32
        return t.ap()

    e_sb, _ = sal("e_sb", [128, 256])
    pr_sb, _ = sal("pr_sb", [128, 256])
    sc1, _ = sal("sc1", [128, 256])
    sc2, _ = sal("sc2", [128, 256])
    m1_sb, _ = sal("m1_sb", [128, 256])
    pcol, _ = sal("pcol", [128, 256])
    lp_sb, _ = sal("lp_sb", [128, 256])
    poscols, _ = sal("poscols", [128, 64])
    cb_sb, _ = sal("cb_sb", [128, 8])
    tmp8, _ = sal("tmp8", [128, 8])
    m2tmp, _ = sal("m2tmp", [128, 8])
    junk8, _ = sal("junk8", [128, 8])
    pv12, _ = sal("pv12", [128, 4])
    tokf, _ = sal("tokf", [128, 1])
    cntc, _ = sal("cntc", [128, 16])
    rowsc, _ = sal("rowsc", [1, 16])
    ebas, _ = sal("ebas", [128, 64])
    s24, _ = sal("s24", [128, 40])
    rows64, _ = sal("rows64", [1, 128])
    iota0, _ = sal("iota0", [128, 1])
    toki = ial("toki", [128, 2])
    posi1 = ial("posi1", [128, 2])
    posi2 = ial("posi2", [128, 2])
    tok32 = ial("tok32", [128, 8])
    idxw = ial("idxw", [128, 8])
    zeri = ial("zeri", [128, 48])
    pvs, _ = sal("pvs", [128, 8])
    COMMON_END = alloc_ptr[0]
    ARENA = (COMMON_END + 255) // 256 * 256

    lamc_col = scv[:, 3:4]
    blamc_col = scv[:, 4:5]
    beta_col = scv[:, 2:3]
    zcol = zo[:, 0:1]
    onecol = zo[:, 1:2]

    wt_sb, _ = sal("wt_sb", [128, 8192], at=ARENA)
    x_tok, _ = sal("x_tok", [128, 2048])
    xn_tok, _ = sal("xn_tok", [128, 2048])
    xa_fm, _ = sal("xa_fm", [128, 2048])
    kodd, _ = sal("kodd", [128, 2048])
    static_t, _ = sal("static_t", [128, 2048])
    vd_yn, _ = sal("vd_yn", [128, 1024])
    bb, _ = sal("bb", [128, 1024])
    u0, _ = sal("u0", [128, 1024])
    u1, _ = sal("u1", [128, 1024])
    gus, _ = sal("gus", [128, 2048])
    gui, _ = sal("gui", [128, 2048])
    ktok, _ = sal("ktok", [128, 1024])
    s_sb, _ = sal("s_sb", [128, 1024])
    part2, _ = sal("part2", [128, 2048])
    junk, _ = sal("junk", [128, 2048])
    xc, _ = sal("xc", [128, 2048])
    lncol, _ = sal("lncol", [128, 32])
    P1_END = alloc_ptr[0]

    xn_ld, _ = sal("xn_ld", [128, 2048], at=ARENA)
    fm_tmp, _ = sal("fm_tmp", [128, 1024])
    silb, _ = sal("silb", [128, 1024])
    xng, _ = sal("xng", [128, 2048])
    xfm, _ = sal("xfm", [128, 8192])
    g_sl, _ = sal("g_sl", [128, 11 * 512])
    w1ab, _ = sal("w1ab", [128, 2048])
    w1bb, _ = sal("w1bb", [128, 2048])
    w2b, _ = sal("w2b", [128, 2048])
    outb, _ = sal("outb", [128, 8192])
    pbuf, _ = sal("pbuf", [128, 2048])
    o1b, _ = sal("o1b", [128, 2048])
    o2b, _ = sal("o2b", [128, 2048])
    ybuf, _ = sal("ybuf", [128, 2048])
    P2_END = alloc_ptr[0]

    P2_END = alloc_ptr[0]
    xn_ld2 = xn_ld
    assert P1_END <= 229000 and P2_END <= 229000, (P1_END, P2_END)

    pb = [es.enter_context(nc.psum_tensor(f"pb{i}", [128, 512], F32)) for i in range(8)]

    sems = {}
    for name in ("sLC", "sLX0", "sLX1", "sW1P0", "sW1P1", "sW2", "sSX0", "sSX1", "sSP0", "sSP1", "sTAP", "sLQ", "sGX0", "sGX1", "sW2B", "sY0", "sY1", "sSC0", "sSC1", "sOUT", "sT", "sV", "sS"):
        s = _Sem(name)
        s.h = es.enter_context(nc.semaphore(name))
        sems[name] = s
    sLC, sW2, sTAP = sems["sLC"], sems["sW2"], sems["sTAP"]
    sLXp = [sems["sLX0"], sems["sLX1"]]
    sW1p = [sems["sW1P0"], sems["sW1P1"]]
    sSXp = [sems["sSX0"], sems["sSX1"]]
    sSPp = [sems["sSP0"], sems["sSP1"]]
    sLQ, sOUT = sems["sLQ"], sems["sOUT"]
    sGXp = [sems["sGX0"], sems["sGX1"]]
    sW2p = [sems["sW2"], sems["sW2B"]]
    sYp = [sems["sY0"], sems["sY1"]]
    sSCp = [sems["sSC0"], sems["sSC1"]]
    sT, sV, sS = sems["sT"], sems["sV"], sems["sS"]
    P = _Prog()
    P.auto_sem = {"tensor": sT, "vector": sV, "scalar": sS}
    W_ = P.wait
    I_ = P.ins

    for i in range(8):
        I_("sync", lambda E, i=i: E.dma_start(out=wt_sb[:, 1024 * i:1024 * i + 1024],
                                              in_=d_wt[128 * i:128 * i + 128, :]), inc=sLC, dma=True)
    for dst, src in ((ident, d_ident), (masks4, d_msks), (maski4, d_mski),
                     (iotac, d_iota), (zo, d_zo), (ones1, d_ones1), (revi8, d_revi),
                     (etain, d_eta), (lmlin, d_lml), (gamin, d_gam), (betin, d_bet),
                     (rwt, d_rwt), (cval, d_cval), (tri8, d_tri8), (th16, d_th16)):
        I_("sync", lambda E, d=dst, s=src: E.dma_start(out=d[:, :], in_=s[:, :]), inc=sLC, dma=True)
    ld_done = sLC.n

    v_init = I_("vector", lambda E: E.memset(s_sb[:, :], 0.0), inc=sV)
    v_scr0 = I_("vector", lambda E: E.memset(screw[0:1, :], 0.0), inc=sV)
    W_("vector", sLC, ld_done)

    def sync_step(eng, build, sem):
        v = I_(eng, build, inc=sem)
        W_(eng, sem, v)
        return v

    W_("scalar", sLC, ld_done)
    W_("scalar", sV, v_scr0)
    sync_step("scalar", lambda E: E.activation(screw[0:1, 0:1], lmlin[0:1, 0:1], AF.Sigmoid), sS)
    sync_step("scalar", lambda E: E.activation(screw[0:1, 1:2], screw[0:1, 0:1], AF.Ln), sS)
    s_ll = sync_step("scalar", lambda E: E.activation(screw[0:1, 3:4], screw[0:1, 1:2], AF.Exp,
                                                      bias=0.0, scale=128.0), sS)
    W_("vector", sS, s_ll)
    sync_step("vector", lambda E: E.tensor_scalar(screw[0:1, 5:6], screw[0:1, 0:1], -1.0, 1.0,
                                                  OP.mult, OP.add), sV)
    sync_step("vector", lambda E: E.tensor_tensor(screw[0:1, 6:7], screw[0:1, 5:6], etain[0:1, 0:1],
                                                  OP.mult), sV)
    sync_step("vector", lambda E: E.tensor_scalar_mul(screw[0:1, 2:3], screw[0:1, 6:7], 1.0 / DH), sV)
    v_beta = sync_step("vector", lambda E: E.tensor_tensor(screw[0:1, 4:5], screw[0:1, 2:3],
                                                           screw[0:1, 3:4], OP.mult), sV)
    W_("tensor", sV, v_beta)
    t_bc = I_("tensor", lambda E: E.matmul(pb[0][:, 0:16], ones1[0:1, :], screw[0:1, :],
                                           start=True, stop=True), inc=sT)
    W_("vector", sT, t_bc)
    v_scv = sync_step("vector", lambda E: E.tensor_copy(scv[:, :], pb[0][:, 0:16]), sV)
    sync_step("vector", lambda E: E.tensor_tensor(lncol[:, 0:1], iotac[:, 0:1], scv[:, 1:2],
                                                  OP.mult), sV)
    v_l2 = sync_step("vector", lambda E: E.tensor_scalar_mul(lncol[:, 1:2], lncol[:, 0:1], -1.0), sV)
    W_("scalar", sV, v_l2)
    I_("scalar", lambda E: E.activation(lampp[:, 0:1], lncol[:, 0:1], AF.Exp,
                                        bias=zcol, scale=onecol), inc=sS)
    s_lamp = I_("scalar", lambda E: E.activation(lampi[:, 0:1], lncol[:, 1:2], AF.Exp,
                                                 bias=zcol, scale=onecol), inc=sS)
    I_("vector", lambda E: E.tensor_scalar_mul(bmasks4[:, :], masks4[:, :], scv[:, 2:3]), inc=sV)
    v_bm = I_("vector", lambda E: E.tensor_scalar_mul(bmaski4[:, :], maski4[:, :], scv[:, 2:3]), inc=sV)
    W_("tensor", sLC, ld_done)
    v_gb = 0
    for i in range(2):
        W_("tensor", sV, v_gb)
        I_("tensor", lambda E, i=i: E.matmul(pb[1][:, 0:512], ones1[0:1, :],
                                             gamin[0:1, 512 * i:512 * i + 512],
                                             start=True, stop=True, skip_group_check=True), inc=sT)
        t_g1 = I_("tensor", lambda E, i=i: E.matmul(pb[2][:, 0:512], ones1[0:1, :],
                                                    betin[0:1, 512 * i:512 * i + 512],
                                                    start=True, stop=True, skip_group_check=True), inc=sT)
        W_("vector", sT, t_g1)
        I_("vector", lambda E, i=i: E.tensor_copy(gbg[:, 512 * i:512 * i + 512], pb[1][:, 0:512]), inc=sV)
        v_gb = I_("vector", lambda E, i=i: E.tensor_copy(gbb[:, 512 * i:512 * i + 512], pb[2][:, 0:512]), inc=sV)
    prolog_v = sV.n
    prolog_s = sS.n
    prolog_t = sT.n

    vPar = {}
    swCnt = {}
    M = {}

    def PAB(buf, cc):
        pa = cc % 2
        return buf[:, 1024 * pa:1024 * pa + 1024]

    def K_ap(h, cc):
        j = h // 2
        if h % 2 == 0:
            return PAB(xa_fm, cc)[0:64, 128 * j:128 * j + 128]
        return PAB(kodd, cc)[0:64, 128 * j:128 * j + 128]

    def LNC(cc, k):
        return lncol[:, 16 * (cc % 2) + k:16 * (cc % 2) + k + 1]

    def emit_load(cc):
        if cc >= nch:
            return
        W_("sync", sV, vPar.get(cc - 2, 0))
        M.setdefault(cc, {})["lx"] = I_("sync", lambda E, cc=cc: E.dma_start(
            out=PAB(x_tok, cc), in_=d_x[C * cc:C * cc + C, :]), inc=sLXp[cc % 2], dma=True)

    def emit_B1(cc):
        m = M.setdefault(cc, {})
        pa = cc % 2
        xtk, xntk = PAB(x_tok, cc), PAB(xn_tok, cc)
        xcb, jkb = PAB(xc, cc), PAB(junk, cc)
        p2 = M.get(cc - 2, {})
        W_("vector", sLXp[pa], m["lx"])
        if cc == 0:
            W_("vector", sS, s_lamp)
        W_("vector", sS, p2.get("s8", 0))
        v1 = I_("vector", lambda E, x=xtk: E.reduce_sum(LNC(cc, 2), x, axis=AX), inc=sV)
        W_("scalar", sV, v1)
        s1 = I_("scalar", lambda E: E.mul(LNC(cc, 3), LNC(cc, 2), -1.0 / D), inc=sS)
        W_("vector", sS, s1)
        v2 = I_("vector", lambda E, x=xtk, xc_=xcb: E.tensor_scalar_add(xc_, x, LNC(cc, 3)), inc=sV)
        W_("vector", sV, v2)
        v3 = I_("vector", lambda E, xc_=xcb, jk=jkb: E.scalar_tensor_tensor(
            jk, xc_, 0.0, xc_, OP.add, OP.mult, accum_out=LNC(cc, 4)), inc=sV)
        W_("scalar", sV, v3)
        s2a = I_("scalar", lambda E: E.mul(LNC(cc, 5), LNC(cc, 4), 1.0 / D), inc=sS)
        W_("scalar", sS, s2a)
        s2 = I_("scalar", lambda E: E.add(LNC(cc, 5), LNC(cc, 5), LNEPS), inc=sS)
        W_("vector", sS, s2)
        v4 = I_("vector", lambda E: E.reciprocal(LNC(cc, 6), LNC(cc, 5)), inc=sV)
        W_("scalar", sV, v4)
        s_sq1 = I_("scalar", lambda E: E.activation(LNC(cc, 7), LNC(cc, 6), AF.Sqrt), inc=sS)
        W_("scalar", sS, s_sq1)
        W_("scalar", sSXp[pa], swCnt.get(("x", cc - 2), 0))
        W_("scalar", sT, p2.get("t_tp_last", 0))
        m["s_xn"] = I_("scalar", lambda E, xn=xntk, xc_=xcb: E.activation(
            xn, xc_, AF.Identity, bias=zcol, scale=LNC(cc, 7)), inc=sS)
        W_("gpsimd", sS, m["s_xn"])
        swCnt[("x", cc)] = I_("gpsimd", lambda E, cc=cc, xn=xntk: E.dma_start(
            out=d_xnscr[C * cc:C * cc + C, :], in_=xn), inc=sSXp[pa], dma=True)

    def emit_B1t(cc):
        m = M[cc]
        pa = cc % 2
        xntk = PAB(xn_tok, cc)
        p2 = M.get(cc - 2, {})
        p1m = M.get(cc - 1, {})
        W_("tensor", sS, m["s_xn"])
        if cc == 0:
            W_("tensor", sV, v_bm)
        W_("tensor", sV, p1m.get("v_n_last", 0))
        W_("tensor", sV, p1m.get("v_k1", 0))
        t_tp = []
        for bk in range(2):
            for jj in range(4):
                j = 4 * bk + jj
                t = I_("tensor", lambda E, bk=bk, jj=jj, j=j, xn=xntk: E.matmul(
                    pb[6 + bk][:, 128 * jj:128 * jj + 128], xn[:, 128 * j:128 * j + 128],
                    ident[:, :], is_transpose=True, start=(jj == 0), stop=(jj == 3),
                    skip_group_check=True), inc=sT)
            t_tp.append(t)
        for bk in range(2):
            for jj in range(4):
                j = 4 * bk + jj
                t = I_("tensor", lambda E, bk=bk, jj=jj, j=j, xn=xntk: E.matmul(
                    pb[bk][0:64, 128 * jj:128 * jj + 128],
                    xn[:, 128 * j + 64:128 * j + 128],
                    ident[:, :], is_transpose=True, start=(jj == 0), stop=(jj == 3),
                    skip_group_check=True), inc=sT)
            t_tp.append(t)
        m["t_tp_last"] = sT.n
        s_tanh = []
        W_("scalar", sT, p2.get("t_sd_last", 0))
        for bk in range(2):
            W_("scalar", sT, t_tp[bk])
            s_tanh.append(I_("scalar", lambda E, bk=bk, cc=cc: E.activation(
                PAB(xa_fm, cc)[:, 512 * bk:512 * bk + 512], pb[6 + bk][:, 0:512],
                AF.Tanh), inc=sS))
        for bk in range(2):
            W_("scalar", sT, t_tp[2 + bk])
            s_tanh.append(I_("scalar", lambda E, bk=bk, cc=cc: E.activation(
                PAB(kodd, cc)[0:64, 512 * bk:512 * bk + 512], pb[bk][0:64, 0:512],
                AF.Tanh), inc=sS))
        m["s_tanh"] = s_tanh

    def emit_B2(cc):
        m = M[cc]
        p1m = M.get(cc - 1, {})
        p2 = M.get(cc - 2, {})
        W_("tensor", sS, m["s_tanh"][1])
        W_("tensor", sV, p1m.get("v_o1", 0))
        t_st = []
        for eb in range(2):
            for dc in range(8):
                t = I_("tensor", lambda E, eb=eb, dc=dc, cc=cc: E.matmul(
                    pb[4 + eb][:, 0:512],
                    PAB(xa_fm, cc)[:, 128 * dc:128 * dc + 128],
                    wt_sb[:, 1024 * dc + 512 * eb:1024 * dc + 512 * eb + 512],
                    start=(dc == 0), stop=(dc == 7), skip_group_check=True), inc=sT)
            t_st.append(t)
        v_st = []
        W_("vector", sS, p2.get("s_vd", 0))
        for eb in range(2):
            W_("vector", sT, t_st[eb])
            v_st.append(I_("vector", lambda E, eb=eb, cc=cc: E.tensor_copy(
                PAB(static_t, cc)[:, 512 * eb:512 * eb + 512], pb[4 + eb][:, 0:512]),
                inc=sV))
        m["v_st"] = v_st
        if _taps and cc == 0:
            W_("gpsimd", sV, v_st[1])
            M[cc]["tap_static"] = I_("gpsimd", lambda E, cc=cc: E.dma_start(
                out=d_dbg[0:128, :], in_=PAB(static_t, cc)), inc=sTAP, dma=True)

    def emit_C1(cc):
        m = M[cc]
        p1m = M.get(cc - 1, {})
        s_tanh = m["s_tanh"]
        W_("tensor", sS, s_tanh[3])
        W_("tensor", sV, p1m.get("v_su", 0))
        t_g = []
        for bk in range(4):
            for qq in range(4):
                h = 4 * bk + qq
                t = I_("tensor", lambda E, bk=bk, qq=qq, h=h, cc=cc: E.matmul(
                    pb[bk][:, 128 * qq:128 * qq + 128], K_ap(h, cc), K_ap(h, cc),
                    start=(qq == 0), stop=(qq == 3), skip_group_check=True), inc=sT)
            t_g.append(t)
        W_("vector", sT, p1m.get("t_sd_last", 0))
        v_g = []
        for bk in range(4):
            W_("vector", sT, t_g[bk])
            I_("vector", lambda E, bk=bk: E.tensor_tensor(
                gus[:, 512 * bk:512 * bk + 512], pb[bk][:, 0:512], bmasks4[:, :],
                OP.mult), inc=sV)
            v_g.append(I_("vector", lambda E, bk=bk: E.tensor_tensor(
                gui[:, 512 * bk:512 * bk + 512], pb[bk][:, 0:512], bmaski4[:, :],
                OP.mult), inc=sV))
        m["v_g"] = v_g
        t_tk = []
        for bk in range(2):
            W_("tensor", sV, v_g[bk])
            for qq in range(8):
                h = 8 * bk + qq
                t = I_("tensor", lambda E, bk=bk, qq=qq, h=h, cc=cc: E.matmul(
                    pb[bk][:, 64 * qq:64 * qq + 64], K_ap(h, cc), ident[0:64, 0:64],
                    is_transpose=True, start=(qq == 0), stop=(qq == 7),
                    skip_group_check=True), inc=sT)
            t_tk.append(t)
        v_k = []
        for bk in range(2):
            W_("vector", sT, t_tk[bk])
            v_k.append(I_("vector", lambda E, bk=bk: E.tensor_copy(
                ktok[:, 512 * bk:512 * bk + 512], pb[bk][:, 0:512]), inc=sV))
        m["v_k1"] = v_k[1]
        m["v_k"] = v_k
        W_("scalar", sV, p1m.get("vg1", 0))
        W_("scalar", sV, m["v_st"][1])
        m["s_vd"] = I_("scalar", lambda E, cc=cc: E.activation(
            vd_yn[:, :], PAB(static_t, cc), AF.Identity, bias=zcol,
            scale=lampi[:, 0:1]), inc=sS)
        if cc == 0:
            W_("tensor", sV, v_init)
        t_ks = []
        for bk in range(2):
            W_("tensor", sV, m["v_st"][bk])
            for qq in range(8):
                h = 8 * bk + qq
                t = I_("tensor", lambda E, bk=bk, qq=qq, h=h, cc=cc: E.matmul(
                    pb[4 + bk][:, 64 * qq:64 * qq + 64], K_ap(h, cc),
                    s_sb[0:64, 64 * h:64 * h + 64],
                    start=(qq == 0), stop=(qq == 7), skip_group_check=True), inc=sT)
            t_ks.append(t)
        v_b = []
        for bk in range(2):
            W_("vector", sT, t_ks[bk]); W_("vector", sS, m["s_vd"])
            v_b.append(I_("vector", lambda E, bk=bk: E.tensor_tensor(
                bb[:, 512 * bk:512 * bk + 512], vd_yn[:, 512 * bk:512 * bk + 512],
                pb[4 + bk][:, 0:512], OP.subtract), inc=sV))
        m["v_b"] = v_b
        if _taps and cc == 0:
            W_("gpsimd", sV, v_b[1])
            I_("gpsimd", lambda E: E.dma_start(out=d_dbg[128:256, :], in_=bb[:, :]),
               inc=sTAP, dma=True)
            M[cc]["tap_vd"] = I_("gpsimd", lambda E: E.dma_start(
                out=d_dbg[256:384, :], in_=vd_yn[:, :]), inc=sTAP, dma=True)
        ubufs = [u0, u1]
        v_n = [0, 0]
        if _taps and cc == 1:
            W_("vector", sTAP, sTAP.n)
        W_("vector", sV, v_b[1])
        for it in range(1, NEU + 1):
            src = bb if it == 1 else ubufs[1 - it % 2]
            dst = ubufs[it % 2]
            t_n = []
            for bk in range(2):
                W_("tensor", sV, v_g[3])
                W_("tensor", sV, v_b[bk] if it == 1 else v_n[bk])
                if it == 1:
                    W_("tensor", sS, s_tanh[bk])
                for qq in range(8):
                    h = 8 * bk + qq
                    t = I_("tensor", lambda E, bk=bk, qq=qq, h=h, src=src: E.matmul(
                        pb[6 + bk][:, 64 * qq:64 * qq + 64],
                        gus[:, 128 * h:128 * h + 128], src[:, 64 * h:64 * h + 64],
                        start=(qq == 0), stop=(qq == 7), skip_group_check=True), inc=sT)
                t_n.append(t)
            for bk in range(2):
                W_("vector", sT, t_n[bk])
                v_n[bk] = I_("vector", lambda E, bk=bk, dst=dst: E.tensor_tensor(
                    dst[:, 512 * bk:512 * bk + 512], bb[:, 512 * bk:512 * bk + 512],
                    pb[6 + bk][:, 0:512], OP.subtract), inc=sV)
        m["v_n"] = list(v_n)
        m["v_n_last"] = max(v_n)
        m["ufin"] = ubufs[NEU % 2]
        if _taps and cc == 0:
            W_("gpsimd", sV, m["v_n_last"])
            M[cc]["tap_u"] = I_("gpsimd", lambda E, u=m["ufin"]: E.dma_start(
                out=d_dbg[384:512, :], in_=u[:, :]), inc=sTAP, dma=True)

    def emit_C2a(cc):
        m = M[cc]
        ufin = m["ufin"]
        t_o = []
        for bk in range(2):
            W_("tensor", sV, m["v_n"][bk])
            W_("tensor", sV, m["v_b"][bk])
            for qq in range(8):
                h = 8 * bk + qq
                t = I_("tensor", lambda E, bk=bk, qq=qq, h=h, u=ufin: E.matmul(
                    pb[4 + bk][:, 64 * qq:64 * qq + 64],
                    gui[:, 128 * h:128 * h + 128], u[:, 64 * h:64 * h + 64],
                    start=False, stop=(qq == 7), skip_group_check=True), inc=sT)
            t_o.append(t)
        m["t_o"] = t_o
        v_o = []
        if _taps and cc == 0 and "tap_static" in m:
            W_("vector", sTAP, sTAP.n)
        for bk in range(2):
            W_("vector", sT, t_o[bk])
            v_o.append(I_("vector", lambda E, bk=bk, cc=cc: E.scalar_tensor_tensor(
                PAB(static_t, cc)[:, 512 * bk:512 * bk + 512], pb[4 + bk][:, 0:512],
                lampp[:, 0:1], PAB(static_t, cc)[:, 512 * bk:512 * bk + 512],
                OP.mult, OP.add), inc=sV))
        m["v_o"] = v_o
        m["v_o1"] = v_o[1]

    def emit_C2b(cc):
        m = M[cc]
        ufin = m["ufin"]
        t_sd = []
        for bk in range(2):
            W_("tensor", sV, m["v_k"][bk]); W_("tensor", sV, m["v_g"][2 + bk])
            for qq in range(8):
                h = 8 * bk + qq
                t = I_("tensor", lambda E, bk=bk, qq=qq, h=h, u=ufin: E.matmul(
                    pb[2 + bk][0:64, 64 * qq:64 * qq + 64],
                    ktok[:, 64 * h:64 * h + 64], u[:, 64 * h:64 * h + 64],
                    start=(qq == 0), stop=(qq == 7), skip_group_check=True), inc=sT)
            t_sd.append(t)
        m["t_sd_last"] = sT.n
        W_("vector", sT, t_sd[0]); W_("vector", sT, t_sd[1])
        v_sm2 = 0
        for bk in range(2):
            v_sm2 = I_("vector", lambda E, bk=bk: E.tensor_scalar_mul(
                s_sb[0:64, 512 * bk:512 * bk + 512],
                s_sb[0:64, 512 * bk:512 * bk + 512], lamc_col[0:64, :]), inc=sV)
        W_("vector", sV, v_sm2)
        v_su = 0
        for bk in range(2):
            v_su = I_("vector", lambda E, bk=bk: E.scalar_tensor_tensor(
                s_sb[0:64, 512 * bk:512 * bk + 512], pb[2 + bk][0:64, 0:512],
                blamc_col[0:64, :], s_sb[0:64, 512 * bk:512 * bk + 512],
                OP.mult, OP.add), inc=sV)
        m["v_su"] = v_su

    def emit_D(cc):
        m = M[cc]
        pa = cc % 2
        xtk, ptk = PAB(x_tok, cc), PAB(part2, cc)
        xcb, jkb = PAB(xc, cc), PAB(junk, cc)
        stt_ = PAB(static_t, cc)
        if _taps and cc == 0:
            W_("gpsimd", sV, m["v_o1"])
            I_("gpsimd", lambda E, s=stt_: E.dma_start(out=d_dbg[512:640, :], in_=s),
               inc=sTAP, dma=True)
            W_("gpsimd", sS, s_lamp)
            I_("gpsimd", lambda E: E.dma_start(out=d_dbg[640:641, 0:1], in_=lampp[0:1, 0:1]),
               inc=sTAP, dma=True)
            I_("gpsimd", lambda E: E.dma_start(out=d_dbg[641:642, 0:16], in_=scv[0:1, :]),
               inc=sTAP, dma=True)
            I_("gpsimd", lambda E: E.dma_start(out=d_dbg[642:643, 0:1], in_=lampi[0:1, 0:1]),
               inc=sTAP, dma=True)
        W_("vector", sV, m["v_o1"])
        v5 = I_("vector", lambda E, s=stt_: E.reduce_sum(LNC(cc, 8), s, axis=AX), inc=sV)
        W_("scalar", sV, v5)
        s5 = I_("scalar", lambda E: E.mul(LNC(cc, 9), LNC(cc, 8), -1.0 / D), inc=sS)
        W_("vector", sS, s5)
        v6 = I_("vector", lambda E, s=stt_, xc_=xcb: E.tensor_scalar_add(
            xc_, s, LNC(cc, 9)), inc=sV)
        W_("vector", sV, v6)
        v7 = I_("vector", lambda E, xc_=xcb, jk=jkb: E.scalar_tensor_tensor(
            jk, xc_, 0.0, xc_, OP.add, OP.mult, accum_out=LNC(cc, 10)), inc=sV)
        W_("scalar", sV, v7)
        s6a = I_("scalar", lambda E: E.mul(LNC(cc, 11), LNC(cc, 10), 1.0 / D), inc=sS)
        W_("scalar", sS, s6a)
        s6 = I_("scalar", lambda E: E.add(LNC(cc, 11), LNC(cc, 11), LNEPS), inc=sS)
        W_("vector", sS, s6)
        v8 = I_("vector", lambda E: E.reciprocal(LNC(cc, 12), LNC(cc, 11)), inc=sV)
        W_("scalar", sV, v8)
        s_sq2 = I_("scalar", lambda E: E.activation(LNC(cc, 13), LNC(cc, 12), AF.Sqrt), inc=sS)
        W_("scalar", sS, s_sq2)
        if _taps and cc == 0 and "tap_vd" in m:
            W_("scalar", sTAP, sTAP.n)
        W_("scalar", sV, m["v_b"][1])
        s8 = I_("scalar", lambda E, xc_=xcb: E.activation(
            vd_yn[:, :], xc_, AF.Identity, bias=zcol, scale=LNC(cc, 13)), inc=sS)
        m["s8"] = s8
        W_("vector", sS, s8)
        W_("vector", sSPp[pa], swCnt.get(("p", cc - 2), 0))
        vg1 = I_("vector", lambda E, jk=jkb: E.tensor_mul(jk, vd_yn[:, :], gbg[:, :]), inc=sV)
        m["vg1"] = vg1
        W_("vector", sV, vg1)
        vg2 = I_("vector", lambda E, jk=jkb: E.tensor_add(jk, jk, gbb[:, :]), inc=sV)
        W_("vector", sV, vg2)
        vP = I_("vector", lambda E, x=xtk, p=ptk, jk=jkb: E.tensor_add(p, jk, x), inc=sV)
        vPar[cc] = vP
        W_("gpsimd", sV, vP)
        swCnt[("p", cc)] = I_("gpsimd", lambda E, cc=cc, p=ptk: E.dma_start(
            out=d_pscr[C * cc:C * cc + C, :], in_=p), inc=sSPp[pa], dma=True)

    emit_load(0)
    emit_load(1)
    if nch > 0:
        emit_B1(0)
        emit_B1t(0)
        emit_B2(0)
    for cc in range(nch):
        emit_C1(cc)
        if cc + 1 < nch:
            emit_B1(cc + 1)
        emit_C2a(cc)
        emit_C2b(cc)
        if cc + 1 < nch:
            emit_B1t(cc + 1)
            emit_B2(cc + 1)
        emit_D(cc)
        emit_load(cc + 2)


    fins = ((sT, sT.n), (sV, sV.n), (sS, sS.n),
            (sSXp[0], sSXp[0].n), (sSXp[1], sSXp[1].n),
            (sSPp[0], sSPp[0].n), (sSPp[1], sSPp[1].n),
            (sLXp[0], sLXp[0].n), (sLXp[1], sLXp[1].n),
            (sLC, sLC.n), (sTAP, sTAP.n))
    for eng in ENGS:
        for sem, val in fins:
            W_(eng, sem, val)

    GIOA = IndirectOffsetOnAxis

    v_i0 = I_("vector", lambda E: E.tensor_scalar_add(iota0[:, 0:1], iotac[:, 0:1], -1.0), inc=sV)

    z_ms = I_("vector", lambda E: E.memset(zeri[:, :], 0), inc=sV)
    W_("gpsimd", sV, z_ms)
    r0 = 0
    while r0 < NROWS:
        rr = min(4096, NROWS - r0)
        I_("gpsimd", lambda E, r0=r0, rr=rr: E.dma_start(
            out=d_idxmap[r0:r0 + rr, 0:1].rearrange("(p f) o -> p (f o)", p=128),
            in_=zeri[:, 0:rr // 128]), inc=sLQ, dma=True)
        I_("gpsimd", lambda E, r0=r0, rr=rr: E.dma_start(
            out=d_pvals[r0:r0 + rr, 0:1].rearrange("(p f) o -> p (f o)", p=128),
            in_=zeri[:, 0:rr // 128]), inc=sLQ, dma=True)
        r0 += rr
    zinit_mark = sLQ.n

    xnld_p = {0: 0, 1: 0}
    t_tr = {}
    t_pl = 0
    for t8 in range(NTIL):
        W_("sync", sT, xnld_p[t8 % 2])
        lxv = I_("sync", lambda E, t8=t8: E.dma_start(
            out=xn_ld[:, 1024 * (t8 % 2):1024 * (t8 % 2) + 1024],
            in_=d_xnscr[128 * t8:128 * t8 + 128, :]), inc=sLXp[t8 % 2], dma=True)
        W_("tensor", sLXp[t8 % 2], lxv)
        for bk in range(2):
            if t8 > 0:
                W_("tensor", sV, t_tr.get("evac%d_%d" % (bk, t8 - 1), 0))
            for jj in range(4):
                dc = 4 * bk + jj
                t = I_("tensor", lambda E, t8=t8, bk=bk, jj=jj, dc=dc: E.matmul(
                    pb[bk][:, 128 * jj:128 * jj + 128],
                    xn_ld[:, 1024 * (t8 % 2) + 128 * dc:1024 * (t8 % 2) + 128 * dc + 128],
                    ident[:, :], is_transpose=True, start=(jj == 0), stop=(jj == 3),
                    skip_group_check=True), inc=sT)
            t_tr["tp%d_%d" % (bk, t8)] = t
        xnld_p[t8 % 2] = sT.n
        ev_last = 0
        for bk in range(2):
            W_("vector", sT, t_tr["tp%d_%d" % (bk, t8)])
            for jj in range(4):
                dc = 4 * bk + jj
                ev_last = I_("vector", lambda E, bk=bk, jj=jj, dc=dc: E.tensor_copy(
                    fm_tmp[:, 128 * dc:128 * dc + 128],
                    pb[bk][:, 128 * jj:128 * jj + 128]), inc=sV)
            t_tr["evac%d_%d" % (bk, t8)] = ev_last
        W_("tensor", sV, ev_last)
        for dc in range(8):
            t_pl = I_("tensor", lambda E, t8=t8, dc=dc: E.matmul(
                pb[2][:, 8 * t8:8 * t8 + 8],
                fm_tmp[:, 128 * dc:128 * dc + 128], rwt[:, 8 * dc:8 * dc + 8],
                start=(t8 == 0 and dc == 0), stop=(dc == 7),
                skip_group_check=True), inc=sT)

    def vgroup(build_fn):
        last = 0
        for tt in range(NTIL):
            last = I_("vector", build_fn(tt), inc=sV if tt == NTIL - 1 else None)
        W_("vector", sV, last)
        return last

    def sgroup(build_fn):
        last = 0
        for tt in range(NTIL):
            last = I_("scalar", build_fn(tt), inc=sS if tt == NTIL - 1 else None)
        W_("scalar", sS, last)
        return last

    W_("vector", sT, t_pl)
    v_m1 = vgroup(lambda tt: lambda E, tt=tt: E.reduce_max(
        pcol[:, 8 * tt:8 * tt + 1], pb[2][:, 8 * tt:8 * tt + 8], axis=AX))
    W_("scalar", sV, v_m1)
    sgroup(lambda tt: lambda E, tt=tt: E.mul(
        pcol[:, 8 * tt + 1:8 * tt + 2], pcol[:, 8 * tt:8 * tt + 1], -1.0))
    s_exp = sgroup(lambda tt: lambda E, tt=tt: E.activation(
        e_sb[:, 8 * tt:8 * tt + 8], pb[2][:, 8 * tt:8 * tt + 8], AF.Exp,
        bias=pcol[:, 8 * tt + 1:8 * tt + 2], scale=onecol))
    W_("vector", sS, s_exp)
    vgroup(lambda tt: lambda E, tt=tt: E.reduce_sum(
        pcol[:, 8 * tt + 2:8 * tt + 3], e_sb[:, 8 * tt:8 * tt + 8], axis=AX))
    v_ri = vgroup(lambda tt: lambda E, tt=tt: E.reciprocal(
        pcol[:, 8 * tt + 3:8 * tt + 4], pcol[:, 8 * tt + 2:8 * tt + 3]))
    v_sm = vgroup(lambda tt: lambda E, tt=tt: E.reduce_max(
        pcol[:, 8 * tt + 7:8 * tt + 8], e_sb[:, 8 * tt:8 * tt + 8], axis=AX))
    W_("scalar", sV, v_ri)
    s_pr = sgroup(lambda tt: lambda E, tt=tt: E.activation(
        pr_sb[:, 8 * tt:8 * tt + 8], e_sb[:, 8 * tt:8 * tt + 8], AF.Identity,
        bias=zcol, scale=pcol[:, 8 * tt + 3:8 * tt + 4]))
    W_("vector", sV, v_sm)
    vgroup(lambda tt: lambda E, tt=tt: E.tensor_single_scalar(
        sc1[:, 8 * tt:8 * tt + 8], e_sb[:, 8 * tt:8 * tt + 8],
        pcol[:, 8 * tt + 7:8 * tt + 8], OP.is_equal))
    vgroup(lambda tt: lambda E, tt=tt: E.tensor_tensor(
        sc1[:, 8 * tt:8 * tt + 8], sc1[:, 8 * tt:8 * tt + 8], revi8[:, :], OP.mult))
    vgroup(lambda tt: lambda E, tt=tt: E.reduce_max(
        pcol[:, 8 * tt + 4:8 * tt + 5], sc1[:, 8 * tt:8 * tt + 8], axis=AX))
    vgroup(lambda tt: lambda E, tt=tt: E.tensor_single_scalar(
        sc1[:, 8 * tt:8 * tt + 8], sc1[:, 8 * tt:8 * tt + 8],
        pcol[:, 8 * tt + 4:8 * tt + 5], OP.is_equal))
    vgroup(lambda tt: lambda E, tt=tt: E.tensor_copy(
        m1_sb[:, 8 * tt:8 * tt + 8], sc1[:, 8 * tt:8 * tt + 8]))
    W_("vector", sS, s_pr)
    vgroup(lambda tt: lambda E, tt=tt: E.scalar_tensor_tensor(
        e_sb[:, 8 * tt:8 * tt + 8], sc1[:, 8 * tt:8 * tt + 8], -2.0,
        e_sb[:, 8 * tt:8 * tt + 8], OP.mult, OP.add))
    vgroup(lambda tt: lambda E, tt=tt: E.reduce_max(
        pcol[:, 8 * tt + 5:8 * tt + 6], e_sb[:, 8 * tt:8 * tt + 8], axis=AX))
    vgroup(lambda tt: lambda E, tt=tt: E.tensor_single_scalar(
        sc2[:, 8 * tt:8 * tt + 8], e_sb[:, 8 * tt:8 * tt + 8],
        pcol[:, 8 * tt + 5:8 * tt + 6], OP.is_equal))
    vgroup(lambda tt: lambda E, tt=tt: E.tensor_tensor(
        sc2[:, 8 * tt:8 * tt + 8], sc2[:, 8 * tt:8 * tt + 8], revi8[:, :], OP.mult))
    vgroup(lambda tt: lambda E, tt=tt: E.reduce_max(
        pcol[:, 8 * tt + 6:8 * tt + 7], sc2[:, 8 * tt:8 * tt + 8], axis=AX))
    vgroup(lambda tt: lambda E, tt=tt: E.tensor_single_scalar(
        sc2[:, 8 * tt:8 * tt + 8], sc2[:, 8 * tt:8 * tt + 8],
        pcol[:, 8 * tt + 6:8 * tt + 7], OP.is_equal))
    vgroup(lambda tt: lambda E, tt=tt: E.tensor_add(
        sc1[:, 8 * tt:8 * tt + 8], sc1[:, 8 * tt:8 * tt + 8], sc2[:, 8 * tt:8 * tt + 8]))
    v_lp = vgroup(lambda tt: lambda E, tt=tt: E.tensor_mul(
        lp_sb[:, 8 * tt:8 * tt + 8], sc1[:, 8 * tt:8 * tt + 8],
        pr_sb[:, 8 * tt:8 * tt + 8]))

    t_cnt = 0
    W_("tensor", sV, v_lp)
    for tt in range(NTIL):
        t_cnt = I_("tensor", lambda E, tt=tt: E.matmul(
            pb[3][0:8, 0:1], sc1[:, 8 * tt:8 * tt + 8], onecol[:, 0:1],
            start=(tt == 0), stop=(tt == NTIL - 1), skip_group_check=True), inc=sT)
    W_("vector", sT, t_cnt)

    def vstep(build, eng="vector"):
        sem = {"vector": sV, "scalar": sS}[eng]
        v = I_(eng, build, inc=sem)
        W_(eng, sem, v)
        return v

    vstep(lambda E: E.tensor_copy(cntc[0:8, 0:1], pb[3][0:8, 0:1]))
    vstep(lambda E: E.tensor_single_scalar(s24[0:8, 24:24 + 16], th16[0:8, :],
                                           cntc[0:8, 0:1], OP.is_lt))
    vstep(lambda E: E.reduce_sum(cntc[0:8, 3:4], s24[0:8, 24:24 + 16], axis=AX))
    v_pad = vstep(lambda E: E.tensor_scalar_mul(cntc[0:8, 4:5], cntc[0:8, 3:4], float(SROWS)))
    W_("tensor", sV, v_pad)
    t_po = I_("tensor", lambda E: E.matmul(pb[3][0:8, 8:9], tri8[:, :], cntc[0:8, 4:5],
                                           start=False, stop=True, skip_group_check=True), inc=sT)
    W_("vector", sT, t_po)
    v_poc = vstep(lambda E: E.tensor_copy(cntc[0:8, 5:6], pb[3][0:8, 8:9]))
    W_("tensor", sV, v_poc)
    t_por = I_("tensor", lambda E: E.matmul(pb[3][0:1, 16:24], cntc[0:8, 5:6], ident[0:8, 0:8],
                                            is_transpose=True, start=False, stop=True,
                                            skip_group_check=True), inc=sT)
    W_("vector", sT, t_por)
    v_porow = vstep(lambda E: E.tensor_copy(rows64[0:1, 0:8], pb[3][0:1, 16:24]))
    W_("tensor", sV, v_porow)
    t_pob = I_("tensor", lambda E: E.matmul(pb[1][:, 0:8], ones1[0:1, :], rows64[0:1, 0:8],
                                            start=True, stop=True, skip_group_check=True), inc=sT)
    W_("vector", sT, t_pob)
    vstep(lambda E: E.tensor_copy(s24[:, 0:8], pb[1][:, 0:8]))
    vstep(lambda E: E.tensor_scalar_mul(cntc[:, 6:7], iota0[:, 0:1], float(SROWS)))
    vstep(lambda E: E.tensor_single_scalar(s24[:, 8:16], s24[:, 0:8], cntc[:, 6:7], OP.is_le))
    vstep(lambda E: E.reduce_sum(cntc[:, 8:9], s24[:, 8:16], axis=AX))
    vstep(lambda E: E.tensor_scalar_add(cntc[:, 9:10], cntc[:, 8:9], -1.0))
    vstep(lambda E: E.tensor_scalar_mul(cntc[:, 10:11], cntc[:, 9:10], float(22 * 128)))
    v_eb = vstep(lambda E: E.tensor_scalar_mul(cntc[:, 11:12], cntc[:, 9:10], float(F)))
    W_("tensor", sV, v_eb)
    t_e1r = I_("tensor", lambda E: E.matmul(pb[3][0:1, 32:32 + NSLOT], cntc[0:NSLOT, 10:11],
                                            ident[0:NSLOT, 0:NSLOT], is_transpose=True,
                                            start=False, stop=True, skip_group_check=True), inc=sT)
    t_e2r = I_("tensor", lambda E: E.matmul(pb[3][0:1, 64:64 + NSLOT], cntc[0:NSLOT, 11:12],
                                            ident[0:NSLOT, 0:NSLOT], is_transpose=True,
                                            start=False, stop=True, skip_group_check=True), inc=sT)
    W_("vector", sT, t_e2r)
    vstep(lambda E: E.tensor_copy(rows64[0:1, 8:8 + NSLOT], pb[3][0:1, 32:32 + NSLOT]))
    v_ebr = vstep(lambda E: E.tensor_copy(rows64[0:1, 64:64 + NSLOT], pb[3][0:1, 64:64 + NSLOT]))
    W_("tensor", sV, v_ebr)
    t_eb1 = I_("tensor", lambda E: E.matmul(pb[1][:, 8:8 + NSLOT], ones1[0:1, :],
                                            rows64[0:1, 8:8 + NSLOT], start=False, stop=True,
                                            skip_group_check=True), inc=sT)
    t_eb2 = I_("tensor", lambda E: E.matmul(pb[1][:, 64:64 + NSLOT], ones1[0:1, :],
                                            rows64[0:1, 64:64 + NSLOT], start=False, stop=True,
                                            skip_group_check=True), inc=sT)
    W_("vector", sT, t_eb2)
    vstep(lambda E: E.tensor_copy(ebas[:, 0:NSLOT], pb[1][:, 8:8 + NSLOT]))
    v_ebas = vstep(lambda E: E.tensor_copy(ebas[:, EB2:EB2 + NSLOT], pb[1][:, 64:64 + NSLOT]))

    prev_vcu = vstep(lambda E: E.tensor_copy(rows64[0:1, 56:64], rows64[0:1, 0:8]))
    sc_p = {0: 0, 1: 0}
    for tt in range(NTIL):
        W_("tensor", sV, prev_vcu)
        t_rk = I_("tensor", lambda E, tt=tt: E.matmul(
            pb[0][:, 0:8], masks4[:, 0:128], sc1[:, 8 * tt:8 * tt + 8],
            start=True, stop=True, skip_group_check=True), inc=sT)
        t_cb = I_("tensor", lambda E: E.matmul(
            pb[0][:, 8:16], ones1[0:1, :], rows64[0:1, 56:64],
            start=False, stop=True, skip_group_check=True), inc=sT)
        t_tsum = I_("tensor", lambda E, tt=tt: E.matmul(
            pb[0][0:1, 16:24], onecol[:, 0:1], sc1[:, 8 * tt:8 * tt + 8],
            start=False, stop=True, skip_group_check=True), inc=sT)
        W_("vector", sT, t_tsum)
        W_("vector", sT, t_cb)
        v_t8 = vstep(lambda E: E.tensor_copy(cb_sb[:, 0:8], pb[0][:, 8:16]))
        v_t9 = vstep(lambda E: E.tensor_tensor(tmp8[:, 0:8], pb[0][:, 0:8], cb_sb[:, 0:8], OP.add))
        vstep(lambda E, tt=tt: E.tensor_tensor(
            m2tmp[:, 0:8], sc1[:, 8 * tt:8 * tt + 8], m1_sb[:, 8 * tt:8 * tt + 8], OP.subtract))
        vstep(lambda E, tt=tt: E.scalar_tensor_tensor(
            junk8[:, 0:8], tmp8[:, 0:8], 0.0, m1_sb[:, 8 * tt:8 * tt + 8],
            OP.add, OP.mult, accum_out=poscols[:, 2 * tt:2 * tt + 1]))
        vstep(lambda E, tt=tt: E.scalar_tensor_tensor(
            junk8[:, 0:8], tmp8[:, 0:8], 0.0, m2tmp[:, 0:8],
            OP.add, OP.mult, accum_out=poscols[:, 2 * tt + 1:2 * tt + 2]))
        tp2 = tt % 2
        W_("vector", sSCp[tp2], sc_p[tp2])
        vstep(lambda E, tt=tt, tp2=tp2: E.scalar_tensor_tensor(
            junk8[:, 0:8], lp_sb[:, 8 * tt:8 * tt + 8], 0.0, m1_sb[:, 8 * tt:8 * tt + 8],
            OP.add, OP.mult, accum_out=pv12[:, 2 * tp2:2 * tp2 + 1]))
        vstep(lambda E, tt=tt, tp2=tp2: E.scalar_tensor_tensor(
            junk8[:, 0:8], lp_sb[:, 8 * tt:8 * tt + 8], 0.0, m2tmp[:, 0:8],
            OP.add, OP.mult, accum_out=pv12[:, 2 * tp2 + 1:2 * tp2 + 2]))
        vstep(lambda E, tt=tt: E.tensor_scalar_add(tokf[:, 0:1], iota0[:, 0:1], float(128 * tt)))
        vstep(lambda E, tp2=tp2: E.tensor_copy(toki[:, tp2:tp2 + 1], tokf[:, 0:1]))
        vstep(lambda E, tt=tt, tp2=tp2: E.tensor_copy(posi1[:, tp2:tp2 + 1],
                                                      poscols[:, 2 * tt:2 * tt + 1]))
        v_cast = vstep(lambda E, tt=tt, tp2=tp2: E.tensor_copy(
            posi2[:, tp2:tp2 + 1], poscols[:, 2 * tt + 1:2 * tt + 2]))
        v_cu = vstep(lambda E: E.tensor_tensor(
            rows64[0:1, 56:64], rows64[0:1, 56:64], pb[0][0:1, 16:24], OP.add))
        prev_vcu = v_cu
        W_("gpsimd", sLQ, zinit_mark)
        W_("gpsimd", sV, v_cast)
        I_("gpsimd", lambda E, tp2=tp2: E.indirect_dma_start(
            d_idxmap[0:NROWS, 0:1], GIOA(posi1[:, tp2:tp2 + 1], 0),
            toki[:, tp2:tp2 + 1], None), inc=sSCp[tp2], dma=True)
        I_("gpsimd", lambda E, tp2=tp2: E.indirect_dma_start(
            d_idxmap[0:NROWS, 0:1], GIOA(posi2[:, tp2:tp2 + 1], 0),
            toki[:, tp2:tp2 + 1], None), inc=sSCp[tp2], dma=True)
        I_("gpsimd", lambda E, tp2=tp2: E.indirect_dma_start(
            d_pvals[0:NROWS, 0:1], GIOA(posi1[:, tp2:tp2 + 1], 0),
            pv12[:, 2 * tp2:2 * tp2 + 1], None), inc=sSCp[tp2], dma=True)
        sc_p[tp2] = I_("gpsimd", lambda E, tp2=tp2: E.indirect_dma_start(
            d_pvals[0:NROWS, 0:1], GIOA(posi2[:, tp2:tp2 + 1], 0),
            pv12[:, 2 * tp2 + 1:2 * tp2 + 2], None), inc=sSCp[tp2], dma=True)

    w1g_p = {0: [], 1: []}
    w2g_p = {0: 0, 1: 0}
    t_w1free = {0: 0, 1: 0}
    t_w2free = {0: 0, 1: 0}
    v_idxw_free = {0: 0, 1: 0}
    g_xng = {0: 0, 1: 0}
    t_xng_free = {0: 0, 1: 0}
    v_yfree = [0, 0]
    v_oev = [0, 0, 0, 0]
    s_silfree = [0, 0]
    st_out = {0: 0, 1: 0}
    W_("tensor", sV, sV.n)
    W_("tensor", sS, sS.n)

    prep_ev = {0: (0, 0), 1: (0, 0)}
    w1mk_all = {}

    def emit_prep(s):
        sp = s % 2
        W_("vector", sV, v_ebas)
        for sem, val in w1g_p[sp]:
            W_("vector", sem, val)
        vstep(lambda E, s=s: E.tensor_tensor(tokf[:, 0:1], ebas[:, s:s + 1], iota0[:, 0:1], OP.add))
        vstep(lambda E, s=s, sp=sp: E.tensor_copy(idxw[:, 4 * sp:4 * sp + 1], tokf[:, 0:1]))
        vstep(lambda E, s=s: E.tensor_tensor(tokf[:, 0:1], ebas[:, EB2 + s:EB2 + s + 1],
                                             iota0[:, 0:1], OP.add))
        v_idxw = vstep(lambda E, s=s, sp=sp: E.tensor_copy(idxw[:, 4 * sp + 1:4 * sp + 2],
                                                           tokf[:, 0:1]))
        M_idxw[sp] = v_idxw
        W_("sync", sSCp[0], sc_p[0]); W_("sync", sSCp[1], sc_p[1])
        W_("sync", sGXp[0], g_xng[0]); W_("sync", sGXp[1], g_xng[1])
        W_("sync", sV, max(v_oev))
        ld_tok = 0
        for tt in range(TPS):
            I_("sync", lambda E, s=s, tt=tt, sp=sp: E.dma_start(
                out=tok32[:, 4 * sp + tt:4 * sp + tt + 1],
                in_=d_idxmap[SROWS * s + 128 * tt:SROWS * s + 128 * tt + 128, 0:1]),
                inc=sLXp[0], dma=True)
            ld_tok = I_("sync", lambda E, s=s, tt=tt, sp=sp: E.dma_start(
                out=pvs[:, 4 * sp + tt:4 * sp + tt + 1],
                in_=d_pvals[SROWS * s + 128 * tt:SROWS * s + 128 * tt + 128, 0:1]),
                inc=sLXp[0], dma=True)
        ev = 0
        for tt in range(TPS):
            gp = tt % 2
            W_("gpsimd", sLXp[0], ld_tok)
            W_("gpsimd", sT, t_xng_free[gp])
            g_xng[gp] = I_("gpsimd", lambda E, tt=tt, gp=gp, sp=sp: E.indirect_dma_start(
                xng[:, 1024 * gp:1024 * gp + 1024], None,
                d_xnscr[0:L, :], GIOA(tok32[:, 4 * sp + tt:4 * sp + tt + 1], 0)),
                inc=sGXp[gp], dma=True)
            W_("tensor", sGXp[gp], g_xng[gp])
            W_("tensor", sV, v_oev[0]); W_("tensor", sV, v_oev[1])
            for bk in range(2):
                W_("tensor", sV, t_tr.get("sev%d" % bk, 0))
                for jj in range(4):
                    dc = 4 * bk + jj
                    t = I_("tensor", lambda E, tt=tt, gp=gp, bk=bk, jj=jj, dc=dc: E.matmul(
                        pb[4 + bk][:, 128 * jj:128 * jj + 128],
                        xng[:, 1024 * gp + 128 * dc:1024 * gp + 128 * dc + 128],
                        ident[:, :], is_transpose=True, start=(jj == 0), stop=(jj == 3),
                        skip_group_check=True), inc=sT)
                t_tr["stp%d" % bk] = t
            t_xng_free[gp] = sT.n
            for bk in range(2):
                W_("vector", sT, t_tr["stp%d" % bk])
                for jj in range(4):
                    dc = 4 * bk + jj
                    ev = I_("vector", lambda E, tt=tt, bk=bk, jj=jj, dc=dc, sp=sp: E.tensor_copy(
                        xfm[:, XFW * sp + 128 * TPS * dc + 128 * tt:
                             XFW * sp + 128 * TPS * dc + 128 * tt + 128],
                        pb[4 + bk][:, 128 * jj:128 * jj + 128]), inc=sV)
                t_tr["sev%d" % bk] = ev
        prep_ev[sp] = (ev, ld_tok)
        w1mk_all[s] = {}
        for fj in (0, 1):
            if fj < NFJ:
                w1mk_all[s][fj] = emit_w1g(fj, s, sp, v_idxw)

    def emit_w1g(fj, s, sp, v_idxw):
        fp = fj % 2
        W_("gpsimd", sV, v_idxw)
        W_("gpsimd", sT, t_w1free[fp])
        mk = 0
        for ab, buf in ((0, w1ab), (1, w1bb)):
            mk = I_("gpsimd", lambda E, s=s, fj=fj, fp=fp, ab=ab, buf=buf, sp=sp: E.indirect_dma_start(
                buf[:, 1024 * fp:1024 * fp + 1024], None,
                d_w1t[0:W1ROWS, :], GIOA(idxw[:, 4 * sp:4 * sp + 1], 0),
                element_offset=128 * (fj + 11 * ab) * 1024), inc=sW1p[fp], dma=True)
        return mk

    M_idxw = {0: 0, 1: 0}
    emit_prep(0)
    for s in range(NSLOT):
        sp = s % 2
        v_idxw = M_idxw[sp]
        w1mk = w1mk_all[s]
        v_xfm, ld_tok = prep_ev[sp]
        W_("tensor", sV, v_xfm)
        for fj in range(NFJ):
            fp = fj % 2
            rot = fj % 2
            bA, bB = 2 * rot, 2 * rot + 1
            W_("tensor", sW1p[fp], w1mk[fj])
            W_("tensor", sV, v_yfree[rot])
            W_("tensor", sS, s_silfree[rot])
            for dc in range(8):
                I_("tensor", lambda E, fj=fj, fp=fp, dc=dc, bA=bA, sp=sp: E.matmul(
                    pb[bA][:, 0:128 * TPS],
                    w1ab[:, 1024 * fp + 128 * dc:1024 * fp + 128 * dc + 128],
                    xfm[:, XFW * sp + 128 * TPS * dc:XFW * sp + 128 * TPS * dc + 128 * TPS],
                    start=(dc == 0), stop=(dc == 7), skip_group_check=True), inc=sT)
            t_y1 = sT.n
            for dc in range(8):
                I_("tensor", lambda E, fj=fj, fp=fp, dc=dc, bB=bB, sp=sp: E.matmul(
                    pb[bB][:, 0:128 * TPS],
                    w1bb[:, 1024 * fp + 128 * dc:1024 * fp + 128 * dc + 128],
                    xfm[:, XFW * sp + 128 * TPS * dc:XFW * sp + 128 * TPS * dc + 128 * TPS],
                    start=(dc == 0), stop=(dc == 7), skip_group_check=True), inc=sT)
            t_y2 = sT.n
            t_w1free[fp] = sT.n
            if fj + 2 < NFJ:
                w1mk[fj + 2] = emit_w1g(fj + 2, s, sp, v_idxw)
            W_("scalar", sT, t_y1)
            W_("scalar", sV, v_yfree[rot])
            s_sil = I_("scalar", lambda E, rot=rot, bA=bA: E.activation(
                silb[:, 512 * rot:512 * rot + 128 * TPS], pb[bA][:, 0:128 * TPS], AF.Silu), inc=sS)
            s_silfree[rot] = s_sil
            W_("vector", sS, s_sil); W_("vector", sT, t_y2)
            v_yfree[rot] = I_("vector", lambda E, fj=fj, rot=rot, bB=bB: E.tensor_tensor(
                g_sl[:, 128 * TPS * fj:128 * TPS * fj + 128 * TPS],
                silb[:, 512 * rot:512 * rot + 128 * TPS], pb[bB][:, 0:128 * TPS], OP.mult), inc=sV)
        v_gdone = sV.n
        if s + 1 < NSLOT:
            emit_prep(s + 1)
        W_("tensor", sV, v_gdone)
        W_("tensor", sV, prep_ev[(s + 1) % 2][0] if s + 1 < NSLOT else 0)
        o_done = 0
        for hh in range(2):
            w2mk = {}

            def emit_w2g(fj, s=s, sp=sp, v_idxw=v_idxw):
                fp = fj % 2
                W_("gpsimd", sV, v_idxw)
                W_("gpsimd", sT, t_w2free[fp])
                return I_("gpsimd", lambda E, s=s, fj=fj, fp=fp, sp=sp: E.indirect_dma_start(
                    w2b[:, 1024 * fp:1024 * fp + 1024], None,
                    d_w2[0:W2ROWS, :], GIOA(idxw[:, 4 * sp + 1:4 * sp + 2], 0),
                    element_offset=128 * fj * 1024), inc=sW2p[fp], dma=True)
            w2mk[0] = emit_w2g(0)
            if NFJ > 1:
                w2mk[1] = emit_w2g(1)
            for fj in range(NFJ):
                fp = fj % 2
                W_("tensor", sW2p[fp], w2mk[fj])
                for tt in range(TPS):
                    bk = 4 + tt
                    if fj == 0:
                        W_("tensor", sV, v_oev[tt])
                    I_("tensor", lambda E, hh=hh, fj=fj, fp=fp, tt=tt, bk=bk: E.matmul(
                        pb[bk][:, 0:512],
                        g_sl[:, 128 * TPS * fj + 128 * tt:128 * TPS * fj + 128 * tt + 128],
                        w2b[:, 1024 * fp + 512 * hh:1024 * fp + 512 * hh + 512],
                        start=(fj == 0), stop=(fj == NFJ - 1), skip_group_check=True), inc=sT)
                t_w2free[fp] = sT.n
                if fj + 2 < NFJ:
                    w2mk[fj + 2] = emit_w2g(fj + 2)
            o_done = sT.n
            for tt in range(TPS):
                bk = 4 + tt
                W_("vector", sT, o_done)
                W_("vector", sLXp[0], ld_tok)
                W_("vector", sSXp[sp], st_out[sp])
                v_oev[tt] = I_("vector", lambda E, hh=hh, tt=tt, bk=bk, sp=sp: E.tensor_scalar_mul(
                    outb[:, XFW * sp + 1024 * tt + 512 * hh:XFW * sp + 1024 * tt + 512 * hh + 512],
                    pb[bk][:, 0:512], pvs[:, 4 * sp + tt:4 * sp + tt + 1]), inc=sV)
        so_mark = 0
        for tt in range(TPS):
            W_("gpsimd", sV, v_oev[tt])
            so_mark = I_("gpsimd", lambda E, s=s, tt=tt, sp=sp: E.dma_start(
                out=d_oscr[SROWS * s + 128 * tt:SROWS * s + 128 * tt + 128, :],
                in_=outb[:, XFW * sp + 1024 * tt:XFW * sp + 1024 * tt + 1024]),
                inc=sSXp[sp], dma=True)
        st_out[sp] = so_mark
        w1g_p[sp] = [(sW1p[0], sW1p[0].n), (sW1p[1], sW1p[1].n),
                     (sW2p[0], sW2p[0].n), (sW2p[1], sW2p[1].n)]

    oscr_all = [(sSXp[0], sSXp[0].n), (sSXp[1], sSXp[1].n)]

    g_cmb = {0: 0, 1: 0}
    v_ycmb = {0: 0, 1: 0}
    ld_pp = {0: 0, 1: 0}
    y_marks = {0: 0, 1: 0}
    for tt in range(NTIL):
        cp = tt % 2
        W_("gpsimd", sV, v_ycmb[cp])
        ld_pp[cp] = I_("gpsimd", lambda E, tt=tt, cp=cp: E.dma_start(
            out=pbuf[:, 1024 * cp:1024 * cp + 1024],
            in_=d_pscr[128 * tt:128 * tt + 128, :]), inc=sSPp[1], dma=True)
        W_("vector", sLQ, max(g_cmb.values()))
        W_("vector", sSCp[0], sc_p[0]); W_("vector", sSCp[1], sc_p[1])
        vstep(lambda E, tt=tt: E.tensor_copy(posi1[:, 0:1], poscols[:, 2 * tt:2 * tt + 1]))
        v_pc = vstep(lambda E, tt=tt: E.tensor_copy(posi2[:, 0:1], poscols[:, 2 * tt + 1:2 * tt + 2]))
        W_("gpsimd", sV, v_pc)
        for sem, val in oscr_all:
            W_("gpsimd", sem, val)
        W_("gpsimd", sV, v_ycmb[cp])
        I_("gpsimd", lambda E, cp=cp: E.indirect_dma_start(
            o1b[:, 1024 * cp:1024 * cp + 1024], None,
            d_oscr[0:NROWS, :], GIOA(posi1[:, 0:1], 0)), inc=sLQ, dma=True)
        g_cmb[cp] = I_("gpsimd", lambda E, cp=cp: E.indirect_dma_start(
            o2b[:, 1024 * cp:1024 * cp + 1024], None,
            d_oscr[0:NROWS, :], GIOA(posi2[:, 0:1], 0)), inc=sLQ, dma=True)
        W_("vector", sLQ, g_cmb[cp])
        W_("vector", sSPp[1], ld_pp[cp])
        W_("vector", sYp[cp], y_marks[cp])
        vstep(lambda E, cp=cp: E.tensor_tensor(
            ybuf[:, 1024 * cp:1024 * cp + 1024], pbuf[:, 1024 * cp:1024 * cp + 1024],
            o1b[:, 1024 * cp:1024 * cp + 1024], OP.add))
        v_ycmb[cp] = vstep(lambda E, cp=cp: E.tensor_tensor(
            ybuf[:, 1024 * cp:1024 * cp + 1024], ybuf[:, 1024 * cp:1024 * cp + 1024],
            o2b[:, 1024 * cp:1024 * cp + 1024], OP.add))
        W_("gpsimd", sV, v_ycmb[cp])
        y_marks[cp] = I_("gpsimd", lambda E, tt=tt, cp=cp: E.dma_start(
            out=d_y[128 * tt:128 * tt + 128, :],
            in_=ybuf[:, 1024 * cp:1024 * cp + 1024]), inc=sYp[cp], dma=True)


    W_("gpsimd", sYp[0], sYp[0].n)
    W_("gpsimd", sYp[1], sYp[1].n)

    block = es.enter_context(nc.Block())

    def run_stream(eng_name):
        def body(E):
            for fn in P.streams[eng_name]:
                fn(E)
        return body

    block.sync(run_stream("sync"))
    block.tensor(run_stream("tensor"))
    block.vector(run_stream("vector"))
    block.scalar(run_stream("scalar"))
    block.gpsimd(run_stream("gpsimd"))

    return nc, es


def _host_inputs(inputs):
    x = np.ascontiguousarray(np.asarray(inputs["L0_attn_x"], dtype=np.float32))
    W = np.asarray(inputs["L0_attn_W"], dtype=np.float32)
    gam = np.asarray(inputs["L0_attn_ln_gamma"], dtype=np.float32).reshape(1, D)
    bet = np.asarray(inputs["L0_attn_ln_beta"], dtype=np.float32).reshape(1, D)
    eta = np.asarray(inputs["L0_attn_eta"], dtype=np.float32).reshape(1, 1)
    lml = np.asarray(inputs["L0_attn_lam_logit"], dtype=np.float32).reshape(1, 1)
    rw = np.asarray(inputs["L0_ffn_router_weight"], dtype=np.float32)
    w1 = np.asarray(inputs["L0_ffn_w1"], dtype=np.float32)
    w2 = np.asarray(inputs["L0_ffn_w2"], dtype=np.float32)

    wt = np.ascontiguousarray(W.T)
    w1t = np.ascontiguousarray(
        w1.reshape(E, 8, 128, 22, 128).transpose(0, 3, 2, 1, 4)).reshape(E * 22 * 128, 1024)
    w2r = np.ascontiguousarray(w2.reshape(E * F, 1024))
    rwt = np.ascontiguousarray(rw.reshape(8, 128, 8).transpose(1, 0, 2)).reshape(128, 64)

    ident = np.eye(128, dtype=np.float32)
    iotac = np.arange(1, 129, dtype=np.float32).reshape(128, 1)
    cval = np.zeros((128, 4), dtype=np.float32)
    cval[:, 0] = LNEPS
    tri8u = np.triu(np.ones((8, 8), dtype=np.float32), 1)
    th16 = np.tile((np.arange(16, dtype=np.float32) * SROWS).reshape(1, 16), (128, 1))
    zo = np.zeros((128, 2), dtype=np.float32)
    zo[:, 1] = 1.0
    ones1 = np.ones((1, 128), dtype=np.float32)
    revi = np.tile((8.0 - np.arange(8, dtype=np.float32)).reshape(1, 8), (128, 1))
    tri_s = np.triu(np.ones((128, 128), dtype=np.float32), 1)
    tri_i = np.triu(np.ones((128, 128), dtype=np.float32), 0)
    msks = np.ascontiguousarray(np.tile(tri_s, (1, 4)))
    mski = np.ascontiguousarray(np.tile(tri_i, (1, 4)))

    shared = {"wt": wt, "gam": gam, "bet": bet, "eta": eta, "lml": lml,
              "rwt": rwt, "w1t": w1t, "w2": w2r, "ident": ident, "iotac": iotac,
              "zo": zo, "ones1": ones1, "revi": revi, "msks": msks, "mski": mski,
              "cval": cval, "tri8": tri8u, "th16": th16}
    return x, shared


def kernel(**inputs):
    x, shared = _host_inputs(inputs)
    nc, es = build_nc()
    in_maps = [dict(shared, x=np.ascontiguousarray(x[c])) for c in range(B)]
    try:
        res = run_bass_kernel_spmd(nc, in_maps, core_ids=list(range(B)))
    finally:
        es.close()
    out = np.stack([np.asarray(res.results[c]["y"]) for c in range(B)], axis=0)
    return out.astype(np.float32)
